# version 03
# baseline (speedup 1.0000x reference)
import numpy as np
import ml_dtypes
import concourse.bass as bass
import concourse.mybir as mybir
from concourse.bass_utils import run_bass_kernel_spmd
from concourse.library_overlay import lower_extended_insts
from contextlib import ExitStack

F32 = mybir.dt.float32
BF16 = mybir.dt.bfloat16
U8 = mybir.dt.uint8
FP8 = mybir.dt.float8e4
AF = mybir.ActivationFunctionType
ALU = mybir.AluOpType
bf = ml_dtypes.bfloat16

T = 1024
CH = 16
D = 2048
RMS_EPS = 1.1920929e-07
LN_EPS = 1e-5
CROUND = 12582912.0
POSBIG = 3.4028235e38
INV_SQRT_HD = 0.08838834764831845

N_BLOCKS = 4
PHASE_LIMIT = 99
ENGS = ("sync", "tensor", "vector", "scalar", "gpsimd")


class Ctr:
    def __init__(self, nc, es, name):
        self.sem = es.enter_context(nc.semaphore(name))
        self.v = 0

    def bump(self, n=1):
        self.v += n
        return self.v


def _bank_splits(lo, hi):
    outl = []
    c = lo
    while c < hi:
        nxt = min((c // 512 + 1) * 512, hi)
        outl.append((c, nxt))
        c = nxt
    return outl


def build_program():
    nc = bass.Bass("TRN2", target_bir_lowering=False)
    es = ExitStack()
    streams = {e: [] for e in ENGS}

    dram = lambda n, s, d, k="ExternalInput": nc.dram_tensor(n, s, d, kind=k).ap()
    x0 = dram("x0", [D, T], F32)
    wqkvT = dram("wqkvT", [D, 3072], F32)
    wprojT = dram("wprojT", [D, D], F32)
    qgc = dram("qgc", [16, 1], F32)
    wqgT = dram("wqgT", [D, 5632], U8)
    wquT = dram("wquT", [D, 5632], U8)
    wqdT = dram("wqdT", [5632, D], U8)
    sg_c = dram("sg_c", [128, 44], F32)
    su_c = dram("su_c", [128, 44], F32)
    sd_c = dram("sd_c", [128, 16], F32)
    l1wT = dram("l1wT", [D, 2304], F32)
    l1bias = dram("l1bias", [128, 18], F32)
    l1owT = dram("l1owT", [D, D], F32)
    l1ob = dram("l1ob", [128, 16], F32)
    wfcT = dram("wfcT", [D, 8192], F32)
    wprT = dram("wprT", [8192, D], F32)
    cosx = dram("cosx", [128, T], BF16)
    sinx = dram("sinx", [128, T], BF16)
    lens_in = dram("lens_in", [128, 8], F32)
    ident_in = dram("ident_in", [128, 128], BF16)
    aone_in = dram("aone_in", [128, 64], BF16)
    onesrow_in = dram("onesrow_in", [65, 128], BF16)
    negrow_in = dram("negrow_in", [65, 128], BF16)
    ceps_in = dram("ceps_in", [128, 2], F32)

    out_t = dram("out", [D, T], F32, "ExternalOutput")
    xs = dram("xs", [D, T], F32, "Internal")
    u_dram = dram("u_dram", [D, T], BF16, "Internal")
    h_dram = dram("h_dram", [8192, T], F32, "Internal")
    scr128 = dram("scr128", [128, 1], F32, "Internal")
    ccA_in = dram("ccA_in", [1, 1024 * 1024], BF16, "Internal")
    ccA_out = dram("ccA_out", [1, 2 * 1024 * 1024], BF16, "Internal")
    ccB_in = dram("ccB_in", [1, 2 * 128 * 1024], BF16, "Internal")
    ccB_out = dram("ccB_out", [1, 4 * 128 * 1024], BF16, "Internal")
    ccm_in = dram("ccm_in", [1, 1], F32, "Internal")
    ccm_out = dram("ccm_out", [1, 1], F32, "Internal")
    cch_in = dram("cch_in", [1, 1], F32, "Internal")
    cch_out = dram("cch_out", [1, 1], F32, "Internal")

    def xview(t, c):
        return t.rearrange("(ch p) t -> ch p t", p=128)[c]

    sbuf = lambda n, s, d: es.enter_context(nc.sbuf_tensor(n, s, d))
    IDENT = sbuf("IDENT", [128, 128], BF16)
    AONE = sbuf("AONE", [128, 64], BF16)
    ONESROW = sbuf("ONESROW", [65, 128], BF16)
    NEGROW = sbuf("NEGROW", [65, 128], BF16)
    CEPS = sbuf("CEPS", [128, 2], F32)
    COSX = sbuf("COSX", [128, T], BF16)
    SINX = sbuf("SINX", [128, T], BF16)
    LENS = sbuf("LENS", [128, 8], F32)
    QGC = sbuf("QGC", [16, 1], F32)
    SGC = sbuf("SGC", [128, 44], F32)
    SUC = sbuf("SUC", [128, 44], F32)
    SDC = sbuf("SDC", [128, 16], F32)
    L1BI = sbuf("L1BI", [128, 18], F32)
    L1OB = sbuf("L1OB", [128, 16], F32)
    WRAW = sbuf("WRAW", [128, 1024], F32)
    WU8 = sbuf("WU8", [128, 2048], U8)
    WBF = sbuf("WBF", [128, 16384], BF16)
    SQ = sbuf("SQ", [128, 2048], BF16)
    XCH = sbuf("XCH", [128, 1024], F32)
    XC2 = sbuf("XC2", [128, 1024], F32)
    ROWS = sbuf("ROWS", [128, 3072], F32)
    BROWS = sbuf("BROWS", [128, 2048], BF16)
    SC2 = sbuf("SC2", [128, 32], F32)
    RMAX = sbuf("RMAX", [1, 256], F32)
    TERN = sbuf("TERN", [16, 512], F32)
    TERNB = sbuf("TERNB", [16, 1536], BF16)
    VW = sbuf("VW", [128, 1024], BF16)
    RT = sbuf("RT", [128, 2048], BF16)
    SMF = sbuf("SMF", [128, 2048], F32)
    PT = sbuf("PT", [128, 1024], BF16)
    SW = sbuf("SW", [128, 1024], BF16)
    SCL = sbuf("SCL", [128, 16], F32)
    BIG = sbuf("BIG", [128, 57344], BF16)

    W8 = WBF[:, 0:8192].bitcast(FP8)
    XN = BIG[:, 0:16384]
    QB = BIG[:, 16384:32768]
    KG = BIG[:, 32768:40960]
    VG = BIG[:, 40960:49152]
    PPB = BIG[:, 49152:57344]
    XQ8 = BIG[:, 0:8192].bitcast(FP8)
    H8 = BIG[:, 8192:40960].bitcast(FP8)

    def tcol(hc):
        if hc < 40:
            return BIG, 16384 + hc * 1024
        if hc < 42:
            return SQ, (hc - 40) * 1024
        return RT, (hc - 42) * 1024

    PS = es.enter_context(nc.psum_tensor("PS", [128, 4096], F32))

    SY = Ctr(nc, es, "SY")
    GD = Ctr(nc, es, "GD")
    TE = Ctr(nc, es, "TE")
    DV = Ctr(nc, es, "DV")
    SC = Ctr(nc, es, "SC")
    CTRS = {"sync": SY, "gpsimd": GD, "tensor": TE, "vector": DV, "scalar": SC}

    channels = {}

    def chan(name):
        if name not in channels:
            channels[name] = Ctr(nc, es, "ch_" + name)
        return channels[name]

    def dma(queue, dst, src, wait=(), ch="g"):
        ctr = chan(ch)
        tok = ctr.bump(16)
        wl = [(c.sem, v) for (c, v) in wait if v > 0]

        def f(e, dst=dst, src=src, wl=wl):
            for (s, v) in wl:
                e.wait_ge(s, v)
            e.dma_start(out=dst, in_=src).then_inc(ctr.sem, 16)
        streams[queue].append(f)
        return (ctr, tok)

    def op(eng, fn, wait=(), inc=1):
        ctr = CTRS[eng]
        tok = ctr.bump(inc) if inc else ctr.v
        wl = [(c.sem, v) for (c, v) in wait if v > 0]

        def f(e, fn=fn, wl=wl, inc=inc):
            for (s, v) in wl:
                e.wait_ge(s, v)
            inst = fn(e)
            if inc:
                inst.then_inc(ctr.sem, inc)
        streams[eng].append(f)
        return tok

    def barrier():
        allc = list(CTRS.items()) + [("ch_" + k, c) for k, c in channels.items()]
        toks = [(c.sem, c.v) for _, c in allc if c.v > 0]
        for eng in ENGS:
            wl = [(s, v) for (s, v) in toks if s is not CTRS[eng].sem]

            def f(e, wl=wl):
                for (s, v) in wl:
                    e.wait_ge(s, v)
            streams[eng].append(f)

    dma("sync", IDENT[:], ident_in, ch="con")
    dma("sync", AONE[:], aone_in, ch="con")
    dma("sync", ONESROW[:], onesrow_in, ch="con")
    dma("sync", NEGROW[:], negrow_in, ch="con")
    dma("sync", CEPS[:], ceps_in, ch="con")
    dma("sync", COSX[:], cosx, ch="con")
    dma("sync", SINX[:], sinx, ch="con")
    dma("sync", LENS[:], lens_in, ch="con")
    dma("sync", QGC[:], qgc, ch="con")
    dma("sync", SGC[:], sg_c, ch="con")
    dma("sync", SUC[:], su_c, ch="con")
    dma("sync", SDC[:], sd_c, ch="con")
    dma("sync", L1BI[:], l1bias, ch="con")
    dma("sync", L1OB[:], l1ob, ch="con")
    barrier()

    def ln_stats_hbm(src_hbm, xb_dst):
        last_dv = 0
        sq_te = [0, 0]
        for c in range(CH):
            t_x = dma("sync", XCH[:], xview(src_hbm, c),
                      wait=[(DV, last_dv), (SC, SC.v if c > 0 else 0)], ch="xch")
            sc_tok = op("scalar", lambda e, c=c: e.copy(
                xb_dst[:, c * T:(c + 1) * T], XCH[:]), wait=[t_x])
            dv_tok = op("vector", lambda e, c=c: e.tensor_mul(
                SQ[:, (c % 2) * 1024:(c % 2) * 1024 + 1024],
                xb_dst[:, c * T:(c + 1) * T], xb_dst[:, c * T:(c + 1) * T]),
                wait=[(SC, sc_tok), (TE, sq_te[c % 2])])
            last_dv = dv_tok
            for hf in range(2):
                lo = 3072 + hf * 512
                op("tensor", lambda e, c=c, hf=hf, lo=lo: e.matmul(
                    PS[0:1, lo:lo + 512], AONE[:, 32:33],
                    xb_dst[:, c * T + hf * 512:c * T + hf * 512 + 512],
                    start=(c == 0), stop=(c == CH - 1)), wait=[(SC, sc_tok)])
                op("tensor", lambda e, c=c, hf=hf, lo=lo: e.matmul(
                    PS[32:33, lo:lo + 512], AONE[:, 32:33],
                    SQ[:, (c % 2) * 1024 + hf * 512:(c % 2) * 1024 + hf * 512 + 512],
                    start=(c == 0), stop=(c == CH - 1)), wait=[(DV, dv_tok)])
            sq_te[c % 2] = TE.v
        return TE.v

    def ln_stats_sbuf(cols):
        sq_te = [0, 0]
        for c in range(CH):
            tns, off = cols(c)
            dv_tok = op("vector", lambda e, tns=tns, off=off, c=c: e.tensor_mul(
                SQ[:, (c % 2) * 1024:(c % 2) * 1024 + 1024],
                tns[:, off:off + T], tns[:, off:off + T]),
                wait=[(TE, sq_te[c % 2])])
            for hf in range(2):
                lo = 3072 + hf * 512
                op("tensor", lambda e, tns=tns, off=off, c=c, hf=hf, lo=lo: e.matmul(
                    PS[0:1, lo:lo + 512], AONE[:, 32:33],
                    tns[:, off + hf * 512:off + hf * 512 + 512],
                    start=(c == 0), stop=(c == CH - 1)))
                op("tensor", lambda e, c=c, hf=hf, lo=lo: e.matmul(
                    PS[32:33, lo:lo + 512], AONE[:, 32:33],
                    SQ[:, (c % 2) * 1024 + hf * 512:(c % 2) * 1024 + hf * 512 + 512],
                    start=(c == 0), stop=(c == CH - 1)), wait=[(DV, dv_tok)])
            sq_te[c % 2] = TE.v
        return TE.v

    def ln_stats_udram(src_hbm_u):
        for c in range(CH):
            t_u = dma("sync", SW[:], xview(src_hbm_u, c),
                      wait=[(TE, TE.v if c > 0 else 0)], ch="sw")
            dv_sq = op("vector", lambda e, c=c: e.tensor_mul(
                SQ[:, (c % 2) * 1024:(c % 2) * 1024 + 1024], SW[:], SW[:]),
                wait=[t_u])
            for hf in range(2):
                lo = 3072 + hf * 512
                op("tensor", lambda e, c=c, hf=hf, lo=lo: e.matmul(
                    PS[0:1, lo:lo + 512], AONE[:, 32:33],
                    SW[:, hf * 512:hf * 512 + 512],
                    start=(c == 0), stop=(c == CH - 1)), wait=[t_u])
                op("tensor", lambda e, c=c, hf=hf, lo=lo: e.matmul(
                    PS[32:33, lo:lo + 512], AONE[:, 32:33],
                    SQ[:, (c % 2) * 1024 + hf * 512:(c % 2) * 1024 + hf * 512 + 512],
                    start=(c == 0), stop=(c == CH - 1)), wait=[(DV, dv_sq)])
        return TE.v

    def ln_smalls(te_stats, fp32_bcast=False):
        def f1(e):
            e.tensor_scalar(out=ROWS[0:1, 0:1024], in0=PS[0:1, 3072:4096],
                            scalar1=1.0 / D, scalar2=None, op0=ALU.mult)
            e.tensor_scalar(out=ROWS[0:1, 1024:2048], in0=PS[32:33, 3072:4096],
                            scalar1=1.0 / D, scalar2=None, op0=ALU.mult)
            e.drain()
            e.tensor_mul(ROWS[0:1, 2048:3072], ROWS[0:1, 0:1024],
                         ROWS[0:1, 0:1024])
            e.drain()
            return e.tensor_sub(ROWS[32:33, 0:1024], ROWS[0:1, 1024:2048],
                                ROWS[0:1, 2048:3072])
        d1 = op("vector", f1, wait=[(TE, te_stats)])
        s1 = op("scalar", lambda e: e.activation(
            out=ROWS[32:33, 1024:2048], in_=ROWS[32:33, 0:1024], func=AF.Sqrt,
            bias=CEPS[32:33, 0:1], scale=1.0), wait=[(DV, d1)])

        def f2(e):
            e.reciprocal(ROWS[0:1, 2048:3072], ROWS[32:33, 1024:2048])
            e.drain()
            e.tensor_mul(ROWS[32:33, 0:1024], ROWS[0:1, 0:1024],
                         ROWS[0:1, 2048:3072])
            e.drain()
            e.tensor_copy(BROWS[0:1, 0:1024], ROWS[0:1, 2048:3072])
            e.tensor_copy(BROWS[32:33, 0:1024], ROWS[32:33, 0:1024])
            if fp32_bcast:
                e.drain()
                e.tensor_copy(ROWS[0:1, 1024:2048], ROWS[32:33, 0:1024])
                e.tensor_copy(SMF[0:1, 0:2048].bitcast(BF16)[:, 0:1024],
                              ROWS[32:33, 0:1024])
                e.drain()
                e.tensor_sub(ROWS[64:65, 0:1024], ROWS[0:1, 2048:3072],
                             BROWS[0:1, 0:1024])
                e.tensor_sub(ROWS[64:65, 1024:2048], ROWS[0:1, 1024:2048],
                             SMF[0:1, 0:2048].bitcast(BF16)[:, 0:1024])
                e.drain()
                e.tensor_copy(BROWS[0:1, 1024:2048], ROWS[64:65, 0:1024])
                return e.tensor_copy(
                    SMF[0:1, 0:2048].bitcast(BF16)[:, 1024:2048],
                    ROWS[64:65, 1024:2048])
            return e.engine_nop()
        d2 = op("vector", f2, wait=[(SC, s1)])
        for hf in range(2):
            sl = slice(hf * 512, hf * 512 + 512)
            sl2 = slice(1024 + hf * 512, 1024 + hf * 512 + 512)
            op("tensor", lambda e, sl=sl: e.matmul(
                PS[:, sl], ONESROW[0:1, :], BROWS[0:1, sl],
                start=True, stop=not fp32_bcast), wait=[(DV, d2)])
            if fp32_bcast:
                op("tensor", lambda e, sl=sl: e.matmul(
                    PS[:, sl], ONESROW[0:1, :],
                    BROWS[0:1, 1024 + sl.start:1024 + sl.start + 512],
                    start=False, stop=True))
            if not fp32_bcast:
                op("tensor", lambda e, sl=sl, sl2=sl2: e.matmul(
                    PS[:, sl2], NEGROW[32:33, :], BROWS[32:33, sl],
                    start=True, stop=True))
            else:
                op("tensor", lambda e, sl=sl, sl2=sl2: e.matmul(
                    PS[:, sl2], NEGROW[0:1, :],
                    SMF[0:1, 0:2048].bitcast(BF16)[:, sl],
                    start=True, stop=False))
                op("tensor", lambda e, sl=sl, sl2=sl2: e.matmul(
                    PS[:, sl2], NEGROW[0:1, :],
                    SMF[0:1, 0:2048].bitcast(BF16)[:, 1024 + sl.start:
                                                   1024 + sl.start + 512],
                    start=False, stop=True))
        return TE.v

    def ln_apply(src_hbm, te_bc, dst_bf):
        last = 0
        sc_gate = SC.v
        for c in range(CH):
            t_x = dma("sync", XCH[:], xview(src_hbm, c),
                      wait=[(DV, last)] + ([(SC, sc_gate)] if c == 0 else []),
                      ch="xch")

            def f(e, c=c):
                e.tensor_mul(XC2[:], XCH[:], PS[:, 0:1024])
                e.drain()
                return e.tensor_add(dst_bf[:, c * T:(c + 1) * T],
                                    XC2[:], PS[:, 1024:2048])
            last = op("vector", f, wait=[t_x, (TE, te_bc)])
        return last

    def ln2_residual(u_cols, src_hbm, dst_hbm, te_bc, u_hbm=None):
        last_store = None
        last_dv = 0
        te_gate = TE.v
        for c in range(CH):
            if u_hbm is not None:
                t_u = dma("sync", SW[:], xview(u_hbm, c),
                          wait=[(DV, last_dv)] +
                               ([(TE, te_gate)] if c == 0 else []),
                          ch="sw")
                usrc = SW[:]
                wl_extra = [t_u]
            else:
                tns, off = u_cols(c)
                usrc = tns[:, off:off + T]
                wl_extra = []
            t_x = dma("sync", XCH[:], xview(src_hbm, c),
                      wait=([last_store] if last_store else
                            [(DV, DV.v), (SC, SC.v)]), ch="xch")

            def f(e, usrc=usrc):
                e.tensor_mul(XC2[:], usrc, PS[:, 0:1024])
                e.drain()
                e.tensor_add(XC2[:], XC2[:], PS[:, 1024:2048])
                e.drain()
                return e.tensor_add(XC2[:], XC2[:], XCH[:])
            d = op("vector", f, wait=[t_x, (TE, te_bc)] + wl_extra)
            last_dv = d
            last_store = dma("gpsimd", xview(dst_hbm, c), XC2[:],
                             wait=[(DV, d)], ch="xst")
        return last_store

    def tern_prep(w_hbm, s_off, slot0, mean_free_dv=0, wbf_free_te=0):
        stop_at = PHASE_LIMIT
        """Ternary prep of a 512-col superchunk into WBF slots slot0..+15.
        mean_free_dv: PS bank4 free (prev superchunk's v-ops done).
        wbf_free_te: WBF slots free (matmuls of 2-ago superchunk done)."""
        sc_toks = []
        te_means = [TE.v, TE.v]
        sc_gate = SC.v
        for g in range(16):
            t_w = dma("sync", WRAW[:, (g % 2) * 512:(g % 2) * 512 + 512],
                      w_hbm[g * 128:(g + 1) * 128, s_off:s_off + 512],
                      wait=[(SC, sc_toks[g - 2] if g >= 2 else sc_gate)],
                      ch="wr%d" % (g % 2))

            def fc(e, g=g):
                sl = slice((g % 2) * 512, (g % 2) * 512 + 512)
                e.copy(WBF[:, (slot0 + g) * 512:(slot0 + g) * 512 + 512],
                       WRAW[:, sl])
                return e.activation(out=SW[:, (g % 2) * 512:(g % 2) * 512 + 512],
                                    in_=WRAW[:, sl], func=AF.Abs)
            st = op("scalar", fc,
                    wait=[t_w, (TE, te_means[g % 2])] +
                         ([(TE, wbf_free_te)] if g == 0 else []))
            sc_toks.append(st)
            op("tensor", lambda e, g=g: e.matmul(
                PS[0:16, 2048:2560], AONE[:, 32 - g:48 - g],
                SW[:, (g % 2) * 512:(g % 2) * 512 + 512],
                start=(g == 0), stop=(g == 15)),
                wait=[(SC, st)] + ([(DV, mean_free_dv)] if g == 0 else []))
            te_means[g % 2] = TE.v

        def fsm(e):
            e.tensor_scalar(out=TERN[0:16, 0:512], in0=PS[0:16, 2048:2560],
                            scalar1=1.0 / 128.0, scalar2=None, op0=ALU.mult)
            e.drain()
            e.tensor_copy(TERNB[0:16, 0:512], TERN[0:16, 0:512])
            e.drain()
            e.tensor_scalar(out=TERNB[0:16, 0:512], in0=TERNB[0:16, 0:512],
                            scalar1=9.99999993922529e-09, scalar2=None,
                            op0=ALU.max)
            e.drain()
            e.reciprocal(TERN[0:16, 0:512], TERNB[0:16, 0:512])
            e.drain()
            e.tensor_copy(TERNB[0:16, 512:1024], TERN[0:16, 0:512])
            e.drain()
            e.tensor_sub(ROWS[32:48, 0:512], TERN[0:16, 0:512],
                         TERNB[0:16, 512:1024])
            e.drain()
            return e.tensor_copy(TERNB[0:16, 1024:1536], ROWS[32:48, 0:512])
        if stop_at <= 1.1:
            return SC.v and DV.v
        dsm = op("vector", fsm, wait=[(TE, TE.v)])
        if stop_at <= 1.15:
            return dsm

        dv_last = dsm
        stg_te = TE.v
        for g in range(16):
            gs = dma("gpsimd", BROWS[0:1, 0:512],
                     TERNB[g:g + 1, 0:512],
                     wait=[(DV, dsm), (TE, stg_te)], ch="stg0")
            gs = dma("gpsimd", BROWS[0:1, 512:1024],
                     TERNB[g:g + 1, 512:1024], ch="stg0")
            gs = dma("gpsimd", BROWS[0:1, 1024:1536],
                     TERNB[g:g + 1, 1024:1536], ch="stg0")
            if stop_at <= 1.18:
                continue
            op("tensor", lambda e: e.matmul(
                PS[:, 2048:2560], ONESROW[0:1, :], BROWS[0:1, 512:1024],
                start=True, stop=False), wait=[gs, (DV, dv_last)])
            op("tensor", lambda e: e.matmul(
                PS[:, 2048:2560], ONESROW[0:1, :], BROWS[0:1, 1024:1536],
                start=False, stop=True))
            op("tensor", lambda e: e.matmul(
                PS[:, 2560:3072], ONESROW[0:1, :], BROWS[0:1, 0:512],
                start=True, stop=True))
            stg_te = TE.v
            if stop_at <= 1.19:
                continue
            te_bc = TE.v

            def fv(e, g=g):
                wsl = slice((slot0 + g) * 512, (slot0 + g) * 512 + 512)
                e.tensor_mul(VW[:, 0:512], WBF[:, wsl], PS[:, 2048:2560])
                e.drain()
                e.tensor_scalar(out=VW[:, 512:1024], in0=VW[:, 0:512],
                                scalar1=-1.25, scalar2=1.25,
                                op0=ALU.max, op1=ALU.min)
                e.drain()
                e.tensor_scalar(out=VW[:, 0:512], in0=VW[:, 512:1024],
                                scalar1=CROUND, scalar2=CROUND,
                                op0=ALU.add, op1=ALU.subtract)
                e.drain()
                return e.tensor_mul(WBF[:, wsl], VW[:, 0:512],
                                    PS[:, 2560:3072])
            dv_last = op("vector", fv, wait=[(TE, te_bc)])
        if stop_at <= 1.19:
            return dsm
        return dv_last

    def cast_prep(w_hbm, s_off, slot0, ncols=512, wbf_free_te=0):
        toks = [SC.v, SC.v]
        for g in range(16):
            t_w = dma("sync", WRAW[:, (g % 2) * 512:(g % 2) * 512 + ncols],
                      w_hbm[g * 128:(g + 1) * 128, s_off:s_off + ncols],
                      wait=[(SC, toks[g % 2])], ch="wr%d" % (g % 2))
            st = op("scalar", lambda e, g=g: e.copy(
                WBF[:, (slot0 + g) * 512:(slot0 + g) * 512 + ncols],
                WRAW[:, (g % 2) * 512:(g % 2) * 512 + ncols]),
                wait=[t_w] + ([(TE, wbf_free_te)] if g == 0 else []))
            toks[g % 2] = st
        return st

    def rowmax_to_scalar(scl_col, sc2_col, wait_dv):
        g1 = dma("gpsimd", scr128, SCL[:, scl_col:scl_col + 1],
                 wait=[(DV, wait_dv)], ch="msc")
        g2 = dma("gpsimd", RMAX[0:1, 0:128],
                 scr128.rearrange("p one -> (one) (p)"), wait=[g1], ch="msc")
        d = op("vector", lambda e: e.tensor_reduce(
            out=SC2[0:1, sc2_col:sc2_col + 1], in_=RMAX[0:1, 0:128],
            axis=mybir.AxisListType.X, op=ALU.max), wait=[g2])
        return d

    def bcast_scalars_f32(sc2_srcs, scl_dst0, wait_dv):
        n = len(sc2_srcs)

        def fhl(e):
            for i, c in enumerate(sc2_srcs):
                e.tensor_copy(BROWS[0:1, 1536 + i:1537 + i],
                              SC2[0:1, c:c + 1])
            e.drain()
            for i, c in enumerate(sc2_srcs):
                e.tensor_sub(SC2[0:1, 24 + i:25 + i], SC2[0:1, c:c + 1],
                             BROWS[0:1, 1536 + i:1537 + i])
            e.drain()
            for i in range(n):
                e.tensor_copy(TERNB[0:1, i:i + 1],
                              SC2[0:1, 24 + i:25 + i])
            return e.engine_nop()
        d_hl = op("vector", fhl, wait=[(DV, wait_dv)])
        for i in range(n):
            op("tensor", lambda e, i=i: e.matmul(
                PS[:, 2048 + i:2048 + i + 1], ONESROW[0:1, :],
                BROWS[0:1, 1536 + i:1537 + i], start=True, stop=False),
                wait=[(DV, d_hl)] if i == 0 else [])
            op("tensor", lambda e, i=i: e.matmul(
                PS[:, 2048 + i:2048 + i + 1], ONESROW[0:1, :],
                TERNB[0:1, i:i + 1], start=False, stop=True))
        d_sc = op("vector", lambda e: e.tensor_copy(
            SCL[:, scl_dst0:scl_dst0 + n], PS[:, 2048:2048 + n]),
            wait=[(TE, TE.v)])
        return d_sc

    def attention(kv_of_head, qsrc, ysrc):
        last_tmr = 0
        last_exp = 0
        last_ptte = 0
        last_yev = 0
        for j in range(8):
            L2 = (j + 1) * 256
            for h in range(16):
                kv = kv_of_head(h)
                for (lo, hi) in _bank_splits(2048, 2048 + L2):
                    op("tensor", lambda e, j=j, h=h, kv=kv, lo=lo, hi=hi: e.matmul(
                        PS[:, lo:hi],
                        qsrc[:, h * T + j * 128:h * T + j * 128 + 128],
                        KG[:, kv * 2048 + (lo - 2048):kv * 2048 + (hi - 2048)],
                        start=True, stop=True),
                        wait=([(DV, last_tmr)] if lo == 2048 else []))
                te_s = TE.v
                def fm(e, j=j, L2=L2):
                    e.tensor_mask_reduce(
                        out=SMF[:, 0:L2], in_=PS[:, 2048:2048 + L2],
                        mask_start=0.0, mask_end=LENS[:, j:j + 1], scale=-0.5,
                        accum_in=POSBIG, op=ALU.min, accum_out=SCL[:, 0:1])
                    e.drain()
                    return e.tensor_scalar(out=SCL[:, 1:2], in0=SCL[:, 0:1],
                                           scalar1=2.0, scalar2=None,
                                           op0=ALU.mult)
                dv_m = op("vector", fm, wait=[(TE, te_s), (SC, last_exp)])
                last_tmr = dv_m
                sc_e = op("scalar", lambda e, L2=L2: e.activation(
                    out=PPB[:, 0:L2], in_=SMF[:, 0:L2], func=AF.Exp,
                    bias=SCL[:, 1:2], scale=-2.0, accum_out=SCL[:, 2:3]),
                    wait=[(DV, dv_m), (TE, last_ptte)])
                last_exp = sc_e

                def fd(e):
                    e.reciprocal(SCL[:, 3:4], SCL[:, 2:3])
                    e.drain()
                    return e.tensor_scalar(out=VW[:, 0:128], in0=IDENT[:],
                                           scalar1=SCL[:, 3:4], scalar2=None,
                                           op0=ALU.mult)
                dv_d = op("vector", fd, wait=[(SC, sc_e)])
                nb = L2 // 128
                for b4 in range(0, nb, 4):
                    bb = min(4, nb - b4)
                    bank = (b4 // 4) % 2
                    for b in range(b4, b4 + bb):
                        op("tensor", lambda e, b=b, b4=b4, bank=bank: e.matmul(
                            PS[:, bank * 512 + (b - b4) * 128:
                               bank * 512 + (b - b4) * 128 + 128],
                            PPB[:, b * 128:b * 128 + 128], VW[:, 0:128],
                            start=True, stop=True),
                            wait=([(DV, dv_d if b4 == 0 else DV.v)]
                                  if b == b4 else []))
                    te_pt = TE.v
                    dv_e = op("vector",
                              lambda e, bank=bank, bb=bb: e.tensor_copy(
                                  PT[:, bank * 512:bank * 512 + bb * 128],
                                  PS[:, bank * 512:bank * 512 + bb * 128]),
                              wait=[(TE, te_pt)])
                    for b in range(b4, b4 + bb):
                        op("tensor", lambda e, b=b, b4=b4, bank=bank, kv=kv, nb=nb: e.matmul(
                            PS[:, 1024:1152],
                            VG[:, kv * 2048 + b * 128:kv * 2048 + b * 128 + 128],
                            PT[:, bank * 512 + (b - b4) * 128:
                               bank * 512 + (b - b4) * 128 + 128],
                            start=(b == 0), stop=(b == nb - 1)),
                            wait=([(DV, dv_e)] if b == b4 else
                                  ([(DV, last_yev)] if b == 0 else [])))
                last_ptte = TE.v
                last_yev = op("vector", lambda e, j=j, h=h: e.tensor_copy(
                    ysrc[:, h * T + j * 128:h * T + j * 128 + 128],
                    PS[:, 1024:1152]), wait=[(TE, last_ptte)])

    def rms_bcast_row(row_ap_bf, ps_lo, base=0, wait=()):
        for hf in range(2):
            op("tensor", lambda e, hf=hf: e.matmul(
                PS[:, ps_lo + hf * 512:ps_lo + hf * 512 + 512],
                ONESROW[base:base + 1, :],
                row_ap_bf[:, hf * 512:hf * 512 + 512], start=True, stop=True),
                wait=(list(wait) if hf == 0 else []))
        return TE.v

    def block_l0_attn(src_hbm, dst_hbm):
        def phase_done(n):
            return PHASE_LIMIT <= n
        te_st = ln_stats_hbm(src_hbm, XN)
        te_bc = ln_smalls(te_st)
        xn_tok = ln_apply(src_hbm, te_bc, XN)
        barrier()
        if phase_done(1):
            return

        ccA_in_v = ccA_in.rearrange("a (k p c) -> a k p c", p=128, c=1024)
        prep_tok = tern_prep(wqkvT, 0, 0)
        if phase_done(1.2):
            barrier()
            return
        te_mm_end = [0, 0]
        ss_te = [0, 0]
        evac_tok = [0, 0]
        for s in range(6 if not phase_done(1.5) else 1):
            slot0 = (s % 2) * 16
            for o4 in range(4):
                oc = s * 4 + o4
                pb = (o4 % 2) * 1024
                for hf in range(2):
                    for g in range(CH):
                        wl = []
                        if g == 0 and hf == 0:
                            if o4 == 0:
                                wl = [(DV, prep_tok), (DV, xn_tok)]
                            wl = wl + [evac_tok[o4 % 2]] if evac_tok[o4 % 2] else wl
                        op("tensor", lambda e, slot0=slot0, g=g, o4=o4, hf=hf, pb=pb: e.matmul(
                            PS[:, pb + hf * 512:pb + hf * 512 + 512],
                            WBF[:, (slot0 + g) * 512 + o4 * 128:
                                (slot0 + g) * 512 + o4 * 128 + 128],
                            XN[:, g * T + hf * 512:g * T + hf * 512 + 512],
                            start=(g == 0), stop=(g == CH - 1)), wait=wl)
                te_mm = TE.v
                if oc < 16:
                    dst = QB[:, oc * T:(oc + 1) * T]
                elif oc < 20:
                    dst = KG[:, (oc - 16) * 2048:(oc - 16) * 2048 + 1024]
                else:
                    dst = PPB[:, (oc - 20) * 1024:(oc - 20) * 1024 + 1024]
                sc_ev = op("scalar", lambda e, dst=dst, pb=pb: e.copy(
                    dst, PS[:, pb:pb + 1024]), wait=[(TE, te_mm)])
                evac_tok[o4 % 2] = (SC, sc_ev)
                if oc < 20:
                    dv_sq = op("vector", lambda e, dst=dst, oc=oc: e.tensor_mul(
                        SQ[:, (oc % 2) * 1024:(oc % 2) * 1024 + 1024],
                        dst, dst), wait=[(SC, sc_ev), (TE, ss_te[oc % 2])])
                    for hf in range(2):
                        op("tensor", lambda e, oc=oc, hf=hf: e.matmul(
                            PS[0:32, 3072 + hf * 512:3072 + hf * 512 + 512],
                            AONE[:, 32 - oc:64 - oc],
                            SQ[:, (oc % 2) * 1024 + hf * 512:
                               (oc % 2) * 1024 + hf * 512 + 512],
                            start=(oc == 0), stop=(oc == 19)),
                            wait=[(DV, dv_sq)])
                    ss_te[oc % 2] = TE.v
            te_mm_end[s % 2] = TE.v
            if s < 5 and not phase_done(1.5):
                prep_tok = tern_prep(wqkvT, (s + 1) * 512, ((s + 1) % 2) * 16,
                                     mean_free_dv=prep_tok,
                                     wbf_free_te=te_mm_end[(s + 1) % 2])
        barrier()
        if phase_done(1.5):
            return
        if phase_done(2):
            return

        d_r = op("vector", lambda e: e.tensor_scalar(
            out=ROWS[0:20, 0:1024], in0=PS[0:20, 3072:4096],
            scalar1=1.0 / 128.0, scalar2=None, op0=ALU.mult))
        s_r = op("scalar", lambda e: e.activation(
            out=ROWS[32:52, 0:1024], in_=ROWS[0:20, 0:1024], func=AF.Sqrt,
            bias=CEPS[0:20, 1:2], scale=1.0), wait=[(DV, d_r)])

        def frms2(e):
            e.reciprocal(ROWS[0:20, 0:1024], ROWS[32:52, 0:1024])
            e.drain()
            e.tensor_scalar(out=ROWS[0:16, 0:1024], in0=ROWS[0:16, 0:1024],
                            scalar1=QGC[:, 0:1], scalar2=None, op0=ALU.mult)
            e.drain()
            return e.tensor_copy(BROWS[0:20, 1024:2048], ROWS[0:20, 0:1024])
        d_r2 = op("vector", frms2, wait=[(SC, s_r)])

        d_rope = d_r2
        te_b = 0
        for idx in range(20):
            if idx < 16:
                src_c = QB[:, idx * T:(idx + 1) * T]
            else:
                src_c = KG[:, (idx - 16) * 2048:(idx - 16) * 2048 + 1024]
            g1 = dma("gpsimd", SW[0:64, :], src_c[64:128, :],
                     wait=[(DV, d_rope)], ch="swp")
            g2 = dma("gpsimd", SW[64:128, :], src_c[0:64, :], ch="swp")
            gs = dma("gpsimd", BROWS[0:1, 0:1024],
                     BROWS[idx:idx + 1, 1024:2048],
                     wait=[(DV, d_r2), (TE, te_b)], ch="stg0")
            te_b = rms_bcast_row(BROWS[0:1, 0:1024], 2048, base=0,
                                 wait=[(DV, d_rope), gs])

            def frope(e, src_c=src_c):
                e.tensor_mul(XCH[:], src_c, COSX[:])
                e.tensor_mul(XC2[:], SW[:], SINX[:])
                e.drain()
                e.tensor_add(XCH[:], XCH[:], XC2[:])
                e.drain()
                return e.tensor_mul(src_c, XCH[:], PS[:, 2048:3072])
            d_rope = op("vector", frope, wait=[g2, (TE, te_b)])
            if idx >= 16:
                dma("gpsimd", ccA_in_v[0, idx - 16], src_c,
                    wait=[(DV, d_rope)], ch="blob")
        dv_prev = d_rope
        for vv in range(4):
            for b in range(8):
                te_t = op("tensor", lambda e, vv=vv, b=b: e.transpose(
                    PS[:, 2048 + (b % 2) * 512:2048 + (b % 2) * 512 + 64].bitcast(BF16),
                    PPB[:, vv * 1024 + b * 128:vv * 1024 + b * 128 + 128],
                    IDENT[:]), wait=[(DV, dv_prev)])
                dv_prev = op("vector", lambda e, vv=vv, b=b: e.tensor_copy(
                    VG[:, vv * 2048 + b * 128:vv * 2048 + b * 128 + 128],
                    PS[:, 2048 + (b % 2) * 512:2048 + (b % 2) * 512 + 64].bitcast(BF16)),
                    wait=[(TE, te_t)])
            dma("gpsimd", ccA_in_v[0, 4 + vv],
                VG[:, vv * 2048:vv * 2048 + 1024], wait=[(DV, dv_prev)],
                ch="blob")
        barrier()
        if phase_done(3):
            return

        ccc = chan("cc")
        blobc = chan("blob")
        cc_tok = (ccc, ccc.bump(1))
        blob_full = (blobc, blobc.v)

        def fcc_emit(e, bf=blob_full):
            e.wait_ge(bf[0].sem, bf[1])
            inst = e.collective_compute(
                "AllGather", ALU.bypass,
                replica_groups=[[0, 1], [2, 3], [4, 5], [6, 7]],
                ins=[ccA_in], outs=[ccA_out])
            inst.then_inc(ccc.sem, 1)
        streams["gpsimd"].append(fcc_emit)

        ccA_out_v = ccA_out.rearrange("a (r k p c) -> a r k p c",
                                      r=2, p=128, c=1024)
        for kk in range(4):
            for par in range(2):
                dma("gpsimd",
                    KG[:, kk * 2048:kk * 2048 + 2048].rearrange(
                        "p (b two c) -> p b two c", two=2, c=128)[:, :, par],
                    ccA_out_v[0, par, kk].rearrange("p (b c) -> p b c", c=128),
                    wait=([cc_tok] if (kk == 0 and par == 0) else []), ch="gath")
        for vv in range(4):
            for par in range(2):
                dma("gpsimd",
                    VG[:, vv * 2048:vv * 2048 + 2048].rearrange(
                        "p (b two c) -> p b two c", two=2, c=128)[:, :, par],
                    ccA_out_v[0, par, 4 + vv].rearrange("p (b c) -> p b c",
                                                        c=128), ch="gath")
        barrier()
        if phase_done(4):
            return

        attention(lambda h: h // 4, QB, XN)
        barrier()
        if phase_done(5):
            return

        te_ys = ln_stats_sbuf(lambda c: (BIG, c * T))

        def fyr(e):
            return e.tensor_scalar(out=ROWS[0:1, 0:1024],
                                   in0=PS[32:33, 3072:4096],
                                   scalar1=1.0 / D, scalar2=None, op0=ALU.mult)
        d_y = op("vector", fyr, wait=[(TE, te_ys)])
        s_y = op("scalar", lambda e: e.activation(
            out=ROWS[32:33, 0:1024], in_=ROWS[0:1, 0:1024], func=AF.Sqrt,
            bias=CEPS[0:1, 1:2], scale=1.0), wait=[(DV, d_y)])

        def fyr2(e):
            e.reciprocal(ROWS[0:1, 0:1024], ROWS[32:33, 0:1024])
            e.drain()
            return e.tensor_copy(BROWS[0:1, 0:1024], ROWS[0:1, 0:1024])
        d_y2 = op("vector", fyr2, wait=[(SC, s_y)])
        te_yb = rms_bcast_row(BROWS[0:1, 0:1024], 0, wait=[(DV, d_y2)])
        yn_tok = 0
        for c in range(CH):
            yn_tok = op("vector", lambda e, c=c: e.tensor_mul(
                QB[:, c * T:(c + 1) * T], XN[:, c * T:(c + 1) * T],
                PS[:, 0:1024]), wait=[(TE, te_yb)])
        barrier()

        prep_tok = tern_prep(wprojT, 0, 0)
        te_mm_end = [0, 0]
        evac_tok = [None, None]
        for s in range(4):
            slot0 = (s % 2) * 16
            for o4 in range(4):
                oc = s * 4 + o4
                pb = (o4 % 2) * 1024
                for hf in range(2):
                    for g in range(CH):
                        wl = []
                        if g == 0 and hf == 0:
                            if o4 == 0:
                                wl = [(DV, prep_tok), (DV, yn_tok)]
                            if evac_tok[o4 % 2]:
                                wl = wl + [evac_tok[o4 % 2]]
                        op("tensor", lambda e, slot0=slot0, g=g, o4=o4, hf=hf, pb=pb: e.matmul(
                            PS[:, pb + hf * 512:pb + hf * 512 + 512],
                            WBF[:, (slot0 + g) * 512 + o4 * 128:
                                (slot0 + g) * 512 + o4 * 128 + 128],
                            QB[:, g * T + hf * 512:g * T + hf * 512 + 512],
                            start=(g == 0), stop=(g == CH - 1)), wait=wl)
                te_mm = TE.v
                sc_ev = op("scalar", lambda e, oc=oc, pb=pb: e.copy(
                    XN[:, oc * T:(oc + 1) * T], PS[:, pb:pb + 1024]),
                    wait=[(TE, te_mm)])
                evac_tok[o4 % 2] = (SC, sc_ev)
            te_mm_end[s % 2] = TE.v
            if s < 3:
                prep_tok = tern_prep(wprojT, (s + 1) * 512, ((s + 1) % 2) * 16,
                                     mean_free_dv=prep_tok,
                                     wbf_free_te=te_mm_end[(s + 1) % 2])
        barrier()

        te_st2 = ln_stats_sbuf(lambda c: (BIG, c * T))
        te_bc2 = ln_smalls(te_st2)
        ln2_residual(lambda c: (BIG, c * T), src_hbm, dst_hbm, te_bc2)
        barrier()

    def block_l0_ffn(src_hbm, dst_hbm):
        te_st = ln_stats_hbm(src_hbm, XN)
        te_bc = ln_smalls(te_st)
        xn_tok = ln_apply(src_hbm, te_bc, XN)
        barrier()

        te_mm_last = 0
        ev_last = []
        for sc_i in range(11):
            prep_toks = []
            u8_tok = [DV.v] * 4
            for w_i, wsrc in ((0, wqgT), (16, wquT)):
                for g in range(16):
                    slot = (0 if w_i == 0 else 2) + (g % 2)
                    t_w = dma("sync", WU8[:, slot * 512:slot * 512 + 512],
                              wsrc[g * 128:(g + 1) * 128,
                                   sc_i * 512:sc_i * 512 + 512],
                              wait=[(DV, u8_tok[slot])], ch="wu%d" % slot)
                    tok = op("vector", lambda e, w_i=w_i, g=g, slot=slot: e.tensor_scalar(
                        out=WBF[:, (w_i + g) * 512:(w_i + g) * 512 + 512],
                        in0=WU8[:, slot * 512:slot * 512 + 512],
                        scalar1=128.0, scalar2=None, op0=ALU.subtract),
                        wait=[t_w] +
                             ([(TE, te_mm_last)] if (w_i == 0 and g == 0) else []))
                    u8_tok[slot] = tok
                    prep_toks.append(tok)
            prep_done = prep_toks[-1]
            for q_i in range(4):
                hc = sc_i * 4 + q_i
                for hf in range(2):
                    for g in range(CH):
                        wl = []
                        if g == 0 and hf == 0:
                            wl = [(DV, prep_done)] + \
                                ([(DV, xn_tok)] if hc == 0 else []) + ev_last
                        op("tensor", lambda e, g=g, q_i=q_i, hf=hf: e.matmul(
                            PS[:, hf * 512:hf * 512 + 512],
                            WBF[:, g * 512 + q_i * 128:g * 512 + q_i * 128 + 128],
                            XN[:, g * T + hf * 512:g * T + hf * 512 + 512],
                            start=(g == 0), stop=(g == CH - 1)), wait=wl)
                    for g in range(CH):
                        op("tensor", lambda e, g=g, q_i=q_i, hf=hf: e.matmul(
                            PS[:, 1024 + hf * 512:1024 + hf * 512 + 512],
                            WBF[:, (16 + g) * 512 + q_i * 128:
                                (16 + g) * 512 + q_i * 128 + 128],
                            XN[:, g * T + hf * 512:g * T + hf * 512 + 512],
                            start=(g == 0), stop=(g == CH - 1)))
                te_mm = TE.v
                s_si = op("scalar", lambda e, hc=hc: e.activation(
                    out=VW[:], in_=PS[:, 0:1024], func=AF.Silu,
                    bias=0.0, scale=SGC[:, hc:hc + 1]), wait=[(TE, te_mm)])
                tns, off = tcol(hc)
                d_t = op("vector", lambda e, tns=tns, off=off, hc=hc: e.scalar_tensor_tensor(
                    out=tns[:, off:off + T], in0=PS[:, 1024:2048],
                    scalar=SUC[:, hc:hc + 1], in1=VW[:],
                    op0=ALU.mult, op1=ALU.mult), wait=[(SC, s_si)])
                ev_last = [(SC, s_si), (DV, d_t)]
            te_mm_last = TE.v
        barrier()

        te_mm_last = 0
        ev_last = [None, None]
        ust_last = [None]
        for os_i in range(8):
            prep_toks = []
            u8_tok = [DV.v] * 4
            for g in range(44):
                slot = g % 4
                t_w = dma("sync", WU8[:, slot * 512:slot * 512 + 256],
                          wqdT[g * 128:(g + 1) * 128,
                               os_i * 256:os_i * 256 + 256],
                          wait=[(DV, u8_tok[slot])], ch="wu%d" % slot)
                tok = op("vector", lambda e, g=g, slot=slot: e.tensor_scalar(
                    out=BIG[:, g * 256:g * 256 + 256],
                    in0=WU8[:, slot * 512:slot * 512 + 256],
                    scalar1=128.0, scalar2=None, op0=ALU.subtract),
                    wait=[t_w] +
                         ([(TE, te_mm_last)] if g == 0 else []))
                u8_tok[slot] = tok
                prep_toks.append(tok)
            prep_done = prep_toks[-1]
            for oc2 in range(2):
                oc = os_i * 2 + oc2
                pb = oc2 * 1024
                for hf in range(2):
                    for g in range(44):
                        tns, off = tcol(g)
                        wl = []
                        if g == 0 and hf == 0:
                            wl = [(DV, prep_done)]
                            if ev_last[oc2]:
                                wl = wl + [ev_last[oc2]]
                        op("tensor", lambda e, g=g, oc2=oc2, hf=hf, pb=pb, tns=tns, off=off: e.matmul(
                            PS[:, pb + hf * 512:pb + hf * 512 + 512],
                            BIG[:, g * 256 + oc2 * 128:g * 256 + oc2 * 128 + 128],
                            tns[:, off + hf * 512:off + hf * 512 + 512],
                            start=(g == 0), stop=(g == 43)), wait=wl)
                te_mm = TE.v
                d_ev = op("vector", lambda e, oc=oc, pb=pb: e.tensor_scalar(
                    out=PT[:], in0=PS[:, pb:pb + 1024],
                    scalar1=SDC[:, oc:oc + 1], scalar2=None, op0=ALU.mult),
                    wait=[(TE, te_mm)] + ([ust_last[0]] if ust_last[0] else []))
                ust_last[0] = dma("gpsimd", xview(u_dram, oc), PT[:],
                                  wait=[(DV, d_ev)], ch="ust")
                ev_last[oc2] = (DV, d_ev)
            te_mm_last = TE.v
        barrier()

        te_st2 = ln_stats_udram(u_dram)
        te_bc2 = ln_smalls(te_st2)
        ln2_residual(None, src_hbm, dst_hbm, te_bc2, u_hbm=u_dram)
        barrier()

    def block_l1_attn(src_hbm, dst_hbm):
        te_st = ln_stats_hbm(src_hbm, XN)
        te_bc = ln_smalls(te_st)
        xn_tok = ln_apply(src_hbm, te_bc, XN)
        barrier()

        ccB_in_v = ccB_in.rearrange("a (k p c) -> a k p c", p=128, c=1024)
        prep_tok = cast_prep(l1wT, 0, 0)
        te_mm_end = [0, 0]
        evac_tok = [None, None]
        for s in range(5):
            slot0 = (s % 2) * 16
            ncols = 512 if s < 4 else 256
            for o4 in range(ncols // 128):
                oc = s * 4 + o4
                pb = (o4 % 2) * 1024
                for hf in range(2):
                    for g in range(CH):
                        wl = []
                        if g == 0 and hf == 0:
                            if o4 == 0:
                                wl = [(SC, prep_tok), (DV, xn_tok)]
                            if evac_tok[o4 % 2]:
                                wl = wl + [evac_tok[o4 % 2]]
                        op("tensor", lambda e, slot0=slot0, g=g, o4=o4, hf=hf, pb=pb: e.matmul(
                            PS[:, pb + hf * 512:pb + hf * 512 + 512],
                            WBF[:, (slot0 + g) * 512 + o4 * 128:
                                (slot0 + g) * 512 + o4 * 128 + 128],
                            XN[:, g * T + hf * 512:g * T + hf * 512 + 512],
                            start=(g == 0), stop=(g == CH - 1)), wait=wl)
                te_mm = TE.v
                if oc < 16:
                    d_e = op("vector", lambda e, oc=oc, pb=pb: e.tensor_scalar(
                        out=QB[:, oc * T:(oc + 1) * T], in0=PS[:, pb:pb + 1024],
                        scalar1=L1BI[:, oc:oc + 1], scalar2=INV_SQRT_HD,
                        op0=ALU.add, op1=ALU.mult), wait=[(TE, te_mm)])
                    evac_tok[o4 % 2] = (DV, d_e)
                elif oc == 16:
                    d_e = op("vector", lambda e, pb=pb: e.tensor_scalar(
                        out=RT[:, 0:1024], in0=PS[:, pb:pb + 1024],
                        scalar1=L1BI[:, 16:17], scalar2=None, op0=ALU.add),
                        wait=[(TE, te_mm)])
                    dma("gpsimd", ccB_in_v[0, 0], RT[:, 0:1024],
                        wait=[(DV, d_e)], ch="blob")
                    evac_tok[o4 % 2] = (DV, d_e)
                else:
                    d_e = op("vector", lambda e, pb=pb: e.tensor_scalar(
                        out=RT[:, 1024:2048], in0=PS[:, pb:pb + 1024],
                        scalar1=L1BI[:, 17:18], scalar2=None, op0=ALU.add),
                        wait=[(TE, te_mm)])
                    evac_tok[o4 % 2] = (DV, d_e)
                    dv_prev = d_e
                    for b in range(8):
                        te_t = op("tensor", lambda e, b=b: e.transpose(
                            PS[:, 2048 + (b % 2) * 512:2048 + (b % 2) * 512 + 64].bitcast(BF16),
                            RT[:, 1024 + b * 128:1024 + b * 128 + 128],
                            IDENT[:]), wait=[(DV, dv_prev)])
                        dv_prev = op("vector", lambda e, b=b: e.tensor_copy(
                            VG[:, b * 128:b * 128 + 128],
                            PS[:, 2048 + (b % 2) * 512:2048 + (b % 2) * 512 + 64].bitcast(BF16)),
                            wait=[(TE, te_t)])
                    dma("gpsimd", ccB_in_v[0, 1], VG[:, 0:1024],
                        wait=[(DV, dv_prev)], ch="blob")
            te_mm_end[s % 2] = TE.v
            if s < 4:
                prep_tok = cast_prep(l1wT, (s + 1) * 512, ((s + 1) % 2) * 16,
                                     ncols=(512 if s < 3 else 256),
                                     wbf_free_te=te_mm_end[(s + 1) % 2])
        barrier()

        ccc = chan("cc")
        blobc = chan("blob")
        cc_tok = (ccc, ccc.bump(1))
        blob_full = (blobc, blobc.v)

        def fcc_emit(e, bf=blob_full):
            e.wait_ge(bf[0].sem, bf[1])
            inst = e.collective_compute(
                "AllGather", ALU.bypass,
                replica_groups=[[0, 1], [2, 3], [4, 5], [6, 7]],
                ins=[ccB_in], outs=[ccB_out])
            inst.then_inc(ccc.sem, 1)
        streams["gpsimd"].append(fcc_emit)
        ccB_out_v = ccB_out.rearrange("a (r k p c) -> a r k p c",
                                      r=2, p=128, c=1024)
        for par in range(2):
            dma("gpsimd",
                KG[:, 0:2048].rearrange("p (b two c) -> p b two c",
                                        two=2, c=128)[:, :, par],
                ccB_out_v[0, par, 0].rearrange("p (b c) -> p b c", c=128),
                wait=([cc_tok] if par == 0 else []), ch="gath")
            dma("gpsimd",
                VG[:, 0:2048].rearrange("p (b two c) -> p b two c",
                                        two=2, c=128)[:, :, par],
                ccB_out_v[0, par, 1].rearrange("p (b c) -> p b c", c=128),
                ch="gath")
        barrier()

        attention(lambda h: 0, QB, XN)
        barrier()

        prep_tok = cast_prep(l1owT, 0, 0)
        te_mm_end = [0, 0]
        evac_tok = [None, None]
        for s in range(4):
            slot0 = (s % 2) * 16
            for o4 in range(4):
                oc = s * 4 + o4
                pb = (o4 % 2) * 1024
                for hf in range(2):
                    for g in range(CH):
                        wl = []
                        if g == 0 and hf == 0:
                            if o4 == 0:
                                wl = [(SC, prep_tok)]
                            if evac_tok[o4 % 2]:
                                wl = wl + [evac_tok[o4 % 2]]
                        op("tensor", lambda e, slot0=slot0, g=g, o4=o4, hf=hf, pb=pb: e.matmul(
                            PS[:, pb + hf * 512:pb + hf * 512 + 512],
                            WBF[:, (slot0 + g) * 512 + o4 * 128:
                                (slot0 + g) * 512 + o4 * 128 + 128],
                            XN[:, g * T + hf * 512:g * T + hf * 512 + 512],
                            start=(g == 0), stop=(g == CH - 1)), wait=wl)
                te_mm = TE.v
                d_e = op("vector", lambda e, oc=oc, pb=pb: e.tensor_scalar(
                    out=QB[:, oc * T:(oc + 1) * T], in0=PS[:, pb:pb + 1024],
                    scalar1=L1OB[:, oc:oc + 1], scalar2=None, op0=ALU.add),
                    wait=[(TE, te_mm)])
                evac_tok[o4 % 2] = (DV, d_e)
            te_mm_end[s % 2] = TE.v
            if s < 3:
                prep_tok = cast_prep(l1owT, (s + 1) * 512, ((s + 1) % 2) * 16,
                                     wbf_free_te=te_mm_end[(s + 1) % 2])
        barrier()

        te_st2 = ln_stats_sbuf(lambda c: (BIG, 16384 + c * T))
        te_bc2 = ln_smalls(te_st2)
        ln2_residual(lambda c: (BIG, 16384 + c * T), src_hbm, dst_hbm, te_bc2)
        barrier()

    def block_l1_ffn(src_hbm, dst_hbm):
        def phase_done(n):
            return PHASE_LIMIT <= n
        def wmax_pass(wsrc, nrows, ncols, dst_scl):
            first = True
            d = DV.v
            for g in range(nrows):
                for c0 in range(0, ncols, 1024):
                    t_w = dma("sync", WRAW[:], wsrc[g * 128:(g + 1) * 128,
                                                    c0:c0 + 1024],
                              wait=[(DV, d)], ch="wr0")

                    def f(e, first=first):
                        e.tensor_reduce(out=SCL[:, 12:13], in_=WRAW[:],
                                        axis=mybir.AxisListType.X, op=ALU.max,
                                        apply_absolute_value=True)
                        e.drain()
                        if first:
                            return e.tensor_copy(SCL[:, dst_scl:dst_scl + 1],
                                                 SCL[:, 12:13])
                        return e.tensor_tensor(SCL[:, dst_scl:dst_scl + 1],
                                               SCL[:, dst_scl:dst_scl + 1],
                                               SCL[:, 12:13], op=ALU.max)
                    d = op("vector", f, wait=[t_w])
                    first = False
            return d
        d1 = wmax_pass(wfcT, 16, 8192, 10)
        dwf = rowmax_to_scalar(10, 0, d1)
        d2 = wmax_pass(wprT, 64, 2048, 11)
        dwp = rowmax_to_scalar(11, 1, d2)

        def fws(e):
            e.tensor_scalar(out=SC2[0:1, 2:4], in0=SC2[0:1, 0:2],
                            scalar1=1.0 / 448.0, scalar2=1e-6,
                            op0=ALU.mult, op1=ALU.max)
            e.drain()
            e.reciprocal(SC2[0:1, 6:8], SC2[0:1, 2:4])
            e.drain()
            return e.tensor_scalar(out=SC2[0:1, 4:6], in0=SC2[0:1, 6:8],
                                   scalar1=0.5, scalar2=None, op0=ALU.mult)
        d_ws = op("vector", fws, wait=[(DV, dwp)])
        barrier()
        if phase_done(3.1):
            return

        for c in range(CH):
            t_x = dma("sync", XCH[:], xview(src_hbm, c),
                      wait=[(TE, TE.v if c > 0 else 0)], ch="xch")
            sc_tok = op("scalar", lambda e: e.copy(SW[:], XCH[:]),
                        wait=[t_x])
            dv_tok = op("vector", lambda e, c=c: e.tensor_mul(
                SQ[:, (c % 2) * 1024:(c % 2) * 1024 + 1024], SW[:], SW[:]),
                wait=[(SC, sc_tok)])
            for hf in range(2):
                lo = 3072 + hf * 512
                op("tensor", lambda e, c=c, hf=hf, lo=lo: e.matmul(
                    PS[0:1, lo:lo + 512], AONE[:, 32:33],
                    SW[:, hf * 512:hf * 512 + 512],
                    start=(c == 0), stop=(c == CH - 1)), wait=[(SC, sc_tok)])
                op("tensor", lambda e, c=c, hf=hf, lo=lo: e.matmul(
                    PS[32:33, lo:lo + 512], AONE[:, 32:33],
                    SQ[:, (c % 2) * 1024 + hf * 512:(c % 2) * 1024 + hf * 512 + 512],
                    start=(c == 0), stop=(c == CH - 1)), wait=[(DV, dv_tok)])
        te_bc = ln_smalls(TE.v, fp32_bcast=True)
        barrier()
        if phase_done(3.2):
            return

        first = True
        d_mx = 0
        sc_gate = SC.v
        for c in range(CH):
            t_x = dma("sync", XCH[:], xview(src_hbm, c),
                      wait=[(DV, d_mx)] + ([(SC, sc_gate)] if c == 0 else []),
                      ch="xch")

            def f(e, first=first):
                e.tensor_mul(XC2[:], XCH[:], PS[:, 0:1024])
                e.drain()
                e.tensor_add(XC2[:], XC2[:], PS[:, 1024:2048])
                e.drain()
                e.tensor_reduce(out=SCL[:, 10:11], in_=XC2[:],
                                axis=mybir.AxisListType.X, op=ALU.max,
                                apply_absolute_value=True)
                e.drain()
                if first:
                    return e.tensor_copy(SCL[:, 13:14], SCL[:, 10:11])
                return e.tensor_tensor(SCL[:, 13:14], SCL[:, 13:14],
                                       SCL[:, 10:11], op=ALU.max)
            d_mx = op("vector", f, wait=[t_x, (TE, te_bc)])
            first = False
        d_xm = rowmax_to_scalar(13, 8, d_mx)
        g2 = dma("gpsimd", ccm_in, SC2[0:1, 8:9], wait=[(DV, d_xm)], ch="msc")
        ccc = chan("cc")
        cc1 = (ccc, ccc.bump(1))

        def fcc1(e, g2=g2):
            e.wait_ge(g2[0].sem, g2[1])
            inst = e.collective_compute(
                "AllReduce", ALU.max,
                replica_groups=[[0, 1, 2, 3, 4, 5, 6, 7]],
                ins=[ccm_in], outs=[ccm_out])
            inst.then_inc(ccc.sem, 1)
        streams["gpsimd"].append(fcc1)
        g3 = dma("gpsimd", SC2[0:1, 9:10], ccm_out, wait=[cc1], ch="msc")

        def fxs(e):
            e.tensor_scalar(out=SC2[0:1, 10:11], in0=SC2[0:1, 9:10],
                            scalar1=1.0 / 448.0, scalar2=1e-6,
                            op0=ALU.mult, op1=ALU.max)
            e.drain()
            e.reciprocal(SC2[0:1, 13:14], SC2[0:1, 10:11])
            e.drain()
            e.tensor_scalar(out=SC2[0:1, 11:12], in0=SC2[0:1, 13:14],
                            scalar1=0.5, scalar2=None, op0=ALU.mult)
            e.drain()
            e.tensor_mul(SC2[0:1, 12:13], SC2[0:1, 10:11], SC2[0:1, 2:3])
            e.drain()
            return e.tensor_scalar(out=SC2[0:1, 12:13], in0=SC2[0:1, 12:13],
                                   scalar1=4.0, scalar2=None, op0=ALU.mult)
        d_xs = op("vector", fxs, wait=[g3])
        d_sc = bcast_scalars_f32([11, 4, 5, 12], 4, d_xs)
        barrier()
        if phase_done(3.3):
            return

        d_q = DV.v
        for c in range(CH):
            t_x = dma("sync", XCH[:], xview(src_hbm, c), wait=[(DV, d_q)],
                      ch="xch")

            def f(e, c=c):
                e.tensor_mul(XC2[:], XCH[:], PS[:, 0:1024])
                e.drain()
                e.tensor_add(XC2[:], XC2[:], PS[:, 1024:2048])
                e.drain()
                return e.tensor_scalar(out=XQ8[:, c * T:(c + 1) * T],
                                       in0=XC2[:], scalar1=SCL[:, 4:5],
                                       scalar2=None, op0=ALU.mult)
            d_q = op("vector", f, wait=[t_x, (DV, d_sc)])
        barrier()
        if phase_done(3.4):
            return

        first_h = True
        te_mm_end = [0, 0]
        h_tok = [None, None]
        hst_last = [None]
        for os_i in range(16):
            prep_toks = []
            w8_tok = [DV.v, DV.v]
            for g in range(16):
                t_w = dma("sync", WRAW[:, (g % 2) * 512:(g % 2) * 512 + 512],
                          wfcT[g * 128:(g + 1) * 128,
                               os_i * 512:os_i * 512 + 512],
                          wait=[(DV, w8_tok[g % 2])], ch="wr%d" % (g % 2))
                tok = op("vector", lambda e, g=g, os_i=os_i: e.tensor_scalar(
                    out=W8[:, ((os_i % 2) * 16 + g) * 512:
                           ((os_i % 2) * 16 + g) * 512 + 512],
                    in0=WRAW[:, (g % 2) * 512:(g % 2) * 512 + 512],
                    scalar1=SCL[:, 5:6], scalar2=None, op0=ALU.mult),
                    wait=[t_w] +
                         ([(TE, te_mm_end[os_i % 2])] if g == 0 else []))
                w8_tok[g % 2] = tok
                prep_toks.append(tok)
            prep_done = prep_toks[-1]
            for o4 in range(4):
                oc = os_i * 4 + o4
                pb = (o4 % 2) * 1024
                for hf in range(2):
                    for g in range(CH):
                        wl = []
                        if g == 0 and hf == 0:
                            if o4 == 0:
                                wl = [(DV, prep_done)]
                            if h_tok[o4 % 2]:
                                wl = wl + [h_tok[o4 % 2]]
                        op("tensor", lambda e, g=g, o4=o4, hf=hf, pb=pb, os_i=os_i: e.matmul(
                            PS[:, pb + hf * 512:pb + hf * 512 + 512],
                            W8[:, ((os_i % 2) * 16 + g) * 512 + o4 * 128:
                               ((os_i % 2) * 16 + g) * 512 + o4 * 128 + 128],
                            XQ8[:, g * T + hf * 512:g * T + hf * 512 + 512],
                            start=(g == 0), stop=(g == CH - 1)), wait=wl)
                te_mm = TE.v
                s_r = op("scalar", lambda e, pb=pb: e.activation(
                    out=XC2[:], in_=PS[:, pb:pb + 1024], func=AF.Relu,
                    bias=0.0, scale=SCL[:, 7:8]),
                    wait=[(TE, te_mm), (DV, DV.v)])

                def fh(e, pb=pb, first_h=first_h):
                    e.scalar_tensor_tensor(out=XCH[:], in0=PS[:, pb:pb + 1024],
                                           scalar=SCL[:, 7:8], in1=XC2[:],
                                           op0=ALU.mult, op1=ALU.mult)
                    e.drain()
                    e.tensor_reduce(out=SCL[:, 10:11], in_=XCH[:],
                                    axis=mybir.AxisListType.X, op=ALU.max)
                    e.drain()
                    if first_h:
                        return e.tensor_copy(SCL[:, 14:15], SCL[:, 10:11])
                    return e.tensor_tensor(SCL[:, 14:15], SCL[:, 14:15],
                                           SCL[:, 10:11], op=ALU.max)
                d_h = op("vector", fh, wait=[(SC, s_r)] +
                         ([hst_last[0]] if hst_last[0] else []))
                first_h = False
                h_tok[o4 % 2] = (DV, d_h)
                hst_last[0] = dma("gpsimd",
                                  h_dram.rearrange("(ch p) t -> ch p t", p=128)[oc],
                                  XCH[:], wait=[(DV, d_h)], ch="hst")
            te_mm_end[os_i % 2] = TE.v
        barrier()
        if phase_done(3.5):
            return

        d_hm = rowmax_to_scalar(14, 16, DV.v)
        g2 = dma("gpsimd", cch_in, SC2[0:1, 16:17], wait=[(DV, d_hm)], ch="msc")
        ccc = chan("cc")
        cc2 = (ccc, ccc.bump(1))

        def fcc2(e, g2=g2):
            e.wait_ge(g2[0].sem, g2[1])
            inst = e.collective_compute(
                "AllReduce", ALU.max,
                replica_groups=[[0, 1, 2, 3, 4, 5, 6, 7]],
                ins=[cch_in], outs=[cch_out])
            inst.then_inc(ccc.sem, 1)
        streams["gpsimd"].append(fcc2)
        g3 = dma("gpsimd", SC2[0:1, 17:18], cch_out, wait=[cc2], ch="msc")

        def fhs(e):
            e.tensor_scalar(out=SC2[0:1, 18:19], in0=SC2[0:1, 17:18],
                            scalar1=1.0 / 448.0, scalar2=1e-6,
                            op0=ALU.mult, op1=ALU.max)
            e.drain()
            e.reciprocal(SC2[0:1, 21:22], SC2[0:1, 18:19])
            e.drain()
            e.tensor_scalar(out=SC2[0:1, 19:20], in0=SC2[0:1, 21:22],
                            scalar1=0.5, scalar2=None, op0=ALU.mult)
            e.drain()
            e.tensor_mul(SC2[0:1, 20:21], SC2[0:1, 18:19], SC2[0:1, 3:4])
            e.drain()
            return e.tensor_scalar(out=SC2[0:1, 20:21], in0=SC2[0:1, 20:21],
                                   scalar1=4.0, scalar2=None, op0=ALU.mult)
        d_hs = op("vector", fhs, wait=[g3])
        d_sc2 = bcast_scalars_f32([19, 20], 8, d_hs)
        barrier()
        if phase_done(3.6):
            return

        d_q = DV.v
        for c in range(64):
            t_h = dma("sync", XCH[:],
                      h_dram.rearrange("(ch p) t -> ch p t", p=128)[c],
                      wait=[(DV, d_q)], ch="xch")
            d_q = op("vector", lambda e, c=c: e.tensor_scalar(
                out=H8[:, c * T:(c + 1) * T], in0=XCH[:],
                scalar1=SCL[:, 8:9], scalar2=None, op0=ALU.mult),
                wait=[t_h, (DV, d_sc2)])
        barrier()
        if phase_done(3.7):
            return

        te_mm_last = 0
        ev_last = [None, None]
        ust_last = [None]
        for os_i in range(8):
            prep_toks = []
            w8_tok = [DV.v] * 4
            for g in range(64):
                slot = g % 4
                t_w = dma("sync", WRAW[:, slot * 256:slot * 256 + 256],
                          wprT[g * 128:(g + 1) * 128,
                               os_i * 256:os_i * 256 + 256],
                          wait=[(DV, w8_tok[slot])], ch="wr%d" % slot)
                tok = op("vector", lambda e, g=g, slot=slot: e.tensor_scalar(
                    out=W8[:, g * 256:g * 256 + 256],
                    in0=WRAW[:, slot * 256:slot * 256 + 256],
                    scalar1=SCL[:, 6:7], scalar2=None, op0=ALU.mult),
                    wait=[t_w] +
                         ([(TE, te_mm_last)] if g == 0 else []))
                w8_tok[slot] = tok
                prep_toks.append(tok)
            prep_done = prep_toks[-1]
            for oc2 in range(2):
                oc = os_i * 2 + oc2
                pb = oc2 * 1024
                for hf in range(2):
                    for g in range(64):
                        wl = []
                        if g == 0 and hf == 0:
                            wl = [(DV, prep_done)]
                            if ev_last[oc2]:
                                wl = wl + [ev_last[oc2]]
                        op("tensor", lambda e, g=g, oc2=oc2, hf=hf, pb=pb: e.matmul(
                            PS[:, pb + hf * 512:pb + hf * 512 + 512],
                            W8[:, g * 256 + oc2 * 128:g * 256 + oc2 * 128 + 128],
                            H8[:, g * T + hf * 512:g * T + hf * 512 + 512],
                            start=(g == 0), stop=(g == 63)), wait=wl)
                te_mm = TE.v
                d_ev = op("vector", lambda e, pb=pb: e.tensor_scalar(
                    out=PT[:], in0=PS[:, pb:pb + 1024],
                    scalar1=SCL[:, 9:10], scalar2=None, op0=ALU.mult),
                    wait=[(TE, te_mm)] + ([ust_last[0]] if ust_last[0] else []))
                ust_last[0] = dma("gpsimd", xview(u_dram, oc), PT[:],
                                  wait=[(DV, d_ev)], ch="ust")
                ev_last[oc2] = (DV, d_ev)
            te_mm_last = TE.v
        barrier()
        if phase_done(3.8):
            return

        te_st2 = ln_stats_udram(u_dram)
        te_bc2 = ln_smalls(te_st2)
        ln2_residual(None, src_hbm, dst_hbm, te_bc2, u_hbm=u_dram)
        barrier()

    blocks = [block_l0_attn, block_l0_ffn, block_l1_attn, block_l1_ffn]
    srcs = [x0, xs, xs, xs]
    dsts = [xs, xs, xs, out_t]
    dsts[N_BLOCKS - 1] = out_t
    for i in range(N_BLOCKS):
        blocks[i](srcs[i], dsts[i])

    final_toks = [(c.sem, c.v) for c in channels.values() if c.v > 0]

    def _final(e, wl=final_toks):
        for (s, v) in wl:
            e.wait_ge(s, v)
    streams["gpsimd"].append(_final)

    with es:
        with nc.Block() as block:
            @block.sync
            def _(e):
                for f in streams["sync"]:
                    f(e)

            @block.tensor
            def _(e):
                for f in streams["tensor"]:
                    f(e)

            @block.vector
            def _(e):
                for f in streams["vector"]:
                    f(e)

            @block.scalar
            def _(e):
                for f in streams["scalar"]:
                    f(e)

            @block.gpsimd
            def _(e):
                for f in streams["gpsimd"]:
                    f(e)
    return nc


def _host_layout(inputs):
    x = np.ascontiguousarray(inputs["L0_attn_x"])
    wqkvT = np.ascontiguousarray(inputs["L0_attn_w_qkv"].T)
    wprojT = np.ascontiguousarray(inputs["L0_attn_w_proj"].T)
    qgc = np.ascontiguousarray(
        (inputs["L0_attn_q_gain"].astype(np.float32) *
         np.float32(INV_SQRT_HD)).reshape(16, 1))

    def u8T(a):
        v = np.ascontiguousarray(a).view(np.uint8).reshape(
            a.shape[0], a.shape[1], 4)[:, :, 0]
        return np.ascontiguousarray(v.T)
    wqgT = u8T(inputs["L0_ffn_wq_gate"])
    wquT = u8T(inputs["L0_ffn_wq_up"])
    wqdT = u8T(inputs["L0_ffn_wq_down"])
    sg_c = np.ascontiguousarray(inputs["L0_ffn_s_gate"].reshape(44, 128).T)
    su_c = np.ascontiguousarray(inputs["L0_ffn_s_up"].reshape(44, 128).T)
    sd_c = np.ascontiguousarray(inputs["L0_ffn_s_down"].reshape(16, 128).T)
    l1wT = np.ascontiguousarray(inputs["L1_attn_Wqkv_w"].T)
    l1bias = np.ascontiguousarray(inputs["L1_attn_Wqkv_b"].reshape(18, 128).T)
    l1owT = np.ascontiguousarray(inputs["L1_attn_out_w"].T)
    l1ob = np.ascontiguousarray(inputs["L1_attn_out_b"].reshape(16, 128).T)
    wfcT = np.ascontiguousarray(inputs["L1_ffn_w_fc"].T)
    wprT = np.ascontiguousarray(inputs["L1_ffn_w_proj"].T)

    ident = np.eye(128, dtype=np.float32).astype(bf)
    aone = np.zeros((128, 64), dtype=np.float32)
    aone[:, 32] = 1.0
    aone = aone.astype(bf)
    onesrow = np.zeros((65, 128), dtype=np.float32)
    onesrow[[0, 32, 64], :] = 1.0
    onesrow = onesrow.astype(bf)
    negrow = np.zeros((65, 128), dtype=np.float32)
    negrow[[0, 32, 64], :] = -1.0
    negrow = negrow.astype(bf)
    ceps = np.zeros((128, 2), dtype=np.float32)
    ceps[:, 0] = LN_EPS
    ceps[:, 1] = RMS_EPS

    inv = 1.0 / (10000.0 ** (np.arange(0, 128, 2, dtype=np.float64) / 128.0))
    in_maps, pos_list = [], []
    for core in range(8):
        b, p = core // 2, core % 2
        pos = np.concatenate([np.arange(128) + (2 * j + p) * 128
                              for j in range(8)])
        pos_list.append(pos)
        xc = np.ascontiguousarray(x[b][pos].T)
        fr = np.outer(pos.astype(np.float64), inv)
        c64 = np.cos(fr).T.astype(np.float32)
        s64 = np.sin(fr).T.astype(np.float32)
        cosx = np.ascontiguousarray(
            np.concatenate([c64, c64], axis=0)).astype(bf)
        sinx = np.ascontiguousarray(
            np.concatenate([s64, -s64], axis=0)).astype(bf)
        lens = np.zeros((128, 8), dtype=np.float32)
        for j in range(8):
            lens[:, j] = (2 * j + p) * 128 + np.arange(128) + 1
        in_maps.append(dict(
            x0=xc, wqkvT=wqkvT, wprojT=wprojT, qgc=qgc,
            wqgT=wqgT, wquT=wquT, wqdT=wqdT,
            sg_c=sg_c, su_c=su_c, sd_c=sd_c,
            l1wT=l1wT, l1bias=l1bias, l1owT=l1owT, l1ob=l1ob,
            wfcT=wfcT, wprT=wprT,
            cosx=cosx, sinx=sinx, lens_in=lens,
            ident_in=ident, aone_in=aone, onesrow_in=onesrow,
            negrow_in=negrow, ceps_in=ceps,
        ))
    return in_maps, pos_list


def kernel(**inputs):
    in_maps, pos_list = _host_layout(inputs)
    nc = build_program()
    lower_extended_insts(nc)
    res = run_bass_kernel_spmd(nc, in_maps, core_ids=list(range(8)))
    out = np.empty((4, 2048, 2048), dtype=np.float32)
    for core in range(8):
        b = core // 2
        out[b, pos_list[core], :] = np.asarray(res.results[core]["out"]).T
    return out


# version 04
# speedup vs baseline: 1.1132x; 1.1132x over previous
import numpy as np
import ml_dtypes
import concourse.bass as bass
import concourse.mybir as mybir
from concourse.bass_utils import run_bass_kernel_spmd
from concourse.library_overlay import lower_extended_insts
from contextlib import ExitStack

F32 = mybir.dt.float32
BF16 = mybir.dt.bfloat16
U8 = mybir.dt.uint8
FP8 = mybir.dt.float8e4
AF = mybir.ActivationFunctionType
ALU = mybir.AluOpType
bf = ml_dtypes.bfloat16

T = 1024
CH = 16
D = 2048
RMS_EPS = 1.1920929e-07
LN_EPS = 1e-5
CROUND = 12582912.0
POSBIG = 3.4028235e38
INV_SQRT_HD = 0.08838834764831845

N_BLOCKS = 4
PHASE_LIMIT = 99
ENGS = ("sync", "tensor", "vector", "scalar", "gpsimd")


class Ctr:
    def __init__(self, nc, es, name):
        self.sem = es.enter_context(nc.semaphore(name))
        self.v = 0

    def bump(self, n=1):
        self.v += n
        return self.v


def _bank_splits(lo, hi):
    outl = []
    c = lo
    while c < hi:
        nxt = min((c // 512 + 1) * 512, hi)
        outl.append((c, nxt))
        c = nxt
    return outl


def build_program():
    nc = bass.Bass("TRN2", target_bir_lowering=False)
    es = ExitStack()
    streams = {e: [] for e in ENGS}

    dram = lambda n, s, d, k="ExternalInput": nc.dram_tensor(n, s, d, kind=k).ap()
    x0 = dram("x0", [D, T], F32)
    wqkvT = dram("wqkvT", [D, 3072], F32)
    wprojT = dram("wprojT", [D, D], F32)
    qgc = dram("qgc", [16, 1], F32)
    wqgT = dram("wqgT", [D, 5632], U8)
    wquT = dram("wquT", [D, 5632], U8)
    wqdT = dram("wqdT", [5632, D], U8)
    sg_c = dram("sg_c", [128, 44], F32)
    su_c = dram("su_c", [128, 44], F32)
    sd_c = dram("sd_c", [128, 16], F32)
    l1wT = dram("l1wT", [D, 2304], F32)
    l1bias = dram("l1bias", [128, 18], F32)
    l1owT = dram("l1owT", [D, D], F32)
    l1ob = dram("l1ob", [128, 16], F32)
    wfcT = dram("wfcT", [D, 8192], F32)
    wprT = dram("wprT", [8192, D], F32)
    cosx = dram("cosx", [128, T], BF16)
    sinx = dram("sinx", [128, T], BF16)
    lens_in = dram("lens_in", [128, 8], F32)
    ident_in = dram("ident_in", [128, 128], BF16)
    aone_in = dram("aone_in", [128, 64], BF16)
    onesrow_in = dram("onesrow_in", [65, 128], BF16)
    negrow_in = dram("negrow_in", [65, 128], BF16)
    ceps_in = dram("ceps_in", [128, 2], F32)

    out_t = dram("out", [D, T], F32, "ExternalOutput")
    xs = dram("xs", [D, T], F32, "Internal")
    u_dram = dram("u_dram", [D, T], BF16, "Internal")
    h_dram = dram("h_dram", [8192, T], F32, "Internal")
    scr128 = dram("scr128", [128, 1], F32, "Internal")
    ccA_in = dram("ccA_in", [1, 1024 * 1024], BF16, "Internal")
    ccA_out = dram("ccA_out", [1, 2 * 1024 * 1024], BF16, "Internal")
    ccB_in = dram("ccB_in", [1, 2 * 128 * 1024], BF16, "Internal")
    ccB_out = dram("ccB_out", [1, 4 * 128 * 1024], BF16, "Internal")
    ccm_in = dram("ccm_in", [1, 1], F32, "Internal")
    ccm_out = dram("ccm_out", [1, 1], F32, "Internal")
    cch_in = dram("cch_in", [1, 1], F32, "Internal")
    cch_out = dram("cch_out", [1, 1], F32, "Internal")

    def xview(t, c):
        return t.rearrange("(ch p) t -> ch p t", p=128)[c]

    sbuf = lambda n, s, d: es.enter_context(nc.sbuf_tensor(n, s, d))
    IDENT = sbuf("IDENT", [128, 128], BF16)
    AONE = sbuf("AONE", [128, 64], BF16)
    ONESROW = sbuf("ONESROW", [65, 128], BF16)
    NEGROW = sbuf("NEGROW", [65, 128], BF16)
    CEPS = sbuf("CEPS", [128, 2], F32)
    COSX = sbuf("COSX", [128, T], BF16)
    SINX = sbuf("SINX", [128, T], BF16)
    LENS = sbuf("LENS", [128, 8], F32)
    QGC = sbuf("QGC", [16, 1], F32)
    SGC = sbuf("SGC", [128, 44], F32)
    SUC = sbuf("SUC", [128, 44], F32)
    SDC = sbuf("SDC", [128, 16], F32)
    L1BI = sbuf("L1BI", [128, 18], F32)
    L1OB = sbuf("L1OB", [128, 16], F32)
    WRAW = sbuf("WRAW", [128, 1024], F32)
    WU8 = sbuf("WU8", [128, 2048], U8)
    WBF = sbuf("WBF", [128, 16384], BF16)
    SQ = sbuf("SQ", [128, 2048], BF16)
    XCH = sbuf("XCH", [128, 1024], F32)
    XC2 = sbuf("XC2", [128, 1024], F32)
    ROWS = sbuf("ROWS", [128, 3072], F32)
    BROWS = sbuf("BROWS", [128, 2048], BF16)
    SC2 = sbuf("SC2", [128, 32], F32)
    RMAX = sbuf("RMAX", [1, 256], F32)
    TERN = sbuf("TERN", [16, 512], F32)
    TERNB = sbuf("TERNB", [16, 1536], BF16)
    VW = sbuf("VW", [128, 1024], BF16)
    RT = sbuf("RT", [128, 2048], BF16)
    SMF = sbuf("SMF", [128, 2048], F32)
    XSLOT2 = None
    PT = sbuf("PT", [128, 1024], BF16)
    SW = sbuf("SW", [128, 1024], BF16)
    SCL = sbuf("SCL", [128, 16], F32)
    BIG = sbuf("BIG", [128, 57344], BF16)

    W8 = WBF[:, 0:8192].bitcast(FP8)
    XN = BIG[:, 0:16384]
    QB = BIG[:, 16384:32768]
    KG = BIG[:, 32768:40960]
    VG = BIG[:, 40960:49152]
    PPB = BIG[:, 49152:57344]
    XQ8 = BIG[:, 0:8192].bitcast(FP8)
    H8 = BIG[:, 8192:40960].bitcast(FP8)

    XSLOT = [XCH, None]

    def tcol(hc):
        if hc < 40:
            return BIG, 16384 + hc * 1024
        if hc < 42:
            return SQ, (hc - 40) * 1024
        return RT, (hc - 42) * 1024

    PS = es.enter_context(nc.psum_tensor("PS", [128, 4096], F32))

    def xsl(c):
        return XCH[:] if c % 2 == 0 else SMF[:, 1024:2048]

    def usl(c):
        return SW[:] if c % 2 == 0 else RT[:, 0:1024]

    SY = Ctr(nc, es, "SY")
    GD = Ctr(nc, es, "GD")
    TE = Ctr(nc, es, "TE")
    DV = Ctr(nc, es, "DV")
    SC = Ctr(nc, es, "SC")
    CTRS = {"sync": SY, "gpsimd": GD, "tensor": TE, "vector": DV, "scalar": SC}

    channels = {}

    def chan(name):
        if name not in channels:
            channels[name] = Ctr(nc, es, "ch_" + name)
        return channels[name]

    def dma(queue, dst, src, wait=(), ch="g"):
        ctr = chan(ch)
        tok = ctr.bump(16)
        wl = [(c.sem, v) for (c, v) in wait if v > 0]

        def f(e, dst=dst, src=src, wl=wl):
            for (s, v) in wl:
                e.wait_ge(s, v)
            e.dma_start(out=dst, in_=src).then_inc(ctr.sem, 16)
        streams[queue].append(f)
        return (ctr, tok)

    def op(eng, fn, wait=(), inc=1):
        ctr = CTRS[eng]
        tok = ctr.bump(inc) if inc else ctr.v
        wl = [(c.sem, v) for (c, v) in wait if v > 0]

        def f(e, fn=fn, wl=wl, inc=inc):
            for (s, v) in wl:
                e.wait_ge(s, v)
            inst = fn(e)
            if inc:
                inst.then_inc(ctr.sem, inc)
        streams[eng].append(f)
        return tok

    def barrier():
        allc = list(CTRS.items()) + [("ch_" + k, c) for k, c in channels.items()]
        toks = [(c.sem, c.v) for _, c in allc if c.v > 0]
        for eng in ENGS:
            wl = [(s, v) for (s, v) in toks if s is not CTRS[eng].sem]

            def f(e, wl=wl):
                for (s, v) in wl:
                    e.wait_ge(s, v)
            streams[eng].append(f)

    dma("sync", IDENT[:], ident_in, ch="con")
    dma("sync", AONE[:], aone_in, ch="con")
    dma("sync", ONESROW[:], onesrow_in, ch="con")
    dma("sync", NEGROW[:], negrow_in, ch="con")
    dma("sync", CEPS[:], ceps_in, ch="con")
    dma("sync", COSX[:], cosx, ch="con")
    dma("sync", SINX[:], sinx, ch="con")
    dma("sync", LENS[:], lens_in, ch="con")
    dma("sync", QGC[:], qgc, ch="con")
    dma("sync", SGC[:], sg_c, ch="con")
    dma("sync", SUC[:], su_c, ch="con")
    dma("sync", SDC[:], sd_c, ch="con")
    dma("sync", L1BI[:], l1bias, ch="con")
    dma("sync", L1OB[:], l1ob, ch="con")
    barrier()

    def ln_stats_hbm(src_hbm, xb_dst):
        sq_te = [0, 0]
        sc_slot = [SC.v, SC.v]
        for c in range(CH):
            t_x = dma("sync", xsl(c), xview(src_hbm, c),
                      wait=[(SC, sc_slot[c % 2])], ch="xch%d" % (c % 2))
            sc_tok = op("scalar", lambda e, c=c: e.copy(
                xb_dst[:, c * T:(c + 1) * T], xsl(c)), wait=[t_x])
            sc_slot[c % 2] = sc_tok
            dv_tok = op("vector", lambda e, c=c: e.tensor_mul(
                SQ[:, (c % 2) * 1024:(c % 2) * 1024 + 1024],
                xb_dst[:, c * T:(c + 1) * T], xb_dst[:, c * T:(c + 1) * T]),
                wait=[(SC, sc_tok), (TE, sq_te[c % 2])])
            last_dv = dv_tok
            for hf in range(2):
                lo = 3072 + hf * 512
                op("tensor", lambda e, c=c, hf=hf, lo=lo: e.matmul(
                    PS[0:1, lo:lo + 512], AONE[:, 32:33],
                    xb_dst[:, c * T + hf * 512:c * T + hf * 512 + 512],
                    start=(c == 0), stop=(c == CH - 1)), wait=[(SC, sc_tok)])
                op("tensor", lambda e, c=c, hf=hf, lo=lo: e.matmul(
                    PS[32:33, lo:lo + 512], AONE[:, 32:33],
                    SQ[:, (c % 2) * 1024 + hf * 512:(c % 2) * 1024 + hf * 512 + 512],
                    start=(c == 0), stop=(c == CH - 1)), wait=[(DV, dv_tok)])
            sq_te[c % 2] = TE.v
        return TE.v

    def ln_stats_sbuf(cols):
        sq_te = [0, 0]
        for c in range(CH):
            tns, off = cols(c)
            dv_tok = op("vector", lambda e, tns=tns, off=off, c=c: e.tensor_mul(
                SQ[:, (c % 2) * 1024:(c % 2) * 1024 + 1024],
                tns[:, off:off + T], tns[:, off:off + T]),
                wait=[(TE, sq_te[c % 2])])
            for hf in range(2):
                lo = 3072 + hf * 512
                op("tensor", lambda e, tns=tns, off=off, c=c, hf=hf, lo=lo: e.matmul(
                    PS[0:1, lo:lo + 512], AONE[:, 32:33],
                    tns[:, off + hf * 512:off + hf * 512 + 512],
                    start=(c == 0), stop=(c == CH - 1)))
                op("tensor", lambda e, c=c, hf=hf, lo=lo: e.matmul(
                    PS[32:33, lo:lo + 512], AONE[:, 32:33],
                    SQ[:, (c % 2) * 1024 + hf * 512:(c % 2) * 1024 + hf * 512 + 512],
                    start=(c == 0), stop=(c == CH - 1)), wait=[(DV, dv_tok)])
            sq_te[c % 2] = TE.v
        return TE.v

    def ln_stats_udram(src_hbm_u):
        te_slot = [TE.v, TE.v]
        for c in range(CH):
            t_u = dma("sync", usl(c), xview(src_hbm_u, c),
                      wait=[(TE, te_slot[c % 2])], ch="sw%d" % (c % 2))
            dv_sq = op("vector", lambda e, c=c: e.tensor_mul(
                SQ[:, (c % 2) * 1024:(c % 2) * 1024 + 1024], usl(c), usl(c)),
                wait=[t_u])
            for hf in range(2):
                lo = 3072 + hf * 512
                op("tensor", lambda e, c=c, hf=hf, lo=lo: e.matmul(
                    PS[0:1, lo:lo + 512], AONE[:, 32:33],
                    usl(c)[:, hf * 512:hf * 512 + 512],
                    start=(c == 0), stop=(c == CH - 1)), wait=[t_u])
                op("tensor", lambda e, c=c, hf=hf, lo=lo: e.matmul(
                    PS[32:33, lo:lo + 512], AONE[:, 32:33],
                    SQ[:, (c % 2) * 1024 + hf * 512:(c % 2) * 1024 + hf * 512 + 512],
                    start=(c == 0), stop=(c == CH - 1)), wait=[(DV, dv_sq)])
            te_slot[c % 2] = TE.v
        return TE.v

    def ln_smalls(te_stats, fp32_bcast=False):
        def f1(e):
            e.tensor_scalar(out=ROWS[0:1, 0:1024], in0=PS[0:1, 3072:4096],
                            scalar1=1.0 / D, scalar2=None, op0=ALU.mult)
            e.tensor_scalar(out=ROWS[0:1, 1024:2048], in0=PS[32:33, 3072:4096],
                            scalar1=1.0 / D, scalar2=None, op0=ALU.mult)
            e.drain()
            e.tensor_mul(ROWS[0:1, 2048:3072], ROWS[0:1, 0:1024],
                         ROWS[0:1, 0:1024])
            e.drain()
            return e.tensor_sub(ROWS[32:33, 0:1024], ROWS[0:1, 1024:2048],
                                ROWS[0:1, 2048:3072])
        d1 = op("vector", f1, wait=[(TE, te_stats)])
        s1 = op("scalar", lambda e: e.activation(
            out=ROWS[32:33, 1024:2048], in_=ROWS[32:33, 0:1024], func=AF.Sqrt,
            bias=CEPS[32:33, 0:1], scale=1.0), wait=[(DV, d1)])

        def f2(e):
            e.reciprocal(ROWS[0:1, 2048:3072], ROWS[32:33, 1024:2048])
            e.drain()
            e.tensor_mul(ROWS[32:33, 0:1024], ROWS[0:1, 0:1024],
                         ROWS[0:1, 2048:3072])
            e.drain()
            e.tensor_copy(BROWS[0:1, 0:1024], ROWS[0:1, 2048:3072])
            e.tensor_copy(BROWS[32:33, 0:1024], ROWS[32:33, 0:1024])
            if fp32_bcast:
                e.drain()
                e.tensor_copy(ROWS[0:1, 1024:2048], ROWS[32:33, 0:1024])
                e.tensor_copy(SMF[0:1, 0:2048].bitcast(BF16)[:, 0:1024],
                              ROWS[32:33, 0:1024])
                e.drain()
                e.tensor_sub(ROWS[64:65, 0:1024], ROWS[0:1, 2048:3072],
                             BROWS[0:1, 0:1024])
                e.tensor_sub(ROWS[64:65, 1024:2048], ROWS[0:1, 1024:2048],
                             SMF[0:1, 0:2048].bitcast(BF16)[:, 0:1024])
                e.drain()
                e.tensor_copy(BROWS[0:1, 1024:2048], ROWS[64:65, 0:1024])
                return e.tensor_copy(
                    SMF[0:1, 0:2048].bitcast(BF16)[:, 1024:2048],
                    ROWS[64:65, 1024:2048])
            return e.engine_nop()
        d2 = op("vector", f2, wait=[(SC, s1)])
        for hf in range(2):
            sl = slice(hf * 512, hf * 512 + 512)
            sl2 = slice(1024 + hf * 512, 1024 + hf * 512 + 512)
            op("tensor", lambda e, sl=sl: e.matmul(
                PS[:, sl], ONESROW[0:1, :], BROWS[0:1, sl],
                start=True, stop=not fp32_bcast), wait=[(DV, d2)])
            if fp32_bcast:
                op("tensor", lambda e, sl=sl: e.matmul(
                    PS[:, sl], ONESROW[0:1, :],
                    BROWS[0:1, 1024 + sl.start:1024 + sl.start + 512],
                    start=False, stop=True))
            if not fp32_bcast:
                op("tensor", lambda e, sl=sl, sl2=sl2: e.matmul(
                    PS[:, sl2], NEGROW[32:33, :], BROWS[32:33, sl],
                    start=True, stop=True))
            else:
                op("tensor", lambda e, sl=sl, sl2=sl2: e.matmul(
                    PS[:, sl2], NEGROW[0:1, :],
                    SMF[0:1, 0:2048].bitcast(BF16)[:, sl],
                    start=True, stop=False))
                op("tensor", lambda e, sl=sl, sl2=sl2: e.matmul(
                    PS[:, sl2], NEGROW[0:1, :],
                    SMF[0:1, 0:2048].bitcast(BF16)[:, 1024 + sl.start:
                                                   1024 + sl.start + 512],
                    start=False, stop=True))
        return TE.v

    def ln_apply(src_hbm, te_bc, dst_bf):
        last = 0
        dv_slot = [DV.v, DV.v]
        sc_gate = SC.v
        for c in range(CH):
            t_x = dma("sync", xsl(c), xview(src_hbm, c),
                      wait=[(DV, dv_slot[c % 2])] +
                           ([(SC, sc_gate)] if c < 2 else []),
                      ch="xch%d" % (c % 2))

            def f(e, c=c):
                e.tensor_mul(XC2[:], xsl(c), PS[:, 0:1024])
                e.drain()
                return e.tensor_add(dst_bf[:, c * T:(c + 1) * T],
                                    XC2[:], PS[:, 1024:2048])
            last = op("vector", f, wait=[t_x, (TE, te_bc)])
            dv_slot[c % 2] = last
        return last

    def ln2_residual(u_cols, src_hbm, dst_hbm, te_bc, u_hbm=None):
        last_store = None
        dv_uslot = [DV.v, DV.v]
        dv_xslot = [DV.v, DV.v]
        te_gate = TE.v
        sc_gate = SC.v
        for c in range(CH):
            if u_hbm is not None:
                t_u = dma("sync", usl(c), xview(u_hbm, c),
                          wait=[(DV, dv_uslot[c % 2])] +
                               ([(TE, te_gate)] if c < 2 else []),
                          ch="sw%d" % (c % 2))
                usrc = usl(c)
                wl_extra = [t_u]
            else:
                tns, off = u_cols(c)
                usrc = tns[:, off:off + T]
                wl_extra = []
            t_x = dma("sync", xsl(c), xview(src_hbm, c),
                      wait=[(DV, dv_xslot[c % 2])] +
                           ([(SC, sc_gate)] if c < 2 else []),
                      ch="xch%d" % (c % 2))

            def f(e, usrc=usrc, c=c):
                e.tensor_mul(XC2[:], usrc, PS[:, 0:1024])
                e.drain()
                e.tensor_add(XC2[:], XC2[:], PS[:, 1024:2048])
                e.drain()
                return e.tensor_add(XC2[:], XC2[:], xsl(c))
            d = op("vector", f, wait=[t_x, (TE, te_bc)] + wl_extra +
                   ([last_store] if last_store else []))
            dv_uslot[c % 2] = d
            dv_xslot[c % 2] = d
            last_store = dma("gpsimd", xview(dst_hbm, c), XC2[:],
                             wait=[(DV, d)], ch="xst")
        return last_store

    def tern_prep(w_hbm, s_off, slot0, mean_free_dv=0, wbf_free_te=0):
        stop_at = PHASE_LIMIT
        """Ternary prep of a 512-col superchunk into WBF slots slot0..+15.
        mean_free_dv: PS bank4 free (prev superchunk's v-ops done).
        wbf_free_te: WBF slots free (matmuls of 2-ago superchunk done)."""
        sc_toks = []
        te_means = [TE.v, TE.v]
        sc_gate = SC.v
        for g in range(16):
            t_w = dma("sync", WRAW[:, (g % 2) * 512:(g % 2) * 512 + 512],
                      w_hbm[g * 128:(g + 1) * 128, s_off:s_off + 512],
                      wait=[(SC, sc_toks[g - 2] if g >= 2 else sc_gate)],
                      ch="wr%d" % (g % 2))

            def fc(e, g=g):
                sl = slice((g % 2) * 512, (g % 2) * 512 + 512)
                e.copy(WBF[:, (slot0 + g) * 512:(slot0 + g) * 512 + 512],
                       WRAW[:, sl])
                return e.activation(out=SW[:, (g % 2) * 512:(g % 2) * 512 + 512],
                                    in_=WRAW[:, sl], func=AF.Abs)
            st = op("scalar", fc,
                    wait=[t_w, (TE, te_means[g % 2])] +
                         ([(TE, wbf_free_te)] if g == 0 else []))
            sc_toks.append(st)
            op("tensor", lambda e, g=g: e.matmul(
                PS[0:16, 2048:2560], AONE[:, 32 - g:48 - g],
                SW[:, (g % 2) * 512:(g % 2) * 512 + 512],
                start=(g == 0), stop=(g == 15)),
                wait=[(SC, st)] + ([(DV, mean_free_dv)] if g == 0 else []))
            te_means[g % 2] = TE.v

        def fsm(e):
            e.tensor_scalar(out=TERN[0:16, 0:512], in0=PS[0:16, 2048:2560],
                            scalar1=1.0 / 128.0, scalar2=None, op0=ALU.mult)
            e.drain()
            e.tensor_copy(TERNB[0:16, 0:512], TERN[0:16, 0:512])
            e.drain()
            e.tensor_scalar(out=TERNB[0:16, 0:512], in0=TERNB[0:16, 0:512],
                            scalar1=9.99999993922529e-09, scalar2=None,
                            op0=ALU.max)
            e.drain()
            e.reciprocal(TERN[0:16, 0:512], TERNB[0:16, 0:512])
            e.drain()
            e.tensor_copy(TERNB[0:16, 512:1024], TERN[0:16, 0:512])
            e.drain()
            e.tensor_sub(ROWS[32:48, 0:512], TERN[0:16, 0:512],
                         TERNB[0:16, 512:1024])
            e.drain()
            return e.tensor_copy(TERNB[0:16, 1024:1536], ROWS[32:48, 0:512])
        if stop_at <= 1.1:
            return SC.v and DV.v
        dsm = op("vector", fsm, wait=[(TE, TE.v)])
        if stop_at <= 1.15:
            return dsm

        dv_last = dsm
        stg_te = TE.v
        for g in range(16):
            gs = dma("gpsimd", BROWS[0:1, 0:512],
                     TERNB[g:g + 1, 0:512],
                     wait=[(DV, dsm), (TE, stg_te)], ch="stg0")
            gs = dma("gpsimd", BROWS[0:1, 512:1024],
                     TERNB[g:g + 1, 512:1024], ch="stg0")
            gs = dma("gpsimd", BROWS[0:1, 1024:1536],
                     TERNB[g:g + 1, 1024:1536], ch="stg0")
            if stop_at <= 1.18:
                continue
            op("tensor", lambda e: e.matmul(
                PS[:, 2048:2560], ONESROW[0:1, :], BROWS[0:1, 512:1024],
                start=True, stop=False), wait=[gs, (DV, dv_last)])
            op("tensor", lambda e: e.matmul(
                PS[:, 2048:2560], ONESROW[0:1, :], BROWS[0:1, 1024:1536],
                start=False, stop=True))
            op("tensor", lambda e: e.matmul(
                PS[:, 2560:3072], ONESROW[0:1, :], BROWS[0:1, 0:512],
                start=True, stop=True))
            stg_te = TE.v
            if stop_at <= 1.19:
                continue
            te_bc = TE.v

            def fv(e, g=g):
                wsl = slice((slot0 + g) * 512, (slot0 + g) * 512 + 512)
                e.tensor_mul(VW[:, 0:512], WBF[:, wsl], PS[:, 2048:2560])
                e.drain()
                e.tensor_scalar(out=VW[:, 512:1024], in0=VW[:, 0:512],
                                scalar1=-1.25, scalar2=1.25,
                                op0=ALU.max, op1=ALU.min)
                e.drain()
                e.tensor_scalar(out=VW[:, 0:512], in0=VW[:, 512:1024],
                                scalar1=CROUND, scalar2=CROUND,
                                op0=ALU.add, op1=ALU.subtract)
                e.drain()
                return e.tensor_mul(WBF[:, wsl], VW[:, 0:512],
                                    PS[:, 2560:3072])
            dv_last = op("vector", fv, wait=[(TE, te_bc)])
        if stop_at <= 1.19:
            return dsm
        return dv_last

    def cast_prep(w_hbm, s_off, slot0, ncols=512, wbf_free_te=0):
        toks = [SC.v, SC.v]
        for g in range(16):
            t_w = dma("sync", WRAW[:, (g % 2) * 512:(g % 2) * 512 + ncols],
                      w_hbm[g * 128:(g + 1) * 128, s_off:s_off + ncols],
                      wait=[(SC, toks[g % 2])], ch="wr%d" % (g % 2))
            st = op("scalar", lambda e, g=g: e.copy(
                WBF[:, (slot0 + g) * 512:(slot0 + g) * 512 + ncols],
                WRAW[:, (g % 2) * 512:(g % 2) * 512 + ncols]),
                wait=[t_w] + ([(TE, wbf_free_te)] if g == 0 else []))
            toks[g % 2] = st
        return st

    def rowmax_to_scalar(scl_col, sc2_col, wait_dv):
        g1 = dma("gpsimd", scr128, SCL[:, scl_col:scl_col + 1],
                 wait=[(DV, wait_dv)], ch="msc")
        g2 = dma("gpsimd", RMAX[0:1, 0:128],
                 scr128.rearrange("p one -> (one) (p)"), wait=[g1], ch="msc")
        d = op("vector", lambda e: e.tensor_reduce(
            out=SC2[0:1, sc2_col:sc2_col + 1], in_=RMAX[0:1, 0:128],
            axis=mybir.AxisListType.X, op=ALU.max), wait=[g2])
        return d

    def bcast_scalars_f32(sc2_srcs, scl_dst0, wait_dv):
        n = len(sc2_srcs)

        def fhl(e):
            for i, c in enumerate(sc2_srcs):
                e.tensor_copy(BROWS[0:1, 1536 + i:1537 + i],
                              SC2[0:1, c:c + 1])
            e.drain()
            for i, c in enumerate(sc2_srcs):
                e.tensor_sub(SC2[0:1, 24 + i:25 + i], SC2[0:1, c:c + 1],
                             BROWS[0:1, 1536 + i:1537 + i])
            e.drain()
            for i in range(n):
                e.tensor_copy(TERNB[0:1, i:i + 1],
                              SC2[0:1, 24 + i:25 + i])
            return e.engine_nop()
        d_hl = op("vector", fhl, wait=[(DV, wait_dv)])
        for i in range(n):
            op("tensor", lambda e, i=i: e.matmul(
                PS[:, 2048 + i:2048 + i + 1], ONESROW[0:1, :],
                BROWS[0:1, 1536 + i:1537 + i], start=True, stop=False),
                wait=[(DV, d_hl)] if i == 0 else [])
            op("tensor", lambda e, i=i: e.matmul(
                PS[:, 2048 + i:2048 + i + 1], ONESROW[0:1, :],
                TERNB[0:1, i:i + 1], start=False, stop=True))
        d_sc = op("vector", lambda e: e.tensor_copy(
            SCL[:, scl_dst0:scl_dst0 + n], PS[:, 2048:2048 + n]),
            wait=[(TE, TE.v)])
        return d_sc

    def attention(kv_of_head, qsrc, ysrc):
        last_tmr = 0
        last_exp = 0
        last_ptte = 0
        last_yev = 0
        for j in range(8):
            L2 = (j + 1) * 256
            for h in range(16):
                kv = kv_of_head(h)
                for (lo, hi) in _bank_splits(2048, 2048 + L2):
                    op("tensor", lambda e, j=j, h=h, kv=kv, lo=lo, hi=hi: e.matmul(
                        PS[:, lo:hi],
                        qsrc[:, h * T + j * 128:h * T + j * 128 + 128],
                        KG[:, kv * 2048 + (lo - 2048):kv * 2048 + (hi - 2048)],
                        start=True, stop=True),
                        wait=([(DV, last_tmr)] if lo == 2048 else []))
                te_s = TE.v
                def fm(e, j=j, L2=L2):
                    e.tensor_mask_reduce(
                        out=SMF[:, 0:L2], in_=PS[:, 2048:2048 + L2],
                        mask_start=0.0, mask_end=LENS[:, j:j + 1], scale=-0.5,
                        accum_in=POSBIG, op=ALU.min, accum_out=SCL[:, 0:1])
                    e.drain()
                    return e.tensor_scalar(out=SCL[:, 1:2], in0=SCL[:, 0:1],
                                           scalar1=2.0, scalar2=None,
                                           op0=ALU.mult)
                dv_m = op("vector", fm, wait=[(TE, te_s), (SC, last_exp)])
                last_tmr = dv_m
                sc_e = op("scalar", lambda e, L2=L2: e.activation(
                    out=PPB[:, 0:L2], in_=SMF[:, 0:L2], func=AF.Exp,
                    bias=SCL[:, 1:2], scale=-2.0, accum_out=SCL[:, 2:3]),
                    wait=[(DV, dv_m), (TE, last_ptte)])
                last_exp = sc_e

                def fd(e):
                    e.reciprocal(SCL[:, 3:4], SCL[:, 2:3])
                    e.drain()
                    return e.tensor_scalar(out=VW[:, 0:128], in0=IDENT[:],
                                           scalar1=SCL[:, 3:4], scalar2=None,
                                           op0=ALU.mult)
                dv_d = op("vector", fd, wait=[(SC, sc_e)])
                nb = L2 // 128
                for b4 in range(0, nb, 4):
                    bb = min(4, nb - b4)
                    bank = (b4 // 4) % 2
                    for b in range(b4, b4 + bb):
                        op("tensor", lambda e, b=b, b4=b4, bank=bank: e.matmul(
                            PS[:, bank * 512 + (b - b4) * 128:
                               bank * 512 + (b - b4) * 128 + 128],
                            PPB[:, b * 128:b * 128 + 128], VW[:, 0:128],
                            start=True, stop=True),
                            wait=([(DV, dv_d if b4 == 0 else DV.v)]
                                  if b == b4 else []))
                    te_pt = TE.v
                    dv_e = op("vector",
                              lambda e, bank=bank, bb=bb: e.tensor_copy(
                                  PT[:, bank * 512:bank * 512 + bb * 128],
                                  PS[:, bank * 512:bank * 512 + bb * 128]),
                              wait=[(TE, te_pt)])
                    for b in range(b4, b4 + bb):
                        op("tensor", lambda e, b=b, b4=b4, bank=bank, kv=kv, nb=nb: e.matmul(
                            PS[:, 1024:1152],
                            VG[:, kv * 2048 + b * 128:kv * 2048 + b * 128 + 128],
                            PT[:, bank * 512 + (b - b4) * 128:
                               bank * 512 + (b - b4) * 128 + 128],
                            start=(b == 0), stop=(b == nb - 1)),
                            wait=([(DV, dv_e)] if b == b4 else
                                  ([(DV, last_yev)] if b == 0 else [])))
                last_ptte = TE.v
                last_yev = op("vector", lambda e, j=j, h=h: e.tensor_copy(
                    ysrc[:, h * T + j * 128:h * T + j * 128 + 128],
                    PS[:, 1024:1152]), wait=[(TE, last_ptte)])

    def rms_bcast_row(row_ap_bf, ps_lo, base=0, wait=()):
        for hf in range(2):
            op("tensor", lambda e, hf=hf: e.matmul(
                PS[:, ps_lo + hf * 512:ps_lo + hf * 512 + 512],
                ONESROW[base:base + 1, :],
                row_ap_bf[:, hf * 512:hf * 512 + 512], start=True, stop=True),
                wait=(list(wait) if hf == 0 else []))
        return TE.v

    wmax_state = {"d": 0, "first": {10: True, 11: True}, "done": {}}
    wmax_tiles = ([("fc", g, c0, 10) for g in range(16) for c0 in range(0, 8192, 1024)] +
                  [("pr", g, c0, 11) for g in range(64) for c0 in range(0, 2048, 1024)])

    def emit_wmax_tiles(n):
        cnt = 0
        while wmax_tiles and cnt < n:
            kind, g, c0, dst_scl = wmax_tiles.pop(0)
            wsrc = wfcT if kind == "fc" else wprT
            t_w = dma("sync", WRAW[:], wsrc[g * 128:(g + 1) * 128, c0:c0 + 1024],
                      wait=[(DV, wmax_state["d"])], ch="wr0")
            first = wmax_state["first"][dst_scl]
            wmax_state["first"][dst_scl] = False

            def f(e, first=first, dst_scl=dst_scl):
                e.tensor_reduce(out=SCL[:, 12:13], in_=WRAW[:],
                                axis=mybir.AxisListType.X, op=ALU.max,
                                apply_absolute_value=True)
                e.drain()
                if first:
                    return e.tensor_copy(SCL[:, dst_scl:dst_scl + 1],
                                         SCL[:, 12:13])
                return e.tensor_tensor(SCL[:, dst_scl:dst_scl + 1],
                                       SCL[:, dst_scl:dst_scl + 1],
                                       SCL[:, 12:13], op=ALU.max)
            wmax_state["d"] = op("vector", f, wait=[t_w])
            cnt += 1

    def finish_wmax():
        emit_wmax_tiles(10 ** 9)
        wmax_state["done"]["dwf"] = rowmax_to_scalar(10, 0, wmax_state["d"])
        wmax_state["done"]["dwp"] = rowmax_to_scalar(11, 1, wmax_state["d"])

    def block_l0_attn(src_hbm, dst_hbm):
        def phase_done(n):
            return PHASE_LIMIT <= n
        te_st = ln_stats_hbm(src_hbm, XN)
        te_bc = ln_smalls(te_st)
        xn_tok = ln_apply(src_hbm, te_bc, XN)
        barrier()
        if phase_done(1):
            return

        ccA_in_v = ccA_in.rearrange("a (k p c) -> a k p c", p=128, c=1024)
        prep_tok = tern_prep(wqkvT, 0, 0)
        if phase_done(1.2):
            barrier()
            return
        te_mm_end = [0, 0]
        ss_te = [0, 0]
        evac_tok = [0, 0]
        for s in range(6 if not phase_done(1.5) else 1):
            slot0 = (s % 2) * 16
            for o4 in range(4):
                oc = s * 4 + o4
                pb = (o4 % 2) * 1024
                for hf in range(2):
                    for g in range(CH):
                        wl = []
                        if g == 0 and hf == 0:
                            if o4 == 0:
                                wl = [(DV, prep_tok), (DV, xn_tok)]
                            wl = wl + [evac_tok[o4 % 2]] if evac_tok[o4 % 2] else wl
                        op("tensor", lambda e, slot0=slot0, g=g, o4=o4, hf=hf, pb=pb: e.matmul(
                            PS[:, pb + hf * 512:pb + hf * 512 + 512],
                            WBF[:, (slot0 + g) * 512 + o4 * 128:
                                (slot0 + g) * 512 + o4 * 128 + 128],
                            XN[:, g * T + hf * 512:g * T + hf * 512 + 512],
                            start=(g == 0), stop=(g == CH - 1)), wait=wl)
                te_mm = TE.v
                if oc < 16:
                    dst = QB[:, oc * T:(oc + 1) * T]
                elif oc < 20:
                    dst = KG[:, (oc - 16) * 2048:(oc - 16) * 2048 + 1024]
                else:
                    dst = PPB[:, (oc - 20) * 1024:(oc - 20) * 1024 + 1024]
                sc_ev = op("scalar", lambda e, dst=dst, pb=pb: e.copy(
                    dst, PS[:, pb:pb + 1024]), wait=[(TE, te_mm)])
                evac_tok[o4 % 2] = (SC, sc_ev)
                if oc < 20:
                    dv_sq = op("vector", lambda e, dst=dst, oc=oc: e.tensor_mul(
                        SQ[:, (oc % 2) * 1024:(oc % 2) * 1024 + 1024],
                        dst, dst), wait=[(SC, sc_ev), (TE, ss_te[oc % 2])])
                    for hf in range(2):
                        op("tensor", lambda e, oc=oc, hf=hf: e.matmul(
                            PS[0:32, 3072 + hf * 512:3072 + hf * 512 + 512],
                            AONE[:, 32 - oc:64 - oc],
                            SQ[:, (oc % 2) * 1024 + hf * 512:
                               (oc % 2) * 1024 + hf * 512 + 512],
                            start=(oc == 0), stop=(oc == 19)),
                            wait=[(DV, dv_sq)])
                    ss_te[oc % 2] = TE.v
            te_mm_end[s % 2] = TE.v
            if s < 5 and not phase_done(1.5):
                prep_tok = tern_prep(wqkvT, (s + 1) * 512, ((s + 1) % 2) * 16,
                                     mean_free_dv=prep_tok,
                                     wbf_free_te=te_mm_end[(s + 1) % 2])
        barrier()
        if phase_done(1.5):
            return
        if phase_done(2):
            return

        d_r = op("vector", lambda e: e.tensor_scalar(
            out=ROWS[0:20, 0:1024], in0=PS[0:20, 3072:4096],
            scalar1=1.0 / 128.0, scalar2=None, op0=ALU.mult))
        s_r = op("scalar", lambda e: e.activation(
            out=ROWS[32:52, 0:1024], in_=ROWS[0:20, 0:1024], func=AF.Sqrt,
            bias=CEPS[0:20, 1:2], scale=1.0), wait=[(DV, d_r)])

        def frms2(e):
            e.reciprocal(ROWS[0:20, 0:1024], ROWS[32:52, 0:1024])
            e.drain()
            e.tensor_scalar(out=ROWS[0:16, 0:1024], in0=ROWS[0:16, 0:1024],
                            scalar1=QGC[:, 0:1], scalar2=None, op0=ALU.mult)
            e.drain()
            return e.tensor_copy(BROWS[0:20, 1024:2048], ROWS[0:20, 0:1024])
        d_r2 = op("vector", frms2, wait=[(SC, s_r)])

        d_rope = d_r2
        te_b = 0
        for idx in range(20):
            if idx < 16:
                src_c = QB[:, idx * T:(idx + 1) * T]
            else:
                src_c = KG[:, (idx - 16) * 2048:(idx - 16) * 2048 + 1024]
            g1 = dma("gpsimd", SW[0:64, :], src_c[64:128, :],
                     wait=[(DV, d_rope)], ch="swp")
            g2 = dma("gpsimd", SW[64:128, :], src_c[0:64, :], ch="swp")
            gs = dma("gpsimd", BROWS[0:1, 0:1024],
                     BROWS[idx:idx + 1, 1024:2048],
                     wait=[(DV, d_r2), (TE, te_b)], ch="stg0")
            te_b = rms_bcast_row(BROWS[0:1, 0:1024], 2048, base=0,
                                 wait=[(DV, d_rope), gs])

            def frope(e, src_c=src_c):
                e.tensor_mul(XCH[:], src_c, COSX[:])
                e.tensor_mul(XC2[:], SW[:], SINX[:])
                e.drain()
                e.tensor_add(XCH[:], XCH[:], XC2[:])
                e.drain()
                return e.tensor_mul(src_c, XCH[:], PS[:, 2048:3072])
            d_rope = op("vector", frope, wait=[g2, (TE, te_b)])
            if idx >= 16:
                dma("gpsimd", ccA_in_v[0, idx - 16], src_c,
                    wait=[(DV, d_rope)], ch="blob")
        dv_prev = d_rope
        for vv in range(4):
            for b in range(8):
                te_t = op("tensor", lambda e, vv=vv, b=b: e.transpose(
                    PS[:, 2048 + (b % 2) * 512:2048 + (b % 2) * 512 + 64].bitcast(BF16),
                    PPB[:, vv * 1024 + b * 128:vv * 1024 + b * 128 + 128],
                    IDENT[:]), wait=[(DV, dv_prev)])
                dv_prev = op("vector", lambda e, vv=vv, b=b: e.tensor_copy(
                    VG[:, vv * 2048 + b * 128:vv * 2048 + b * 128 + 128],
                    PS[:, 2048 + (b % 2) * 512:2048 + (b % 2) * 512 + 64].bitcast(BF16)),
                    wait=[(TE, te_t)])
            dma("gpsimd", ccA_in_v[0, 4 + vv],
                VG[:, vv * 2048:vv * 2048 + 1024], wait=[(DV, dv_prev)],
                ch="blob")
        barrier()
        if phase_done(3):
            return

        ccc = chan("cc")
        blobc = chan("blob")
        cc_tok = (ccc, ccc.bump(1))
        blob_full = (blobc, blobc.v)

        def fcc_emit(e, bf=blob_full):
            e.wait_ge(bf[0].sem, bf[1])
            inst = e.collective_compute(
                "AllGather", ALU.bypass,
                replica_groups=[[0, 1], [2, 3], [4, 5], [6, 7]],
                ins=[ccA_in], outs=[ccA_out])
            inst.then_inc(ccc.sem, 1)
        streams["gpsimd"].append(fcc_emit)

        ccA_out_v = ccA_out.rearrange("a (r k p c) -> a r k p c",
                                      r=2, p=128, c=1024)
        for kk in range(4):
            for par in range(2):
                dma("gpsimd",
                    KG[:, kk * 2048:kk * 2048 + 2048].rearrange(
                        "p (b two c) -> p b two c", two=2, c=128)[:, :, par],
                    ccA_out_v[0, par, kk].rearrange("p (b c) -> p b c", c=128),
                    wait=([cc_tok] if (kk == 0 and par == 0) else []), ch="gath")
        for vv in range(4):
            for par in range(2):
                dma("gpsimd",
                    VG[:, vv * 2048:vv * 2048 + 2048].rearrange(
                        "p (b two c) -> p b two c", two=2, c=128)[:, :, par],
                    ccA_out_v[0, par, 4 + vv].rearrange("p (b c) -> p b c",
                                                        c=128), ch="gath")
        barrier()
        if phase_done(4):
            return

        attention(lambda h: h // 4, QB, XN)
        barrier()
        if phase_done(5):
            return

        te_ys = ln_stats_sbuf(lambda c: (BIG, c * T))

        def fyr(e):
            return e.tensor_scalar(out=ROWS[0:1, 0:1024],
                                   in0=PS[32:33, 3072:4096],
                                   scalar1=1.0 / D, scalar2=None, op0=ALU.mult)
        d_y = op("vector", fyr, wait=[(TE, te_ys)])
        s_y = op("scalar", lambda e: e.activation(
            out=ROWS[32:33, 0:1024], in_=ROWS[0:1, 0:1024], func=AF.Sqrt,
            bias=CEPS[0:1, 1:2], scale=1.0), wait=[(DV, d_y)])

        def fyr2(e):
            e.reciprocal(ROWS[0:1, 0:1024], ROWS[32:33, 0:1024])
            e.drain()
            return e.tensor_copy(BROWS[0:1, 0:1024], ROWS[0:1, 0:1024])
        d_y2 = op("vector", fyr2, wait=[(SC, s_y)])
        te_yb = rms_bcast_row(BROWS[0:1, 0:1024], 0, wait=[(DV, d_y2)])
        yn_tok = 0
        for c in range(CH):
            yn_tok = op("vector", lambda e, c=c: e.tensor_mul(
                QB[:, c * T:(c + 1) * T], XN[:, c * T:(c + 1) * T],
                PS[:, 0:1024]), wait=[(TE, te_yb)])
        barrier()

        prep_tok = tern_prep(wprojT, 0, 0)
        te_mm_end = [0, 0]
        evac_tok = [None, None]
        for s in range(4):
            slot0 = (s % 2) * 16
            for o4 in range(4):
                oc = s * 4 + o4
                pb = (o4 % 2) * 1024
                for hf in range(2):
                    for g in range(CH):
                        wl = []
                        if g == 0 and hf == 0:
                            if o4 == 0:
                                wl = [(DV, prep_tok), (DV, yn_tok)]
                            if evac_tok[o4 % 2]:
                                wl = wl + [evac_tok[o4 % 2]]
                        op("tensor", lambda e, slot0=slot0, g=g, o4=o4, hf=hf, pb=pb: e.matmul(
                            PS[:, pb + hf * 512:pb + hf * 512 + 512],
                            WBF[:, (slot0 + g) * 512 + o4 * 128:
                                (slot0 + g) * 512 + o4 * 128 + 128],
                            QB[:, g * T + hf * 512:g * T + hf * 512 + 512],
                            start=(g == 0), stop=(g == CH - 1)), wait=wl)
                te_mm = TE.v
                sc_ev = op("scalar", lambda e, oc=oc, pb=pb: e.copy(
                    XN[:, oc * T:(oc + 1) * T], PS[:, pb:pb + 1024]),
                    wait=[(TE, te_mm)])
                evac_tok[o4 % 2] = (SC, sc_ev)
            te_mm_end[s % 2] = TE.v
            if s < 3:
                prep_tok = tern_prep(wprojT, (s + 1) * 512, ((s + 1) % 2) * 16,
                                     mean_free_dv=prep_tok,
                                     wbf_free_te=te_mm_end[(s + 1) % 2])
        barrier()

        te_st2 = ln_stats_sbuf(lambda c: (BIG, c * T))
        te_bc2 = ln_smalls(te_st2)
        ln2_residual(lambda c: (BIG, c * T), src_hbm, dst_hbm, te_bc2)
        barrier()

    def block_l0_ffn(src_hbm, dst_hbm):
        te_st = ln_stats_hbm(src_hbm, XN)
        te_bc = ln_smalls(te_st)
        xn_tok = ln_apply(src_hbm, te_bc, XN)
        barrier()

        te_mm_last = 0
        ev_last = []
        for sc_i in range(11):
            prep_toks = []
            u8_tok = [DV.v] * 4
            for w_i, wsrc in ((0, wqgT), (16, wquT)):
                for g in range(16):
                    slot = (0 if w_i == 0 else 2) + (g % 2)
                    t_w = dma("sync", WU8[:, slot * 512:slot * 512 + 512],
                              wsrc[g * 128:(g + 1) * 128,
                                   sc_i * 512:sc_i * 512 + 512],
                              wait=[(DV, u8_tok[slot])], ch="wu%d" % slot)
                    tok = op("vector", lambda e, w_i=w_i, g=g, slot=slot: e.tensor_scalar(
                        out=WBF[:, (w_i + g) * 512:(w_i + g) * 512 + 512],
                        in0=WU8[:, slot * 512:slot * 512 + 512],
                        scalar1=128.0, scalar2=None, op0=ALU.subtract),
                        wait=[t_w] +
                             ([(TE, te_mm_last)] if (w_i == 0 and g == 0) else []))
                    u8_tok[slot] = tok
                    prep_toks.append(tok)
            prep_done = prep_toks[-1]
            for q_i in range(4):
                hc = sc_i * 4 + q_i
                for hf in range(2):
                    for g in range(CH):
                        wl = []
                        if g == 0 and hf == 0:
                            wl = [(DV, prep_done)] + \
                                ([(DV, xn_tok)] if hc == 0 else []) + ev_last
                        op("tensor", lambda e, g=g, q_i=q_i, hf=hf: e.matmul(
                            PS[:, hf * 512:hf * 512 + 512],
                            WBF[:, g * 512 + q_i * 128:g * 512 + q_i * 128 + 128],
                            XN[:, g * T + hf * 512:g * T + hf * 512 + 512],
                            start=(g == 0), stop=(g == CH - 1)), wait=wl)
                    for g in range(CH):
                        op("tensor", lambda e, g=g, q_i=q_i, hf=hf: e.matmul(
                            PS[:, 1024 + hf * 512:1024 + hf * 512 + 512],
                            WBF[:, (16 + g) * 512 + q_i * 128:
                                (16 + g) * 512 + q_i * 128 + 128],
                            XN[:, g * T + hf * 512:g * T + hf * 512 + 512],
                            start=(g == 0), stop=(g == CH - 1)))
                te_mm = TE.v
                s_si = op("scalar", lambda e, hc=hc: e.activation(
                    out=VW[:], in_=PS[:, 0:1024], func=AF.Silu,
                    bias=0.0, scale=SGC[:, hc:hc + 1]), wait=[(TE, te_mm)])
                tns, off = tcol(hc)
                d_t = op("vector", lambda e, tns=tns, off=off, hc=hc: e.scalar_tensor_tensor(
                    out=tns[:, off:off + T], in0=PS[:, 1024:2048],
                    scalar=SUC[:, hc:hc + 1], in1=VW[:],
                    op0=ALU.mult, op1=ALU.mult), wait=[(SC, s_si)])
                ev_last = [(SC, s_si), (DV, d_t)]
                emit_wmax_tiles(5)
            te_mm_last = TE.v
        barrier()

        te_mm_last = 0
        ev_last = [None, None]
        ust_last = [None]
        for os_i in range(8):
            prep_toks = []
            u8_tok = [DV.v] * 4
            for g in range(44):
                slot = g % 4
                t_w = dma("sync", WU8[:, slot * 512:slot * 512 + 256],
                          wqdT[g * 128:(g + 1) * 128,
                               os_i * 256:os_i * 256 + 256],
                          wait=[(DV, u8_tok[slot])], ch="wu%d" % slot)
                tok = op("vector", lambda e, g=g, slot=slot: e.tensor_scalar(
                    out=BIG[:, g * 256:g * 256 + 256],
                    in0=WU8[:, slot * 512:slot * 512 + 256],
                    scalar1=128.0, scalar2=None, op0=ALU.subtract),
                    wait=[t_w] +
                         ([(TE, te_mm_last)] if g == 0 else []))
                u8_tok[slot] = tok
                prep_toks.append(tok)
            prep_done = prep_toks[-1]
            for oc2 in range(2):
                oc = os_i * 2 + oc2
                pb = oc2 * 1024
                for hf in range(2):
                    for g in range(44):
                        tns, off = tcol(g)
                        wl = []
                        if g == 0 and hf == 0:
                            wl = [(DV, prep_done)]
                            if ev_last[oc2]:
                                wl = wl + [ev_last[oc2]]
                        op("tensor", lambda e, g=g, oc2=oc2, hf=hf, pb=pb, tns=tns, off=off: e.matmul(
                            PS[:, pb + hf * 512:pb + hf * 512 + 512],
                            BIG[:, g * 256 + oc2 * 128:g * 256 + oc2 * 128 + 128],
                            tns[:, off + hf * 512:off + hf * 512 + 512],
                            start=(g == 0), stop=(g == 43)), wait=wl)
                te_mm = TE.v
                d_ev = op("vector", lambda e, oc=oc, pb=pb: e.tensor_scalar(
                    out=PT[:], in0=PS[:, pb:pb + 1024],
                    scalar1=SDC[:, oc:oc + 1], scalar2=None, op0=ALU.mult),
                    wait=[(TE, te_mm)] + ([ust_last[0]] if ust_last[0] else []))
                ust_last[0] = dma("gpsimd", xview(u_dram, oc), PT[:],
                                  wait=[(DV, d_ev)], ch="ust")
                ev_last[oc2] = (DV, d_ev)
                emit_wmax_tiles(4)
            te_mm_last = TE.v
        finish_wmax()
        barrier()

        te_st2 = ln_stats_udram(u_dram)
        te_bc2 = ln_smalls(te_st2)
        ln2_residual(None, src_hbm, dst_hbm, te_bc2, u_hbm=u_dram)
        barrier()

    def block_l1_attn(src_hbm, dst_hbm):
        te_st = ln_stats_hbm(src_hbm, XN)
        te_bc = ln_smalls(te_st)
        xn_tok = ln_apply(src_hbm, te_bc, XN)
        barrier()

        ccB_in_v = ccB_in.rearrange("a (k p c) -> a k p c", p=128, c=1024)
        prep_tok = cast_prep(l1wT, 0, 0)
        te_mm_end = [0, 0]
        evac_tok = [None, None]
        for s in range(5):
            slot0 = (s % 2) * 16
            ncols = 512 if s < 4 else 256
            for o4 in range(ncols // 128):
                oc = s * 4 + o4
                pb = (o4 % 2) * 1024
                for hf in range(2):
                    for g in range(CH):
                        wl = []
                        if g == 0 and hf == 0:
                            if o4 == 0:
                                wl = [(SC, prep_tok), (DV, xn_tok)]
                            if evac_tok[o4 % 2]:
                                wl = wl + [evac_tok[o4 % 2]]
                        op("tensor", lambda e, slot0=slot0, g=g, o4=o4, hf=hf, pb=pb: e.matmul(
                            PS[:, pb + hf * 512:pb + hf * 512 + 512],
                            WBF[:, (slot0 + g) * 512 + o4 * 128:
                                (slot0 + g) * 512 + o4 * 128 + 128],
                            XN[:, g * T + hf * 512:g * T + hf * 512 + 512],
                            start=(g == 0), stop=(g == CH - 1)), wait=wl)
                te_mm = TE.v
                if oc < 16:
                    d_e = op("vector", lambda e, oc=oc, pb=pb: e.tensor_scalar(
                        out=QB[:, oc * T:(oc + 1) * T], in0=PS[:, pb:pb + 1024],
                        scalar1=L1BI[:, oc:oc + 1], scalar2=INV_SQRT_HD,
                        op0=ALU.add, op1=ALU.mult), wait=[(TE, te_mm)])
                    evac_tok[o4 % 2] = (DV, d_e)
                elif oc == 16:
                    d_e = op("vector", lambda e, pb=pb: e.tensor_scalar(
                        out=RT[:, 0:1024], in0=PS[:, pb:pb + 1024],
                        scalar1=L1BI[:, 16:17], scalar2=None, op0=ALU.add),
                        wait=[(TE, te_mm)])
                    dma("gpsimd", ccB_in_v[0, 0], RT[:, 0:1024],
                        wait=[(DV, d_e)], ch="blob")
                    evac_tok[o4 % 2] = (DV, d_e)
                else:
                    d_e = op("vector", lambda e, pb=pb: e.tensor_scalar(
                        out=RT[:, 1024:2048], in0=PS[:, pb:pb + 1024],
                        scalar1=L1BI[:, 17:18], scalar2=None, op0=ALU.add),
                        wait=[(TE, te_mm)])
                    evac_tok[o4 % 2] = (DV, d_e)
                    dv_prev = d_e
                    for b in range(8):
                        te_t = op("tensor", lambda e, b=b: e.transpose(
                            PS[:, 2048 + (b % 2) * 512:2048 + (b % 2) * 512 + 64].bitcast(BF16),
                            RT[:, 1024 + b * 128:1024 + b * 128 + 128],
                            IDENT[:]), wait=[(DV, dv_prev)])
                        dv_prev = op("vector", lambda e, b=b: e.tensor_copy(
                            VG[:, b * 128:b * 128 + 128],
                            PS[:, 2048 + (b % 2) * 512:2048 + (b % 2) * 512 + 64].bitcast(BF16)),
                            wait=[(TE, te_t)])
                    dma("gpsimd", ccB_in_v[0, 1], VG[:, 0:1024],
                        wait=[(DV, dv_prev)], ch="blob")
            te_mm_end[s % 2] = TE.v
            if s < 4:
                prep_tok = cast_prep(l1wT, (s + 1) * 512, ((s + 1) % 2) * 16,
                                     ncols=(512 if s < 3 else 256),
                                     wbf_free_te=te_mm_end[(s + 1) % 2])
        barrier()

        ccc = chan("cc")
        blobc = chan("blob")
        cc_tok = (ccc, ccc.bump(1))
        blob_full = (blobc, blobc.v)

        def fcc_emit(e, bf=blob_full):
            e.wait_ge(bf[0].sem, bf[1])
            inst = e.collective_compute(
                "AllGather", ALU.bypass,
                replica_groups=[[0, 1], [2, 3], [4, 5], [6, 7]],
                ins=[ccB_in], outs=[ccB_out])
            inst.then_inc(ccc.sem, 1)
        streams["gpsimd"].append(fcc_emit)
        ccB_out_v = ccB_out.rearrange("a (r k p c) -> a r k p c",
                                      r=2, p=128, c=1024)
        for par in range(2):
            dma("gpsimd",
                KG[:, 0:2048].rearrange("p (b two c) -> p b two c",
                                        two=2, c=128)[:, :, par],
                ccB_out_v[0, par, 0].rearrange("p (b c) -> p b c", c=128),
                wait=([cc_tok] if par == 0 else []), ch="gath")
            dma("gpsimd",
                VG[:, 0:2048].rearrange("p (b two c) -> p b two c",
                                        two=2, c=128)[:, :, par],
                ccB_out_v[0, par, 1].rearrange("p (b c) -> p b c", c=128),
                ch="gath")
        barrier()

        attention(lambda h: 0, QB, XN)
        barrier()

        prep_tok = cast_prep(l1owT, 0, 0)
        te_mm_end = [0, 0]
        evac_tok = [None, None]
        for s in range(4):
            slot0 = (s % 2) * 16
            for o4 in range(4):
                oc = s * 4 + o4
                pb = (o4 % 2) * 1024
                for hf in range(2):
                    for g in range(CH):
                        wl = []
                        if g == 0 and hf == 0:
                            if o4 == 0:
                                wl = [(SC, prep_tok)]
                            if evac_tok[o4 % 2]:
                                wl = wl + [evac_tok[o4 % 2]]
                        op("tensor", lambda e, slot0=slot0, g=g, o4=o4, hf=hf, pb=pb: e.matmul(
                            PS[:, pb + hf * 512:pb + hf * 512 + 512],
                            WBF[:, (slot0 + g) * 512 + o4 * 128:
                                (slot0 + g) * 512 + o4 * 128 + 128],
                            XN[:, g * T + hf * 512:g * T + hf * 512 + 512],
                            start=(g == 0), stop=(g == CH - 1)), wait=wl)
                te_mm = TE.v
                d_e = op("vector", lambda e, oc=oc, pb=pb: e.tensor_scalar(
                    out=QB[:, oc * T:(oc + 1) * T], in0=PS[:, pb:pb + 1024],
                    scalar1=L1OB[:, oc:oc + 1], scalar2=None, op0=ALU.add),
                    wait=[(TE, te_mm)])
                evac_tok[o4 % 2] = (DV, d_e)
            te_mm_end[s % 2] = TE.v
            if s < 3:
                prep_tok = cast_prep(l1owT, (s + 1) * 512, ((s + 1) % 2) * 16,
                                     wbf_free_te=te_mm_end[(s + 1) % 2])
        barrier()

        te_st2 = ln_stats_sbuf(lambda c: (BIG, 16384 + c * T))
        te_bc2 = ln_smalls(te_st2)
        ln2_residual(lambda c: (BIG, 16384 + c * T), src_hbm, dst_hbm, te_bc2)
        barrier()

    def block_l1_ffn(src_hbm, dst_hbm):
        def phase_done(n):
            return PHASE_LIMIT <= n
        def fws(e):
            e.tensor_scalar(out=SC2[0:1, 2:4], in0=SC2[0:1, 0:2],
                            scalar1=1.0 / 448.0, scalar2=1e-6,
                            op0=ALU.mult, op1=ALU.max)
            e.drain()
            e.reciprocal(SC2[0:1, 6:8], SC2[0:1, 2:4])
            e.drain()
            return e.tensor_scalar(out=SC2[0:1, 4:6], in0=SC2[0:1, 6:8],
                                   scalar1=0.5, scalar2=None, op0=ALU.mult)
        d_ws = op("vector", fws,
                  wait=[(DV, wmax_state["done"]["dwf"]),
                        (DV, wmax_state["done"]["dwp"])])
        barrier()
        if phase_done(3.1):
            return

        for c in range(CH):
            t_x = dma("sync", XCH[:], xview(src_hbm, c),
                      wait=[(TE, TE.v if c > 0 else 0)], ch="xch")
            sc_tok = op("scalar", lambda e: e.copy(SW[:], XCH[:]),
                        wait=[t_x])
            dv_tok = op("vector", lambda e, c=c: e.tensor_mul(
                SQ[:, (c % 2) * 1024:(c % 2) * 1024 + 1024], SW[:], SW[:]),
                wait=[(SC, sc_tok)])
            for hf in range(2):
                lo = 3072 + hf * 512
                op("tensor", lambda e, c=c, hf=hf, lo=lo: e.matmul(
                    PS[0:1, lo:lo + 512], AONE[:, 32:33],
                    SW[:, hf * 512:hf * 512 + 512],
                    start=(c == 0), stop=(c == CH - 1)), wait=[(SC, sc_tok)])
                op("tensor", lambda e, c=c, hf=hf, lo=lo: e.matmul(
                    PS[32:33, lo:lo + 512], AONE[:, 32:33],
                    SQ[:, (c % 2) * 1024 + hf * 512:(c % 2) * 1024 + hf * 512 + 512],
                    start=(c == 0), stop=(c == CH - 1)), wait=[(DV, dv_tok)])
        te_bc = ln_smalls(TE.v, fp32_bcast=True)
        barrier()
        if phase_done(3.2):
            return

        first = True
        d_mx = 0
        sc_gate = SC.v
        for c in range(CH):
            t_x = dma("sync", XCH[:], xview(src_hbm, c),
                      wait=[(DV, d_mx)] + ([(SC, sc_gate)] if c == 0 else []),
                      ch="xch")

            def f(e, first=first):
                e.tensor_mul(XC2[:], XCH[:], PS[:, 0:1024])
                e.drain()
                e.tensor_add(XC2[:], XC2[:], PS[:, 1024:2048])
                e.drain()
                e.tensor_reduce(out=SCL[:, 10:11], in_=XC2[:],
                                axis=mybir.AxisListType.X, op=ALU.max,
                                apply_absolute_value=True)
                e.drain()
                if first:
                    return e.tensor_copy(SCL[:, 13:14], SCL[:, 10:11])
                return e.tensor_tensor(SCL[:, 13:14], SCL[:, 13:14],
                                       SCL[:, 10:11], op=ALU.max)
            d_mx = op("vector", f, wait=[t_x, (TE, te_bc)])
            first = False
        d_xm = rowmax_to_scalar(13, 8, d_mx)
        g2 = dma("gpsimd", ccm_in, SC2[0:1, 8:9], wait=[(DV, d_xm)], ch="msc")
        ccc = chan("cc")
        cc1 = (ccc, ccc.bump(1))

        def fcc1(e, g2=g2):
            e.wait_ge(g2[0].sem, g2[1])
            inst = e.collective_compute(
                "AllReduce", ALU.max,
                replica_groups=[[0, 1, 2, 3, 4, 5, 6, 7]],
                ins=[ccm_in], outs=[ccm_out])
            inst.then_inc(ccc.sem, 1)
        streams["gpsimd"].append(fcc1)
        g3 = dma("gpsimd", SC2[0:1, 9:10], ccm_out, wait=[cc1], ch="msc")

        def fxs(e):
            e.tensor_scalar(out=SC2[0:1, 10:11], in0=SC2[0:1, 9:10],
                            scalar1=1.0 / 448.0, scalar2=1e-6,
                            op0=ALU.mult, op1=ALU.max)
            e.drain()
            e.reciprocal(SC2[0:1, 13:14], SC2[0:1, 10:11])
            e.drain()
            e.tensor_scalar(out=SC2[0:1, 11:12], in0=SC2[0:1, 13:14],
                            scalar1=0.5, scalar2=None, op0=ALU.mult)
            e.drain()
            e.tensor_mul(SC2[0:1, 12:13], SC2[0:1, 10:11], SC2[0:1, 2:3])
            e.drain()
            return e.tensor_scalar(out=SC2[0:1, 12:13], in0=SC2[0:1, 12:13],
                                   scalar1=4.0, scalar2=None, op0=ALU.mult)
        d_xs = op("vector", fxs, wait=[g3])
        d_sc = bcast_scalars_f32([11, 4, 5, 12], 4, d_xs)
        barrier()
        if phase_done(3.3):
            return

        d_q = DV.v
        for c in range(CH):
            t_x = dma("sync", XCH[:], xview(src_hbm, c), wait=[(DV, d_q)],
                      ch="xch")

            def f(e, c=c):
                e.tensor_mul(XC2[:], XCH[:], PS[:, 0:1024])
                e.drain()
                e.tensor_add(XC2[:], XC2[:], PS[:, 1024:2048])
                e.drain()
                return e.tensor_scalar(out=XQ8[:, c * T:(c + 1) * T],
                                       in0=XC2[:], scalar1=SCL[:, 4:5],
                                       scalar2=None, op0=ALU.mult)
            d_q = op("vector", f, wait=[t_x, (DV, d_sc)])
        barrier()
        if phase_done(3.4):
            return

        first_h = True
        te_mm_end = [0, 0]
        h_tok = [None, None]
        hst_last = [None]
        for os_i in range(16):
            prep_toks = []
            w8_tok = [DV.v, DV.v]
            for g in range(16):
                t_w = dma("sync", WRAW[:, (g % 2) * 512:(g % 2) * 512 + 512],
                          wfcT[g * 128:(g + 1) * 128,
                               os_i * 512:os_i * 512 + 512],
                          wait=[(DV, w8_tok[g % 2])], ch="wr%d" % (g % 2))
                tok = op("vector", lambda e, g=g, os_i=os_i: e.tensor_scalar(
                    out=W8[:, ((os_i % 2) * 16 + g) * 512:
                           ((os_i % 2) * 16 + g) * 512 + 512],
                    in0=WRAW[:, (g % 2) * 512:(g % 2) * 512 + 512],
                    scalar1=SCL[:, 5:6], scalar2=None, op0=ALU.mult),
                    wait=[t_w] +
                         ([(TE, te_mm_end[os_i % 2])] if g == 0 else []))
                w8_tok[g % 2] = tok
                prep_toks.append(tok)
            prep_done = prep_toks[-1]
            for o4 in range(4):
                oc = os_i * 4 + o4
                pb = (o4 % 2) * 1024
                for hf in range(2):
                    for g in range(CH):
                        wl = []
                        if g == 0 and hf == 0:
                            if o4 == 0:
                                wl = [(DV, prep_done)]
                            if h_tok[o4 % 2]:
                                wl = wl + [h_tok[o4 % 2]]
                        op("tensor", lambda e, g=g, o4=o4, hf=hf, pb=pb, os_i=os_i: e.matmul(
                            PS[:, pb + hf * 512:pb + hf * 512 + 512],
                            W8[:, ((os_i % 2) * 16 + g) * 512 + o4 * 128:
                               ((os_i % 2) * 16 + g) * 512 + o4 * 128 + 128],
                            XQ8[:, g * T + hf * 512:g * T + hf * 512 + 512],
                            start=(g == 0), stop=(g == CH - 1)), wait=wl)
                te_mm = TE.v
                s_r = op("scalar", lambda e, pb=pb: e.activation(
                    out=XC2[:], in_=PS[:, pb:pb + 1024], func=AF.Relu,
                    bias=0.0, scale=SCL[:, 7:8]),
                    wait=[(TE, te_mm), (DV, DV.v)])

                def fh(e, pb=pb, first_h=first_h):
                    e.scalar_tensor_tensor(out=XCH[:], in0=PS[:, pb:pb + 1024],
                                           scalar=SCL[:, 7:8], in1=XC2[:],
                                           op0=ALU.mult, op1=ALU.mult)
                    e.drain()
                    e.tensor_reduce(out=SCL[:, 10:11], in_=XCH[:],
                                    axis=mybir.AxisListType.X, op=ALU.max)
                    e.drain()
                    if first_h:
                        return e.tensor_copy(SCL[:, 14:15], SCL[:, 10:11])
                    return e.tensor_tensor(SCL[:, 14:15], SCL[:, 14:15],
                                           SCL[:, 10:11], op=ALU.max)
                d_h = op("vector", fh, wait=[(SC, s_r)] +
                         ([hst_last[0]] if hst_last[0] else []))
                first_h = False
                h_tok[o4 % 2] = (DV, d_h)
                hst_last[0] = dma("gpsimd",
                                  h_dram.rearrange("(ch p) t -> ch p t", p=128)[oc],
                                  XCH[:], wait=[(DV, d_h)], ch="hst")
            te_mm_end[os_i % 2] = TE.v
        barrier()
        if phase_done(3.5):
            return

        d_hm = rowmax_to_scalar(14, 16, DV.v)
        g2 = dma("gpsimd", cch_in, SC2[0:1, 16:17], wait=[(DV, d_hm)], ch="msc")
        ccc = chan("cc")
        cc2 = (ccc, ccc.bump(1))

        def fcc2(e, g2=g2):
            e.wait_ge(g2[0].sem, g2[1])
            inst = e.collective_compute(
                "AllReduce", ALU.max,
                replica_groups=[[0, 1, 2, 3, 4, 5, 6, 7]],
                ins=[cch_in], outs=[cch_out])
            inst.then_inc(ccc.sem, 1)
        streams["gpsimd"].append(fcc2)
        g3 = dma("gpsimd", SC2[0:1, 17:18], cch_out, wait=[cc2], ch="msc")

        def fhs(e):
            e.tensor_scalar(out=SC2[0:1, 18:19], in0=SC2[0:1, 17:18],
                            scalar1=1.0 / 448.0, scalar2=1e-6,
                            op0=ALU.mult, op1=ALU.max)
            e.drain()
            e.reciprocal(SC2[0:1, 21:22], SC2[0:1, 18:19])
            e.drain()
            e.tensor_scalar(out=SC2[0:1, 19:20], in0=SC2[0:1, 21:22],
                            scalar1=0.5, scalar2=None, op0=ALU.mult)
            e.drain()
            e.tensor_mul(SC2[0:1, 20:21], SC2[0:1, 18:19], SC2[0:1, 3:4])
            e.drain()
            return e.tensor_scalar(out=SC2[0:1, 20:21], in0=SC2[0:1, 20:21],
                                   scalar1=4.0, scalar2=None, op0=ALU.mult)
        d_hs = op("vector", fhs, wait=[g3])
        d_sc2 = bcast_scalars_f32([19, 20], 8, d_hs)
        barrier()
        if phase_done(3.6):
            return

        d_q = DV.v
        for c in range(64):
            t_h = dma("sync", XCH[:],
                      h_dram.rearrange("(ch p) t -> ch p t", p=128)[c],
                      wait=[(DV, d_q)], ch="xch")
            d_q = op("vector", lambda e, c=c: e.tensor_scalar(
                out=H8[:, c * T:(c + 1) * T], in0=XCH[:],
                scalar1=SCL[:, 8:9], scalar2=None, op0=ALU.mult),
                wait=[t_h, (DV, d_sc2)])
        barrier()
        if phase_done(3.7):
            return

        te_mm_last = 0
        ev_last = [None, None]
        ust_last = [None]
        for os_i in range(8):
            prep_toks = []
            w8_tok = [DV.v] * 4
            for g in range(64):
                slot = g % 4
                t_w = dma("sync", WRAW[:, slot * 256:slot * 256 + 256],
                          wprT[g * 128:(g + 1) * 128,
                               os_i * 256:os_i * 256 + 256],
                          wait=[(DV, w8_tok[slot])], ch="wr%d" % slot)
                tok = op("vector", lambda e, g=g, slot=slot: e.tensor_scalar(
                    out=W8[:, g * 256:g * 256 + 256],
                    in0=WRAW[:, slot * 256:slot * 256 + 256],
                    scalar1=SCL[:, 6:7], scalar2=None, op0=ALU.mult),
                    wait=[t_w] +
                         ([(TE, te_mm_last)] if g == 0 else []))
                w8_tok[slot] = tok
                prep_toks.append(tok)
            prep_done = prep_toks[-1]
            for oc2 in range(2):
                oc = os_i * 2 + oc2
                pb = oc2 * 1024
                for hf in range(2):
                    for g in range(64):
                        wl = []
                        if g == 0 and hf == 0:
                            wl = [(DV, prep_done)]
                            if ev_last[oc2]:
                                wl = wl + [ev_last[oc2]]
                        op("tensor", lambda e, g=g, oc2=oc2, hf=hf, pb=pb: e.matmul(
                            PS[:, pb + hf * 512:pb + hf * 512 + 512],
                            W8[:, g * 256 + oc2 * 128:g * 256 + oc2 * 128 + 128],
                            H8[:, g * T + hf * 512:g * T + hf * 512 + 512],
                            start=(g == 0), stop=(g == 63)), wait=wl)
                te_mm = TE.v
                d_ev = op("vector", lambda e, pb=pb: e.tensor_scalar(
                    out=PT[:], in0=PS[:, pb:pb + 1024],
                    scalar1=SCL[:, 9:10], scalar2=None, op0=ALU.mult),
                    wait=[(TE, te_mm)] + ([ust_last[0]] if ust_last[0] else []))
                ust_last[0] = dma("gpsimd", xview(u_dram, oc), PT[:],
                                  wait=[(DV, d_ev)], ch="ust")
                ev_last[oc2] = (DV, d_ev)
            te_mm_last = TE.v
        barrier()
        if phase_done(3.8):
            return

        te_st2 = ln_stats_udram(u_dram)
        te_bc2 = ln_smalls(te_st2)
        ln2_residual(None, src_hbm, dst_hbm, te_bc2, u_hbm=u_dram)
        barrier()

    blocks = [block_l0_attn, block_l0_ffn, block_l1_attn, block_l1_ffn]
    srcs = [x0, xs, xs, xs]
    dsts = [xs, xs, xs, out_t]
    dsts[N_BLOCKS - 1] = out_t
    for i in range(N_BLOCKS):
        blocks[i](srcs[i], dsts[i])

    final_toks = [(c.sem, c.v) for c in channels.values() if c.v > 0]

    def _final(e, wl=final_toks):
        for (s, v) in wl:
            e.wait_ge(s, v)
    streams["gpsimd"].append(_final)

    with es:
        with nc.Block() as block:
            @block.sync
            def _(e):
                for f in streams["sync"]:
                    f(e)

            @block.tensor
            def _(e):
                for f in streams["tensor"]:
                    f(e)

            @block.vector
            def _(e):
                for f in streams["vector"]:
                    f(e)

            @block.scalar
            def _(e):
                for f in streams["scalar"]:
                    f(e)

            @block.gpsimd
            def _(e):
                for f in streams["gpsimd"]:
                    f(e)
    return nc


def _host_layout(inputs):
    x = np.ascontiguousarray(inputs["L0_attn_x"])
    wqkvT = np.ascontiguousarray(inputs["L0_attn_w_qkv"].T)
    wprojT = np.ascontiguousarray(inputs["L0_attn_w_proj"].T)
    qgc = np.ascontiguousarray(
        (inputs["L0_attn_q_gain"].astype(np.float32) *
         np.float32(INV_SQRT_HD)).reshape(16, 1))

    def u8T(a):
        v = np.ascontiguousarray(a).view(np.uint8).reshape(
            a.shape[0], a.shape[1], 4)[:, :, 0]
        return np.ascontiguousarray(v.T)
    wqgT = u8T(inputs["L0_ffn_wq_gate"])
    wquT = u8T(inputs["L0_ffn_wq_up"])
    wqdT = u8T(inputs["L0_ffn_wq_down"])
    sg_c = np.ascontiguousarray(inputs["L0_ffn_s_gate"].reshape(44, 128).T)
    su_c = np.ascontiguousarray(inputs["L0_ffn_s_up"].reshape(44, 128).T)
    sd_c = np.ascontiguousarray(inputs["L0_ffn_s_down"].reshape(16, 128).T)
    l1wT = np.ascontiguousarray(inputs["L1_attn_Wqkv_w"].T)
    l1bias = np.ascontiguousarray(inputs["L1_attn_Wqkv_b"].reshape(18, 128).T)
    l1owT = np.ascontiguousarray(inputs["L1_attn_out_w"].T)
    l1ob = np.ascontiguousarray(inputs["L1_attn_out_b"].reshape(16, 128).T)
    wfcT = np.ascontiguousarray(inputs["L1_ffn_w_fc"].T)
    wprT = np.ascontiguousarray(inputs["L1_ffn_w_proj"].T)

    ident = np.eye(128, dtype=np.float32).astype(bf)
    aone = np.zeros((128, 64), dtype=np.float32)
    aone[:, 32] = 1.0
    aone = aone.astype(bf)
    onesrow = np.zeros((65, 128), dtype=np.float32)
    onesrow[[0, 32, 64], :] = 1.0
    onesrow = onesrow.astype(bf)
    negrow = np.zeros((65, 128), dtype=np.float32)
    negrow[[0, 32, 64], :] = -1.0
    negrow = negrow.astype(bf)
    ceps = np.zeros((128, 2), dtype=np.float32)
    ceps[:, 0] = LN_EPS
    ceps[:, 1] = RMS_EPS

    inv = 1.0 / (10000.0 ** (np.arange(0, 128, 2, dtype=np.float64) / 128.0))
    in_maps, pos_list = [], []
    for core in range(8):
        b, p = core // 2, core % 2
        pos = np.concatenate([np.arange(128) + (2 * j + p) * 128
                              for j in range(8)])
        pos_list.append(pos)
        xc = np.ascontiguousarray(x[b][pos].T)
        fr = np.outer(pos.astype(np.float64), inv)
        c64 = np.cos(fr).T.astype(np.float32)
        s64 = np.sin(fr).T.astype(np.float32)
        cosx = np.ascontiguousarray(
            np.concatenate([c64, c64], axis=0)).astype(bf)
        sinx = np.ascontiguousarray(
            np.concatenate([s64, -s64], axis=0)).astype(bf)
        lens = np.zeros((128, 8), dtype=np.float32)
        for j in range(8):
            lens[:, j] = (2 * j + p) * 128 + np.arange(128) + 1
        in_maps.append(dict(
            x0=xc, wqkvT=wqkvT, wprojT=wprojT, qgc=qgc,
            wqgT=wqgT, wquT=wquT, wqdT=wqdT,
            sg_c=sg_c, su_c=su_c, sd_c=sd_c,
            l1wT=l1wT, l1bias=l1bias, l1owT=l1owT, l1ob=l1ob,
            wfcT=wfcT, wprT=wprT,
            cosx=cosx, sinx=sinx, lens_in=lens,
            ident_in=ident, aone_in=aone, onesrow_in=onesrow,
            negrow_in=negrow, ceps_in=ceps,
        ))
    return in_maps, pos_list


def kernel(**inputs):
    in_maps, pos_list = _host_layout(inputs)
    nc = build_program()
    lower_extended_insts(nc)
    res = run_bass_kernel_spmd(nc, in_maps, core_ids=list(range(8)))
    out = np.empty((4, 2048, 2048), dtype=np.float32)
    for core in range(8):
        b = core // 2
        out[b, pos_list[core], :] = np.asarray(res.results[core]["out"]).T
    return out


# version 05
# speedup vs baseline: 1.1290x; 1.0142x over previous
import numpy as np
import ml_dtypes
import concourse.bass as bass
import concourse.mybir as mybir
from concourse.bass_utils import run_bass_kernel_spmd
from concourse.library_overlay import lower_extended_insts
from contextlib import ExitStack

F32 = mybir.dt.float32
BF16 = mybir.dt.bfloat16
U8 = mybir.dt.uint8
FP8 = mybir.dt.float8e4
AF = mybir.ActivationFunctionType
ALU = mybir.AluOpType
bf = ml_dtypes.bfloat16

T = 1024
CH = 16
D = 2048
RMS_EPS = 1.1920929e-07
LN_EPS = 1e-5
CROUND = 12582912.0
POSBIG = 3.4028235e38
INV_SQRT_HD = 0.08838834764831845

N_BLOCKS = 4
PHASE_LIMIT = 99
ENGS = ("sync", "tensor", "vector", "scalar", "gpsimd")


class Ctr:
    def __init__(self, nc, es, name):
        self.sem = es.enter_context(nc.semaphore(name))
        self.v = 0

    def bump(self, n=1):
        self.v += n
        return self.v


def _bank_splits(lo, hi):
    outl = []
    c = lo
    while c < hi:
        nxt = min((c // 512 + 1) * 512, hi)
        outl.append((c, nxt))
        c = nxt
    return outl


def build_program():
    nc = bass.Bass("TRN2", target_bir_lowering=False)
    es = ExitStack()
    streams = {e: [] for e in ENGS}

    dram = lambda n, s, d, k="ExternalInput": nc.dram_tensor(n, s, d, kind=k).ap()
    x0 = dram("x0", [D, T], F32)
    wqkvT = dram("wqkvT", [D, 3072], F32)
    wprojT = dram("wprojT", [D, D], F32)
    qgc = dram("qgc", [16, 1], F32)
    wqgT = dram("wqgT", [D, 5632], U8)
    wquT = dram("wquT", [D, 5632], U8)
    wqdT = dram("wqdT", [5632, D], U8)
    sg_c = dram("sg_c", [128, 44], F32)
    su_c = dram("su_c", [128, 44], F32)
    sd_c = dram("sd_c", [128, 16], F32)
    l1wT = dram("l1wT", [D, 2304], F32)
    l1bias = dram("l1bias", [128, 18], F32)
    l1owT = dram("l1owT", [D, D], F32)
    l1ob = dram("l1ob", [128, 16], F32)
    wfcT = dram("wfcT", [D, 8192], F32)
    wprT = dram("wprT", [8192, D], F32)
    cosx = dram("cosx", [128, T], BF16)
    sinx = dram("sinx", [128, T], BF16)
    lens_in = dram("lens_in", [128, 8], F32)
    lens2_in = dram("lens2_in", [128, 8], F32)
    ident_in = dram("ident_in", [128, 128], BF16)
    aone_in = dram("aone_in", [128, 64], BF16)
    onesrow_in = dram("onesrow_in", [65, 128], BF16)
    negrow_in = dram("negrow_in", [65, 128], BF16)
    ceps_in = dram("ceps_in", [128, 2], F32)

    out_t = dram("out", [D, T], F32, "ExternalOutput")
    xs = dram("xs", [D, T], F32, "Internal")
    u_dram = dram("u_dram", [D, T], BF16, "Internal")
    h_dram = dram("h_dram", [8192, T], F32, "Internal")
    scr128 = dram("scr128", [128, 1], F32, "Internal")
    ccA_in = dram("ccA_in", [1, 1024 * 1024], BF16, "Internal")
    ccA_out = dram("ccA_out", [1, 2 * 1024 * 1024], BF16, "Internal")
    ccB_in = dram("ccB_in", [1, 2 * 128 * 1024], BF16, "Internal")
    ccB_out = dram("ccB_out", [1, 4 * 128 * 1024], BF16, "Internal")
    ccm_in = dram("ccm_in", [1, 1], F32, "Internal")
    ccm_out = dram("ccm_out", [1, 1], F32, "Internal")
    cch_in = dram("cch_in", [1, 1], F32, "Internal")
    cch_out = dram("cch_out", [1, 1], F32, "Internal")

    def xview(t, c):
        return t.rearrange("(ch p) t -> ch p t", p=128)[c]

    sbuf = lambda n, s, d: es.enter_context(nc.sbuf_tensor(n, s, d))
    IDENT = sbuf("IDENT", [128, 128], BF16)
    AONE = sbuf("AONE", [128, 64], BF16)
    ONESROW = sbuf("ONESROW", [65, 128], BF16)
    NEGROW = sbuf("NEGROW", [65, 128], BF16)
    CEPS = sbuf("CEPS", [128, 2], F32)
    COSX = sbuf("COSX", [128, T], BF16)
    SINX = sbuf("SINX", [128, T], BF16)
    LENS = sbuf("LENS", [128, 8], F32)
    LENS2 = sbuf("LENS2", [128, 8], F32)
    QGC = sbuf("QGC", [16, 1], F32)
    SGC = sbuf("SGC", [128, 44], F32)
    SUC = sbuf("SUC", [128, 44], F32)
    SDC = sbuf("SDC", [128, 16], F32)
    L1BI = sbuf("L1BI", [128, 18], F32)
    L1OB = sbuf("L1OB", [128, 16], F32)
    WRAW = sbuf("WRAW", [128, 1024], F32)
    WU8 = sbuf("WU8", [128, 2048], U8)
    WBF = sbuf("WBF", [128, 16384], BF16)
    SQ = sbuf("SQ", [128, 2048], BF16)
    XCH = sbuf("XCH", [128, 1024], F32)
    XC2 = sbuf("XC2", [128, 1024], F32)
    ROWS = sbuf("ROWS", [128, 3072], F32)
    BROWS = sbuf("BROWS", [128, 2048], BF16)
    SC2 = sbuf("SC2", [128, 32], F32)
    RMAX = sbuf("RMAX", [1, 256], F32)
    TERN = sbuf("TERN", [16, 512], F32)
    TERNB = sbuf("TERNB", [16, 1536], BF16)
    VW = sbuf("VW", [128, 1024], BF16)
    RT = sbuf("RT", [128, 2048], BF16)
    SMF = sbuf("SMF", [128, 2048], F32)
    XSLOT2 = None
    PT = sbuf("PT", [128, 1024], BF16)
    SW = sbuf("SW", [128, 1024], BF16)
    SCL = sbuf("SCL", [128, 16], F32)
    BIG = sbuf("BIG", [128, 57344], BF16)

    W8 = WBF[:, 0:8192].bitcast(FP8)
    XN = BIG[:, 0:16384]
    QB = BIG[:, 16384:32768]
    KG = BIG[:, 32768:40960]
    VG = BIG[:, 40960:49152]
    PPB = BIG[:, 49152:57344]
    XQ8 = BIG[:, 0:8192].bitcast(FP8)
    H8 = BIG[:, 8192:40960].bitcast(FP8)

    XSLOT = [XCH, None]

    def tcol(hc):
        if hc < 40:
            return BIG, 16384 + hc * 1024
        if hc < 42:
            return SQ, (hc - 40) * 1024
        return RT, (hc - 42) * 1024

    PS = es.enter_context(nc.psum_tensor("PS", [128, 4096], F32))

    def xsl(c):
        return XCH[:] if c % 2 == 0 else SMF[:, 1024:2048]

    def usl(c):
        return SW[:] if c % 2 == 0 else RT[:, 0:1024]

    SY = Ctr(nc, es, "SY")
    GD = Ctr(nc, es, "GD")
    TE = Ctr(nc, es, "TE")
    DV = Ctr(nc, es, "DV")
    SC = Ctr(nc, es, "SC")
    CTRS = {"sync": SY, "gpsimd": GD, "tensor": TE, "vector": DV, "scalar": SC}

    channels = {}

    def chan(name):
        if name not in channels:
            channels[name] = Ctr(nc, es, "ch_" + name)
        return channels[name]

    def dma(queue, dst, src, wait=(), ch="g"):
        ctr = chan(ch)
        tok = ctr.bump(16)
        wl = [(c.sem, v) for (c, v) in wait if v > 0]

        def f(e, dst=dst, src=src, wl=wl):
            for (s, v) in wl:
                e.wait_ge(s, v)
            e.dma_start(out=dst, in_=src).then_inc(ctr.sem, 16)
        streams[queue].append(f)
        return (ctr, tok)

    def op(eng, fn, wait=(), inc=1):
        ctr = CTRS[eng]
        tok = ctr.bump(inc) if inc else ctr.v
        wl = [(c.sem, v) for (c, v) in wait if v > 0]

        def f(e, fn=fn, wl=wl, inc=inc):
            for (s, v) in wl:
                e.wait_ge(s, v)
            inst = fn(e)
            if inc:
                inst.then_inc(ctr.sem, inc)
        streams[eng].append(f)
        return tok

    def barrier():
        allc = list(CTRS.items()) + [("ch_" + k, c) for k, c in channels.items()]
        toks = [(c.sem, c.v) for _, c in allc if c.v > 0]
        for eng in ENGS:
            wl = [(s, v) for (s, v) in toks if s is not CTRS[eng].sem]

            def f(e, wl=wl):
                for (s, v) in wl:
                    e.wait_ge(s, v)
            streams[eng].append(f)

    dma("sync", IDENT[:], ident_in, ch="con")
    dma("sync", AONE[:], aone_in, ch="con")
    dma("sync", ONESROW[:], onesrow_in, ch="con")
    dma("sync", NEGROW[:], negrow_in, ch="con")
    dma("sync", CEPS[:], ceps_in, ch="con")
    dma("sync", COSX[:], cosx, ch="con")
    dma("sync", SINX[:], sinx, ch="con")
    dma("sync", LENS[:], lens_in, ch="con")
    dma("sync", LENS2[:], lens2_in, ch="con")
    dma("sync", QGC[:], qgc, ch="con")
    dma("sync", SGC[:], sg_c, ch="con")
    dma("sync", SUC[:], su_c, ch="con")
    dma("sync", SDC[:], sd_c, ch="con")
    dma("sync", L1BI[:], l1bias, ch="con")
    dma("sync", L1OB[:], l1ob, ch="con")
    barrier()

    def ln_stats_hbm(src_hbm, xb_dst):
        sq_te = [0, 0]
        sc_slot = [SC.v, SC.v]
        for c in range(CH):
            t_x = dma("sync", xsl(c), xview(src_hbm, c),
                      wait=[(SC, sc_slot[c % 2])], ch="xch%d" % (c % 2))
            sc_tok = op("scalar", lambda e, c=c: e.copy(
                xb_dst[:, c * T:(c + 1) * T], xsl(c)), wait=[t_x])
            sc_slot[c % 2] = sc_tok
            dv_tok = op("vector", lambda e, c=c: e.tensor_mul(
                SQ[:, (c % 2) * 1024:(c % 2) * 1024 + 1024],
                xb_dst[:, c * T:(c + 1) * T], xb_dst[:, c * T:(c + 1) * T]),
                wait=[(SC, sc_tok), (TE, sq_te[c % 2])])
            last_dv = dv_tok
            for hf in range(2):
                lo = 3072 + hf * 512
                op("tensor", lambda e, c=c, hf=hf, lo=lo: e.matmul(
                    PS[0:1, lo:lo + 512], AONE[:, 32:33],
                    xb_dst[:, c * T + hf * 512:c * T + hf * 512 + 512],
                    start=(c == 0), stop=(c == CH - 1)), wait=[(SC, sc_tok)])
                op("tensor", lambda e, c=c, hf=hf, lo=lo: e.matmul(
                    PS[32:33, lo:lo + 512], AONE[:, 32:33],
                    SQ[:, (c % 2) * 1024 + hf * 512:(c % 2) * 1024 + hf * 512 + 512],
                    start=(c == 0), stop=(c == CH - 1)), wait=[(DV, dv_tok)])
            sq_te[c % 2] = TE.v
        return TE.v

    def ln_stats_sbuf(cols):
        sq_te = [0, 0]
        for c in range(CH):
            tns, off = cols(c)
            dv_tok = op("vector", lambda e, tns=tns, off=off, c=c: e.tensor_mul(
                SQ[:, (c % 2) * 1024:(c % 2) * 1024 + 1024],
                tns[:, off:off + T], tns[:, off:off + T]),
                wait=[(TE, sq_te[c % 2])])
            for hf in range(2):
                lo = 3072 + hf * 512
                op("tensor", lambda e, tns=tns, off=off, c=c, hf=hf, lo=lo: e.matmul(
                    PS[0:1, lo:lo + 512], AONE[:, 32:33],
                    tns[:, off + hf * 512:off + hf * 512 + 512],
                    start=(c == 0), stop=(c == CH - 1)))
                op("tensor", lambda e, c=c, hf=hf, lo=lo: e.matmul(
                    PS[32:33, lo:lo + 512], AONE[:, 32:33],
                    SQ[:, (c % 2) * 1024 + hf * 512:(c % 2) * 1024 + hf * 512 + 512],
                    start=(c == 0), stop=(c == CH - 1)), wait=[(DV, dv_tok)])
            sq_te[c % 2] = TE.v
        return TE.v

    def ln_stats_udram(src_hbm_u):
        te_slot = [TE.v, TE.v]
        for c in range(CH):
            t_u = dma("sync", usl(c), xview(src_hbm_u, c),
                      wait=[(TE, te_slot[c % 2])], ch="sw%d" % (c % 2))
            dv_sq = op("vector", lambda e, c=c: e.tensor_mul(
                SQ[:, (c % 2) * 1024:(c % 2) * 1024 + 1024], usl(c), usl(c)),
                wait=[t_u])
            for hf in range(2):
                lo = 3072 + hf * 512
                op("tensor", lambda e, c=c, hf=hf, lo=lo: e.matmul(
                    PS[0:1, lo:lo + 512], AONE[:, 32:33],
                    usl(c)[:, hf * 512:hf * 512 + 512],
                    start=(c == 0), stop=(c == CH - 1)), wait=[t_u])
                op("tensor", lambda e, c=c, hf=hf, lo=lo: e.matmul(
                    PS[32:33, lo:lo + 512], AONE[:, 32:33],
                    SQ[:, (c % 2) * 1024 + hf * 512:(c % 2) * 1024 + hf * 512 + 512],
                    start=(c == 0), stop=(c == CH - 1)), wait=[(DV, dv_sq)])
            te_slot[c % 2] = TE.v
        return TE.v

    def ln_smalls(te_stats, fp32_bcast=False):
        def f1(e):
            e.tensor_scalar(out=ROWS[0:1, 0:1024], in0=PS[0:1, 3072:4096],
                            scalar1=1.0 / D, scalar2=None, op0=ALU.mult)
            e.tensor_scalar(out=ROWS[0:1, 1024:2048], in0=PS[32:33, 3072:4096],
                            scalar1=1.0 / D, scalar2=None, op0=ALU.mult)
            e.drain()
            e.tensor_mul(ROWS[0:1, 2048:3072], ROWS[0:1, 0:1024],
                         ROWS[0:1, 0:1024])
            e.drain()
            return e.tensor_sub(ROWS[32:33, 0:1024], ROWS[0:1, 1024:2048],
                                ROWS[0:1, 2048:3072])
        d1 = op("vector", f1, wait=[(TE, te_stats)])
        s1 = op("scalar", lambda e: e.activation(
            out=ROWS[32:33, 1024:2048], in_=ROWS[32:33, 0:1024], func=AF.Sqrt,
            bias=CEPS[32:33, 0:1], scale=1.0), wait=[(DV, d1)])

        def f2(e):
            e.reciprocal(ROWS[0:1, 2048:3072], ROWS[32:33, 1024:2048])
            e.drain()
            e.tensor_mul(ROWS[32:33, 0:1024], ROWS[0:1, 0:1024],
                         ROWS[0:1, 2048:3072])
            e.drain()
            e.tensor_copy(BROWS[0:1, 0:1024], ROWS[0:1, 2048:3072])
            e.tensor_copy(BROWS[32:33, 0:1024], ROWS[32:33, 0:1024])
            if fp32_bcast:
                e.drain()
                e.tensor_copy(ROWS[0:1, 1024:2048], ROWS[32:33, 0:1024])
                e.tensor_copy(SMF[0:1, 0:2048].bitcast(BF16)[:, 0:1024],
                              ROWS[32:33, 0:1024])
                e.drain()
                e.tensor_sub(ROWS[64:65, 0:1024], ROWS[0:1, 2048:3072],
                             BROWS[0:1, 0:1024])
                e.tensor_sub(ROWS[64:65, 1024:2048], ROWS[0:1, 1024:2048],
                             SMF[0:1, 0:2048].bitcast(BF16)[:, 0:1024])
                e.drain()
                e.tensor_copy(BROWS[0:1, 1024:2048], ROWS[64:65, 0:1024])
                return e.tensor_copy(
                    SMF[0:1, 0:2048].bitcast(BF16)[:, 1024:2048],
                    ROWS[64:65, 1024:2048])
            return e.engine_nop()
        d2 = op("vector", f2, wait=[(SC, s1)])
        for hf in range(2):
            sl = slice(hf * 512, hf * 512 + 512)
            sl2 = slice(1024 + hf * 512, 1024 + hf * 512 + 512)
            op("tensor", lambda e, sl=sl: e.matmul(
                PS[:, sl], ONESROW[0:1, :], BROWS[0:1, sl],
                start=True, stop=not fp32_bcast), wait=[(DV, d2)])
            if fp32_bcast:
                op("tensor", lambda e, sl=sl: e.matmul(
                    PS[:, sl], ONESROW[0:1, :],
                    BROWS[0:1, 1024 + sl.start:1024 + sl.start + 512],
                    start=False, stop=True))
            if not fp32_bcast:
                op("tensor", lambda e, sl=sl, sl2=sl2: e.matmul(
                    PS[:, sl2], NEGROW[32:33, :], BROWS[32:33, sl],
                    start=True, stop=True))
            else:
                op("tensor", lambda e, sl=sl, sl2=sl2: e.matmul(
                    PS[:, sl2], NEGROW[0:1, :],
                    SMF[0:1, 0:2048].bitcast(BF16)[:, sl],
                    start=True, stop=False))
                op("tensor", lambda e, sl=sl, sl2=sl2: e.matmul(
                    PS[:, sl2], NEGROW[0:1, :],
                    SMF[0:1, 0:2048].bitcast(BF16)[:, 1024 + sl.start:
                                                   1024 + sl.start + 512],
                    start=False, stop=True))
        return TE.v

    def ln_apply(src_hbm, te_bc, dst_bf):
        last = 0
        dv_slot = [DV.v, DV.v]
        sc_gate = SC.v
        for c in range(CH):
            t_x = dma("sync", xsl(c), xview(src_hbm, c),
                      wait=[(DV, dv_slot[c % 2])] +
                           ([(SC, sc_gate)] if c < 2 else []),
                      ch="xch%d" % (c % 2))

            def f(e, c=c):
                e.tensor_mul(XC2[:], xsl(c), PS[:, 0:1024])
                e.drain()
                return e.tensor_add(dst_bf[:, c * T:(c + 1) * T],
                                    XC2[:], PS[:, 1024:2048])
            last = op("vector", f, wait=[t_x, (TE, te_bc)])
            dv_slot[c % 2] = last
        return last

    def ln2_residual(u_cols, src_hbm, dst_hbm, te_bc, u_hbm=None):
        last_store = None
        dv_uslot = [DV.v, DV.v]
        dv_xslot = [DV.v, DV.v]
        te_gate = TE.v
        sc_gate = SC.v
        for c in range(CH):
            if u_hbm is not None:
                t_u = dma("sync", usl(c), xview(u_hbm, c),
                          wait=[(DV, dv_uslot[c % 2])] +
                               ([(TE, te_gate)] if c < 2 else []),
                          ch="sw%d" % (c % 2))
                usrc = usl(c)
                wl_extra = [t_u]
            else:
                tns, off = u_cols(c)
                usrc = tns[:, off:off + T]
                wl_extra = []
            t_x = dma("sync", xsl(c), xview(src_hbm, c),
                      wait=[(DV, dv_xslot[c % 2])] +
                           ([(SC, sc_gate)] if c < 2 else []),
                      ch="xch%d" % (c % 2))

            def f(e, usrc=usrc, c=c):
                e.tensor_mul(XC2[:], usrc, PS[:, 0:1024])
                e.drain()
                e.tensor_add(XC2[:], XC2[:], PS[:, 1024:2048])
                e.drain()
                return e.tensor_add(XC2[:], XC2[:], xsl(c))
            d = op("vector", f, wait=[t_x, (TE, te_bc)] + wl_extra +
                   ([last_store] if last_store else []))
            dv_uslot[c % 2] = d
            dv_xslot[c % 2] = d
            last_store = dma("gpsimd", xview(dst_hbm, c), XC2[:],
                             wait=[(DV, d)], ch="xst")
        return last_store

    def tern_prep(w_hbm, s_off, slot0, mean_free_dv=0, wbf_free_te=0):
        stop_at = PHASE_LIMIT
        """Ternary prep of a 512-col superchunk into WBF slots slot0..+15.
        mean_free_dv: PS bank4 free (prev superchunk's v-ops done).
        wbf_free_te: WBF slots free (matmuls of 2-ago superchunk done)."""
        sc_toks = []
        te_means = [TE.v, TE.v]
        sc_gate = SC.v
        for g in range(16):
            t_w = dma("sync", WRAW[:, (g % 2) * 512:(g % 2) * 512 + 512],
                      w_hbm[g * 128:(g + 1) * 128, s_off:s_off + 512],
                      wait=[(SC, sc_toks[g - 2] if g >= 2 else sc_gate)],
                      ch="wr%d" % (g % 2))

            def fc(e, g=g):
                sl = slice((g % 2) * 512, (g % 2) * 512 + 512)
                e.copy(WBF[:, (slot0 + g) * 512:(slot0 + g) * 512 + 512],
                       WRAW[:, sl])
                return e.activation(out=SW[:, (g % 2) * 512:(g % 2) * 512 + 512],
                                    in_=WRAW[:, sl], func=AF.Abs)
            st = op("scalar", fc,
                    wait=[t_w, (TE, te_means[g % 2])] +
                         ([(TE, wbf_free_te)] if g == 0 else []))
            sc_toks.append(st)
            op("tensor", lambda e, g=g: e.matmul(
                PS[0:16, 2048:2560], AONE[:, 32 - g:48 - g],
                SW[:, (g % 2) * 512:(g % 2) * 512 + 512],
                start=(g == 0), stop=(g == 15)),
                wait=[(SC, st)] + ([(DV, mean_free_dv)] if g == 0 else []))
            te_means[g % 2] = TE.v

        def fsm(e):
            e.tensor_scalar(out=TERN[0:16, 0:512], in0=PS[0:16, 2048:2560],
                            scalar1=1.0 / 128.0, scalar2=None, op0=ALU.mult)
            e.drain()
            e.tensor_copy(TERNB[0:16, 0:512], TERN[0:16, 0:512])
            e.drain()
            e.tensor_scalar(out=TERNB[0:16, 0:512], in0=TERNB[0:16, 0:512],
                            scalar1=9.99999993922529e-09, scalar2=None,
                            op0=ALU.max)
            e.drain()
            e.reciprocal(TERN[0:16, 0:512], TERNB[0:16, 0:512])
            e.drain()
            e.tensor_copy(TERNB[0:16, 512:1024], TERN[0:16, 0:512])
            e.drain()
            e.tensor_sub(ROWS[32:48, 0:512], TERN[0:16, 0:512],
                         TERNB[0:16, 512:1024])
            e.drain()
            return e.tensor_copy(TERNB[0:16, 1024:1536], ROWS[32:48, 0:512])
        if stop_at <= 1.1:
            return SC.v and DV.v
        dsm = op("vector", fsm, wait=[(TE, TE.v)])
        if stop_at <= 1.15:
            return dsm

        dv_last = dsm
        stg_te = TE.v
        for g in range(16):
            gs = dma("gpsimd", BROWS[0:1, 0:512],
                     TERNB[g:g + 1, 0:512],
                     wait=[(DV, dsm), (TE, stg_te)], ch="stg0")
            gs = dma("gpsimd", BROWS[0:1, 512:1024],
                     TERNB[g:g + 1, 512:1024], ch="stg0")
            gs = dma("gpsimd", BROWS[0:1, 1024:1536],
                     TERNB[g:g + 1, 1024:1536], ch="stg0")
            if stop_at <= 1.18:
                continue
            op("tensor", lambda e: e.matmul(
                PS[:, 2048:2560], ONESROW[0:1, :], BROWS[0:1, 512:1024],
                start=True, stop=False), wait=[gs, (DV, dv_last)])
            op("tensor", lambda e: e.matmul(
                PS[:, 2048:2560], ONESROW[0:1, :], BROWS[0:1, 1024:1536],
                start=False, stop=True))
            op("tensor", lambda e: e.matmul(
                PS[:, 2560:3072], ONESROW[0:1, :], BROWS[0:1, 0:512],
                start=True, stop=True))
            stg_te = TE.v
            if stop_at <= 1.19:
                continue
            te_bc = TE.v

            def fv(e, g=g):
                wsl = slice((slot0 + g) * 512, (slot0 + g) * 512 + 512)
                e.tensor_mul(VW[:, 0:512], WBF[:, wsl], PS[:, 2048:2560])
                e.drain()
                e.tensor_scalar(out=VW[:, 512:1024], in0=VW[:, 0:512],
                                scalar1=-1.25, scalar2=1.25,
                                op0=ALU.max, op1=ALU.min)
                e.drain()
                e.tensor_scalar(out=VW[:, 0:512], in0=VW[:, 512:1024],
                                scalar1=CROUND, scalar2=CROUND,
                                op0=ALU.add, op1=ALU.subtract)
                e.drain()
                return e.tensor_mul(WBF[:, wsl], VW[:, 0:512],
                                    PS[:, 2560:3072])
            dv_last = op("vector", fv, wait=[(TE, te_bc)])
        if stop_at <= 1.19:
            return dsm
        return dv_last

    def cast_prep(w_hbm, s_off, slot0, ncols=512, wbf_free_te=0):
        toks = [SC.v, SC.v]
        for g in range(16):
            t_w = dma("sync", WRAW[:, (g % 2) * 512:(g % 2) * 512 + ncols],
                      w_hbm[g * 128:(g + 1) * 128, s_off:s_off + ncols],
                      wait=[(SC, toks[g % 2])], ch="wr%d" % (g % 2))
            st = op("scalar", lambda e, g=g: e.copy(
                WBF[:, (slot0 + g) * 512:(slot0 + g) * 512 + ncols],
                WRAW[:, (g % 2) * 512:(g % 2) * 512 + ncols]),
                wait=[t_w] + ([(TE, wbf_free_te)] if g == 0 else []))
            toks[g % 2] = st
        return st

    def rowmax_to_scalar(scl_col, sc2_col, wait_dv):
        g1 = dma("gpsimd", scr128, SCL[:, scl_col:scl_col + 1],
                 wait=[(DV, wait_dv)], ch="msc")
        g2 = dma("gpsimd", RMAX[0:1, 0:128],
                 scr128.rearrange("p one -> (one) (p)"), wait=[g1], ch="msc")
        d = op("vector", lambda e: e.tensor_reduce(
            out=SC2[0:1, sc2_col:sc2_col + 1], in_=RMAX[0:1, 0:128],
            axis=mybir.AxisListType.X, op=ALU.max), wait=[g2])
        return d

    def bcast_scalars_f32(sc2_srcs, scl_dst0, wait_dv):
        n = len(sc2_srcs)

        def fhl(e):
            for i, c in enumerate(sc2_srcs):
                e.tensor_copy(BROWS[0:1, 1536 + i:1537 + i],
                              SC2[0:1, c:c + 1])
            e.drain()
            for i, c in enumerate(sc2_srcs):
                e.tensor_sub(SC2[0:1, 24 + i:25 + i], SC2[0:1, c:c + 1],
                             BROWS[0:1, 1536 + i:1537 + i])
            e.drain()
            for i in range(n):
                e.tensor_copy(TERNB[0:1, i:i + 1],
                              SC2[0:1, 24 + i:25 + i])
            return e.engine_nop()
        d_hl = op("vector", fhl, wait=[(DV, wait_dv)])
        for i in range(n):
            op("tensor", lambda e, i=i: e.matmul(
                PS[:, 2048 + i:2048 + i + 1], ONESROW[0:1, :],
                BROWS[0:1, 1536 + i:1537 + i], start=True, stop=False),
                wait=[(DV, d_hl)] if i == 0 else [])
            op("tensor", lambda e, i=i: e.matmul(
                PS[:, 2048 + i:2048 + i + 1], ONESROW[0:1, :],
                TERNB[0:1, i:i + 1], start=False, stop=True))
        d_sc = op("vector", lambda e: e.tensor_copy(
            SCL[:, scl_dst0:scl_dst0 + n], PS[:, 2048:2048 + n]),
            wait=[(TE, TE.v)])
        return d_sc

    def attention(kv_of_head, qsrc, ysrc):
        sbase = [2048, 3072]
        tail = [None]
        sub_tmr = {}
        last_exp = [0]
        last_pt_te = [0, 0]
        last_yev = [0]
        prev_diag_dv = [0]
        unit_ctr = [0]

        def emit_head(j, h, kv):
            L2 = (j + 1) * 256
            nsub = 1 if L2 <= 1024 else 2
            tmr_toks = []
            for s in range(nsub):
                lo = s * 1024
                ln = min(1024, L2 - lo)
                buf = unit_ctr[0] % 2
                unit_ctr[0] += 1
                w_sbuf = sub_tmr.get(buf, 0)
                first = True
                for (a, b) in _bank_splits(sbase[buf], sbase[buf] + ln):
                    op("tensor", lambda e, j=j, h=h, kv=kv, a=a, b=b, lo=lo, buf=buf: e.matmul(
                        PS[:, a:b],
                        qsrc[:, h * T + j * 128:h * T + j * 128 + 128],
                        KG[:, kv * 2048 + lo + (a - sbase[buf]):
                           kv * 2048 + lo + (b - sbase[buf])],
                        start=True, stop=True),
                        wait=([(DV, w_sbuf)] if first else []))
                    first = False
                te_s = TE.v
                lens_ap = LENS if s == 0 else LENS2

                def fm(e, j=j, s=s, lo=lo, ln=ln, buf=buf, nsub=nsub,
                       lens_ap=lens_ap):
                    if s > 0:
                        e.drain()
                    e.tensor_mask_reduce(
                        out=SMF[:, lo:lo + ln], in_=PS[:, sbase[buf]:sbase[buf] + ln],
                        mask_start=0.0, mask_end=lens_ap[:, j:j + 1], scale=-0.5,
                        accum_in=(POSBIG if s == 0 else SCL[:, 0:1]),
                        op=ALU.min, accum_out=SCL[:, 0:1])
                    if s == nsub - 1:
                        e.drain()
                        return e.tensor_scalar(out=SCL[:, 1:2], in0=SCL[:, 0:1],
                                               scalar1=2.0, scalar2=None,
                                               op0=ALU.mult)
                    return e.engine_nop()
                dv_m = op("vector", fm,
                          wait=[(TE, te_s)] +
                               ([(SC, last_exp[0])] if s == 0 else []))
                sub_tmr[(unit_ctr[0] - 1) % 2] = dv_m
                tmr_toks.append(dv_m)
            ph = (j * 16 + h) % 2
            sc_e = op("scalar", lambda e, L2=L2, ph=ph: e.activation(
                out=PPB[:, ph * 4096:ph * 4096 + L2], in_=SMF[:, 0:L2],
                func=AF.Exp, bias=SCL[:, 1:2], scale=-2.0,
                accum_out=SCL[:, 2:3]),
                wait=[(DV, tmr_toks[-1]), (TE, last_pt_te[ph])])
            last_exp[0] = sc_e
            return (j, h, kv, L2, ph, sc_e)

        def emit_tail(args):
            j, h, kv, L2, ph, sc_e = args

            def fd(e):
                e.reciprocal(SCL[:, 3:4], SCL[:, 2:3])
                e.drain()
                return e.tensor_scalar(out=VW[:, 0:128], in0=IDENT[:],
                                       scalar1=SCL[:, 3:4], scalar2=None,
                                       op0=ALU.mult)
            dv_d = op("vector", fd, wait=[(SC, sc_e), (DV, prev_diag_dv[0])])
            prev_diag_dv[0] = dv_d
            nb = L2 // 128
            for b4 in range(0, nb, 4):
                bb = min(4, nb - b4)
                bank = (b4 // 4) % 2
                for b in range(b4, b4 + bb):
                    op("tensor", lambda e, b=b, b4=b4, bank=bank, ph=ph: e.matmul(
                        PS[:, bank * 512 + (b - b4) * 128:
                           bank * 512 + (b - b4) * 128 + 128],
                        PPB[:, ph * 4096 + b * 128:ph * 4096 + b * 128 + 128],
                        VW[:, 0:128], start=True, stop=True),
                        wait=([(DV, dv_d if b4 == 0 else DV.v)]
                              if b == b4 else []))
                te_pt = TE.v
                dv_e = op("vector", lambda e, bank=bank, bb=bb: e.tensor_copy(
                    PT[:, bank * 512:bank * 512 + bb * 128],
                    PS[:, bank * 512:bank * 512 + bb * 128]),
                    wait=[(TE, te_pt)])
                for b in range(b4, b4 + bb):
                    op("tensor", lambda e, b=b, b4=b4, bank=bank, kv=kv, nb=nb: e.matmul(
                        PS[:, 1024:1152],
                        VG[:, kv * 2048 + b * 128:kv * 2048 + b * 128 + 128],
                        PT[:, bank * 512 + (b - b4) * 128:
                           bank * 512 + (b - b4) * 128 + 128],
                        start=(b == 0), stop=(b == nb - 1)),
                        wait=([(DV, dv_e)] if b == b4 else
                              ([(DV, last_yev[0])] if b == 0 else [])))
            last_pt_te[ph] = TE.v
            last_yev[0] = op("vector", lambda e, j=j, h=h: e.tensor_copy(
                ysrc[:, h * T + j * 128:h * T + j * 128 + 128],
                PS[:, 1024:1152]), wait=[(TE, TE.v)])

        for j in range(8):
            for h in range(16):
                args = emit_head(j, h, kv_of_head(h))
                if tail[0] is not None:
                    emit_tail(tail[0])
                tail[0] = args
        emit_tail(tail[0])

    def rms_bcast_row(row_ap_bf, ps_lo, base=0, wait=()):
        for hf in range(2):
            op("tensor", lambda e, hf=hf: e.matmul(
                PS[:, ps_lo + hf * 512:ps_lo + hf * 512 + 512],
                ONESROW[base:base + 1, :],
                row_ap_bf[:, hf * 512:hf * 512 + 512], start=True, stop=True),
                wait=(list(wait) if hf == 0 else []))
        return TE.v

    wmax_state = {"d": 0, "first": {10: True, 11: True}, "done": {}}
    wmax_tiles = ([("fc", g, c0, 10) for g in range(16) for c0 in range(0, 8192, 1024)] +
                  [("pr", g, c0, 11) for g in range(64) for c0 in range(0, 2048, 1024)])

    def emit_wmax_tiles(n):
        cnt = 0
        while wmax_tiles and cnt < n:
            kind, g, c0, dst_scl = wmax_tiles.pop(0)
            wsrc = wfcT if kind == "fc" else wprT
            t_w = dma("sync", WRAW[:], wsrc[g * 128:(g + 1) * 128, c0:c0 + 1024],
                      wait=[(DV, wmax_state["d"])], ch="wr0")
            first = wmax_state["first"][dst_scl]
            wmax_state["first"][dst_scl] = False

            def f(e, first=first, dst_scl=dst_scl):
                e.tensor_reduce(out=SCL[:, 12:13], in_=WRAW[:],
                                axis=mybir.AxisListType.X, op=ALU.max,
                                apply_absolute_value=True)
                e.drain()
                if first:
                    return e.tensor_copy(SCL[:, dst_scl:dst_scl + 1],
                                         SCL[:, 12:13])
                return e.tensor_tensor(SCL[:, dst_scl:dst_scl + 1],
                                       SCL[:, dst_scl:dst_scl + 1],
                                       SCL[:, 12:13], op=ALU.max)
            wmax_state["d"] = op("vector", f, wait=[t_w])
            cnt += 1

    def finish_wmax():
        emit_wmax_tiles(10 ** 9)
        wmax_state["done"]["dwf"] = rowmax_to_scalar(10, 0, wmax_state["d"])
        wmax_state["done"]["dwp"] = rowmax_to_scalar(11, 1, wmax_state["d"])

    def block_l0_attn(src_hbm, dst_hbm):
        def phase_done(n):
            return PHASE_LIMIT <= n
        te_st = ln_stats_hbm(src_hbm, XN)
        te_bc = ln_smalls(te_st)
        xn_tok = ln_apply(src_hbm, te_bc, XN)
        barrier()
        if phase_done(1):
            return

        ccA_in_v = ccA_in.rearrange("a (k p c) -> a k p c", p=128, c=1024)
        prep_tok = tern_prep(wqkvT, 0, 0)
        if phase_done(1.2):
            barrier()
            return
        te_mm_end = [0, 0]
        ss_te = [0, 0]
        evac_tok = [0, 0]
        for s in range(6 if not phase_done(1.5) else 1):
            slot0 = (s % 2) * 16
            for o4 in range(4):
                oc = s * 4 + o4
                pb = (o4 % 2) * 1024
                for hf in range(2):
                    for g in range(CH):
                        wl = []
                        if g == 0 and hf == 0:
                            if o4 == 0:
                                wl = [(DV, prep_tok), (DV, xn_tok)]
                            wl = wl + [evac_tok[o4 % 2]] if evac_tok[o4 % 2] else wl
                        op("tensor", lambda e, slot0=slot0, g=g, o4=o4, hf=hf, pb=pb: e.matmul(
                            PS[:, pb + hf * 512:pb + hf * 512 + 512],
                            WBF[:, (slot0 + g) * 512 + o4 * 128:
                                (slot0 + g) * 512 + o4 * 128 + 128],
                            XN[:, g * T + hf * 512:g * T + hf * 512 + 512],
                            start=(g == 0), stop=(g == CH - 1)), wait=wl)
                te_mm = TE.v
                if oc < 16:
                    dst = QB[:, oc * T:(oc + 1) * T]
                elif oc < 20:
                    dst = KG[:, (oc - 16) * 2048:(oc - 16) * 2048 + 1024]
                else:
                    dst = PPB[:, (oc - 20) * 1024:(oc - 20) * 1024 + 1024]
                sc_ev = op("scalar", lambda e, dst=dst, pb=pb: e.copy(
                    dst, PS[:, pb:pb + 1024]), wait=[(TE, te_mm)])
                evac_tok[o4 % 2] = (SC, sc_ev)
                if oc < 20:
                    dv_sq = op("vector", lambda e, dst=dst, oc=oc: e.tensor_mul(
                        SQ[:, (oc % 2) * 1024:(oc % 2) * 1024 + 1024],
                        dst, dst), wait=[(SC, sc_ev), (TE, ss_te[oc % 2])])
                    for hf in range(2):
                        op("tensor", lambda e, oc=oc, hf=hf: e.matmul(
                            PS[0:32, 3072 + hf * 512:3072 + hf * 512 + 512],
                            AONE[:, 32 - oc:64 - oc],
                            SQ[:, (oc % 2) * 1024 + hf * 512:
                               (oc % 2) * 1024 + hf * 512 + 512],
                            start=(oc == 0), stop=(oc == 19)),
                            wait=[(DV, dv_sq)])
                    ss_te[oc % 2] = TE.v
            te_mm_end[s % 2] = TE.v
            if s < 5 and not phase_done(1.5):
                prep_tok = tern_prep(wqkvT, (s + 1) * 512, ((s + 1) % 2) * 16,
                                     mean_free_dv=prep_tok,
                                     wbf_free_te=te_mm_end[(s + 1) % 2])
        barrier()
        if phase_done(1.5):
            return
        if phase_done(2):
            return

        d_r = op("vector", lambda e: e.tensor_scalar(
            out=ROWS[0:20, 0:1024], in0=PS[0:20, 3072:4096],
            scalar1=1.0 / 128.0, scalar2=None, op0=ALU.mult))
        s_r = op("scalar", lambda e: e.activation(
            out=ROWS[32:52, 0:1024], in_=ROWS[0:20, 0:1024], func=AF.Sqrt,
            bias=CEPS[0:20, 1:2], scale=1.0), wait=[(DV, d_r)])

        def frms2(e):
            e.reciprocal(ROWS[0:20, 0:1024], ROWS[32:52, 0:1024])
            e.drain()
            e.tensor_scalar(out=ROWS[0:16, 0:1024], in0=ROWS[0:16, 0:1024],
                            scalar1=QGC[:, 0:1], scalar2=None, op0=ALU.mult)
            e.drain()
            return e.tensor_copy(BROWS[0:20, 1024:2048], ROWS[0:20, 0:1024])
        d_r2 = op("vector", frms2, wait=[(SC, s_r)])

        d_rope = d_r2
        te_b = 0
        for idx in range(20):
            if idx < 16:
                src_c = QB[:, idx * T:(idx + 1) * T]
            else:
                src_c = KG[:, (idx - 16) * 2048:(idx - 16) * 2048 + 1024]
            g1 = dma("gpsimd", SW[0:64, :], src_c[64:128, :],
                     wait=[(DV, d_rope)], ch="swp")
            g2 = dma("gpsimd", SW[64:128, :], src_c[0:64, :], ch="swp")
            gs = dma("gpsimd", BROWS[0:1, 0:1024],
                     BROWS[idx:idx + 1, 1024:2048],
                     wait=[(DV, d_r2), (TE, te_b)], ch="stg0")
            te_b = rms_bcast_row(BROWS[0:1, 0:1024], 2048, base=0,
                                 wait=[(DV, d_rope), gs])

            def frope(e, src_c=src_c):
                e.tensor_mul(XCH[:], src_c, COSX[:])
                e.tensor_mul(XC2[:], SW[:], SINX[:])
                e.drain()
                e.tensor_add(XCH[:], XCH[:], XC2[:])
                e.drain()
                return e.tensor_mul(src_c, XCH[:], PS[:, 2048:3072])
            d_rope = op("vector", frope, wait=[g2, (TE, te_b)])
            if idx >= 16:
                dma("gpsimd", ccA_in_v[0, idx - 16], src_c,
                    wait=[(DV, d_rope)], ch="blob")
        dv_prev = d_rope
        for vv in range(4):
            for b in range(8):
                te_t = op("tensor", lambda e, vv=vv, b=b: e.transpose(
                    PS[:, 2048 + (b % 2) * 512:2048 + (b % 2) * 512 + 64].bitcast(BF16),
                    PPB[:, vv * 1024 + b * 128:vv * 1024 + b * 128 + 128],
                    IDENT[:]), wait=[(DV, dv_prev)])
                dv_prev = op("vector", lambda e, vv=vv, b=b: e.tensor_copy(
                    VG[:, vv * 2048 + b * 128:vv * 2048 + b * 128 + 128],
                    PS[:, 2048 + (b % 2) * 512:2048 + (b % 2) * 512 + 64].bitcast(BF16)),
                    wait=[(TE, te_t)])
            dma("gpsimd", ccA_in_v[0, 4 + vv],
                VG[:, vv * 2048:vv * 2048 + 1024], wait=[(DV, dv_prev)],
                ch="blob")
        barrier()
        if phase_done(3):
            return

        ccc = chan("cc")
        blobc = chan("blob")
        cc_tok = (ccc, ccc.bump(1))
        blob_full = (blobc, blobc.v)

        def fcc_emit(e, bf=blob_full):
            e.wait_ge(bf[0].sem, bf[1])
            inst = e.collective_compute(
                "AllGather", ALU.bypass,
                replica_groups=[[0, 1], [2, 3], [4, 5], [6, 7]],
                ins=[ccA_in], outs=[ccA_out])
            inst.then_inc(ccc.sem, 1)
        streams["gpsimd"].append(fcc_emit)

        ccA_out_v = ccA_out.rearrange("a (r k p c) -> a r k p c",
                                      r=2, p=128, c=1024)
        for kk in range(4):
            for par in range(2):
                dma("gpsimd",
                    KG[:, kk * 2048:kk * 2048 + 2048].rearrange(
                        "p (b two c) -> p b two c", two=2, c=128)[:, :, par],
                    ccA_out_v[0, par, kk].rearrange("p (b c) -> p b c", c=128),
                    wait=([cc_tok] if (kk == 0 and par == 0) else []), ch="gath")
        for vv in range(4):
            for par in range(2):
                dma("gpsimd",
                    VG[:, vv * 2048:vv * 2048 + 2048].rearrange(
                        "p (b two c) -> p b two c", two=2, c=128)[:, :, par],
                    ccA_out_v[0, par, 4 + vv].rearrange("p (b c) -> p b c",
                                                        c=128), ch="gath")
        barrier()
        if phase_done(4):
            return

        attention(lambda h: h // 4, QB, XN)
        barrier()
        if phase_done(5):
            return

        te_ys = ln_stats_sbuf(lambda c: (BIG, c * T))

        def fyr(e):
            return e.tensor_scalar(out=ROWS[0:1, 0:1024],
                                   in0=PS[32:33, 3072:4096],
                                   scalar1=1.0 / D, scalar2=None, op0=ALU.mult)
        d_y = op("vector", fyr, wait=[(TE, te_ys)])
        s_y = op("scalar", lambda e: e.activation(
            out=ROWS[32:33, 0:1024], in_=ROWS[0:1, 0:1024], func=AF.Sqrt,
            bias=CEPS[0:1, 1:2], scale=1.0), wait=[(DV, d_y)])

        def fyr2(e):
            e.reciprocal(ROWS[0:1, 0:1024], ROWS[32:33, 0:1024])
            e.drain()
            return e.tensor_copy(BROWS[0:1, 0:1024], ROWS[0:1, 0:1024])
        d_y2 = op("vector", fyr2, wait=[(SC, s_y)])
        te_yb = rms_bcast_row(BROWS[0:1, 0:1024], 0, wait=[(DV, d_y2)])
        yn_tok = 0
        for c in range(CH):
            yn_tok = op("vector", lambda e, c=c: e.tensor_mul(
                QB[:, c * T:(c + 1) * T], XN[:, c * T:(c + 1) * T],
                PS[:, 0:1024]), wait=[(TE, te_yb)])
        barrier()

        prep_tok = tern_prep(wprojT, 0, 0)
        te_mm_end = [0, 0]
        evac_tok = [None, None]
        for s in range(4):
            slot0 = (s % 2) * 16
            for o4 in range(4):
                oc = s * 4 + o4
                pb = (o4 % 2) * 1024
                for hf in range(2):
                    for g in range(CH):
                        wl = []
                        if g == 0 and hf == 0:
                            if o4 == 0:
                                wl = [(DV, prep_tok), (DV, yn_tok)]
                            if evac_tok[o4 % 2]:
                                wl = wl + [evac_tok[o4 % 2]]
                        op("tensor", lambda e, slot0=slot0, g=g, o4=o4, hf=hf, pb=pb: e.matmul(
                            PS[:, pb + hf * 512:pb + hf * 512 + 512],
                            WBF[:, (slot0 + g) * 512 + o4 * 128:
                                (slot0 + g) * 512 + o4 * 128 + 128],
                            QB[:, g * T + hf * 512:g * T + hf * 512 + 512],
                            start=(g == 0), stop=(g == CH - 1)), wait=wl)
                te_mm = TE.v
                sc_ev = op("scalar", lambda e, oc=oc, pb=pb: e.copy(
                    XN[:, oc * T:(oc + 1) * T], PS[:, pb:pb + 1024]),
                    wait=[(TE, te_mm)])
                evac_tok[o4 % 2] = (SC, sc_ev)
            te_mm_end[s % 2] = TE.v
            if s < 3:
                prep_tok = tern_prep(wprojT, (s + 1) * 512, ((s + 1) % 2) * 16,
                                     mean_free_dv=prep_tok,
                                     wbf_free_te=te_mm_end[(s + 1) % 2])
        barrier()

        te_st2 = ln_stats_sbuf(lambda c: (BIG, c * T))
        te_bc2 = ln_smalls(te_st2)
        ln2_residual(lambda c: (BIG, c * T), src_hbm, dst_hbm, te_bc2)
        barrier()

    def block_l0_ffn(src_hbm, dst_hbm):
        te_st = ln_stats_hbm(src_hbm, XN)
        te_bc = ln_smalls(te_st)
        xn_tok = ln_apply(src_hbm, te_bc, XN)
        barrier()

        te_mm_last = 0
        ev_last = []
        for sc_i in range(11):
            prep_toks = []
            u8_tok = [DV.v] * 4
            for w_i, wsrc in ((0, wqgT), (16, wquT)):
                for g in range(16):
                    slot = (0 if w_i == 0 else 2) + (g % 2)
                    t_w = dma("sync", WU8[:, slot * 512:slot * 512 + 512],
                              wsrc[g * 128:(g + 1) * 128,
                                   sc_i * 512:sc_i * 512 + 512],
                              wait=[(DV, u8_tok[slot])], ch="wu%d" % slot)
                    tok = op("vector", lambda e, w_i=w_i, g=g, slot=slot: e.tensor_scalar(
                        out=WBF[:, (w_i + g) * 512:(w_i + g) * 512 + 512],
                        in0=WU8[:, slot * 512:slot * 512 + 512],
                        scalar1=128.0, scalar2=None, op0=ALU.subtract),
                        wait=[t_w] +
                             ([(TE, te_mm_last)] if (w_i == 0 and g == 0) else []))
                    u8_tok[slot] = tok
                    prep_toks.append(tok)
            prep_done = prep_toks[-1]
            for q_i in range(4):
                hc = sc_i * 4 + q_i
                for hf in range(2):
                    for g in range(CH):
                        wl = []
                        if g == 0 and hf == 0:
                            wl = [(DV, prep_done)] + \
                                ([(DV, xn_tok)] if hc == 0 else []) + ev_last
                        op("tensor", lambda e, g=g, q_i=q_i, hf=hf: e.matmul(
                            PS[:, hf * 512:hf * 512 + 512],
                            WBF[:, g * 512 + q_i * 128:g * 512 + q_i * 128 + 128],
                            XN[:, g * T + hf * 512:g * T + hf * 512 + 512],
                            start=(g == 0), stop=(g == CH - 1)), wait=wl)
                    for g in range(CH):
                        op("tensor", lambda e, g=g, q_i=q_i, hf=hf: e.matmul(
                            PS[:, 1024 + hf * 512:1024 + hf * 512 + 512],
                            WBF[:, (16 + g) * 512 + q_i * 128:
                                (16 + g) * 512 + q_i * 128 + 128],
                            XN[:, g * T + hf * 512:g * T + hf * 512 + 512],
                            start=(g == 0), stop=(g == CH - 1)))
                te_mm = TE.v
                s_si = op("scalar", lambda e, hc=hc: e.activation(
                    out=VW[:], in_=PS[:, 0:1024], func=AF.Silu,
                    bias=0.0, scale=SGC[:, hc:hc + 1]), wait=[(TE, te_mm)])
                tns, off = tcol(hc)
                d_t = op("vector", lambda e, tns=tns, off=off, hc=hc: e.scalar_tensor_tensor(
                    out=tns[:, off:off + T], in0=PS[:, 1024:2048],
                    scalar=SUC[:, hc:hc + 1], in1=VW[:],
                    op0=ALU.mult, op1=ALU.mult), wait=[(SC, s_si)])
                ev_last = [(SC, s_si), (DV, d_t)]
                emit_wmax_tiles(5)
            te_mm_last = TE.v
        barrier()

        te_mm_last = 0
        ev_last = [None, None]
        ust_last = [None]
        for os_i in range(8):
            prep_toks = []
            u8_tok = [DV.v] * 4
            for g in range(44):
                slot = g % 4
                t_w = dma("sync", WU8[:, slot * 512:slot * 512 + 256],
                          wqdT[g * 128:(g + 1) * 128,
                               os_i * 256:os_i * 256 + 256],
                          wait=[(DV, u8_tok[slot])], ch="wu%d" % slot)
                tok = op("vector", lambda e, g=g, slot=slot: e.tensor_scalar(
                    out=BIG[:, g * 256:g * 256 + 256],
                    in0=WU8[:, slot * 512:slot * 512 + 256],
                    scalar1=128.0, scalar2=None, op0=ALU.subtract),
                    wait=[t_w] +
                         ([(TE, te_mm_last)] if g == 0 else []))
                u8_tok[slot] = tok
                prep_toks.append(tok)
            prep_done = prep_toks[-1]
            for oc2 in range(2):
                oc = os_i * 2 + oc2
                pb = oc2 * 1024
                for hf in range(2):
                    for g in range(44):
                        tns, off = tcol(g)
                        wl = []
                        if g == 0 and hf == 0:
                            wl = [(DV, prep_done)]
                            if ev_last[oc2]:
                                wl = wl + [ev_last[oc2]]
                        op("tensor", lambda e, g=g, oc2=oc2, hf=hf, pb=pb, tns=tns, off=off: e.matmul(
                            PS[:, pb + hf * 512:pb + hf * 512 + 512],
                            BIG[:, g * 256 + oc2 * 128:g * 256 + oc2 * 128 + 128],
                            tns[:, off + hf * 512:off + hf * 512 + 512],
                            start=(g == 0), stop=(g == 43)), wait=wl)
                te_mm = TE.v
                d_ev = op("vector", lambda e, oc=oc, pb=pb: e.tensor_scalar(
                    out=PT[:], in0=PS[:, pb:pb + 1024],
                    scalar1=SDC[:, oc:oc + 1], scalar2=None, op0=ALU.mult),
                    wait=[(TE, te_mm)] + ([ust_last[0]] if ust_last[0] else []))
                ust_last[0] = dma("gpsimd", xview(u_dram, oc), PT[:],
                                  wait=[(DV, d_ev)], ch="ust")
                ev_last[oc2] = (DV, d_ev)
                emit_wmax_tiles(4)
            te_mm_last = TE.v
        finish_wmax()
        barrier()

        te_st2 = ln_stats_udram(u_dram)
        te_bc2 = ln_smalls(te_st2)
        ln2_residual(None, src_hbm, dst_hbm, te_bc2, u_hbm=u_dram)
        barrier()

    def block_l1_attn(src_hbm, dst_hbm):
        te_st = ln_stats_hbm(src_hbm, XN)
        te_bc = ln_smalls(te_st)
        xn_tok = ln_apply(src_hbm, te_bc, XN)
        barrier()

        ccB_in_v = ccB_in.rearrange("a (k p c) -> a k p c", p=128, c=1024)
        prep_tok = cast_prep(l1wT, 0, 0)
        te_mm_end = [0, 0]
        evac_tok = [None, None]
        for s in range(5):
            slot0 = (s % 2) * 16
            ncols = 512 if s < 4 else 256
            for o4 in range(ncols // 128):
                oc = s * 4 + o4
                pb = (o4 % 2) * 1024
                for hf in range(2):
                    for g in range(CH):
                        wl = []
                        if g == 0 and hf == 0:
                            if o4 == 0:
                                wl = [(SC, prep_tok), (DV, xn_tok)]
                            if evac_tok[o4 % 2]:
                                wl = wl + [evac_tok[o4 % 2]]
                        op("tensor", lambda e, slot0=slot0, g=g, o4=o4, hf=hf, pb=pb: e.matmul(
                            PS[:, pb + hf * 512:pb + hf * 512 + 512],
                            WBF[:, (slot0 + g) * 512 + o4 * 128:
                                (slot0 + g) * 512 + o4 * 128 + 128],
                            XN[:, g * T + hf * 512:g * T + hf * 512 + 512],
                            start=(g == 0), stop=(g == CH - 1)), wait=wl)
                te_mm = TE.v
                if oc < 16:
                    d_e = op("vector", lambda e, oc=oc, pb=pb: e.tensor_scalar(
                        out=QB[:, oc * T:(oc + 1) * T], in0=PS[:, pb:pb + 1024],
                        scalar1=L1BI[:, oc:oc + 1], scalar2=INV_SQRT_HD,
                        op0=ALU.add, op1=ALU.mult), wait=[(TE, te_mm)])
                    evac_tok[o4 % 2] = (DV, d_e)
                elif oc == 16:
                    d_e = op("vector", lambda e, pb=pb: e.tensor_scalar(
                        out=RT[:, 0:1024], in0=PS[:, pb:pb + 1024],
                        scalar1=L1BI[:, 16:17], scalar2=None, op0=ALU.add),
                        wait=[(TE, te_mm)])
                    dma("gpsimd", ccB_in_v[0, 0], RT[:, 0:1024],
                        wait=[(DV, d_e)], ch="blob")
                    evac_tok[o4 % 2] = (DV, d_e)
                else:
                    d_e = op("vector", lambda e, pb=pb: e.tensor_scalar(
                        out=RT[:, 1024:2048], in0=PS[:, pb:pb + 1024],
                        scalar1=L1BI[:, 17:18], scalar2=None, op0=ALU.add),
                        wait=[(TE, te_mm)])
                    evac_tok[o4 % 2] = (DV, d_e)
                    dv_prev = d_e
                    for b in range(8):
                        te_t = op("tensor", lambda e, b=b: e.transpose(
                            PS[:, 2048 + (b % 2) * 512:2048 + (b % 2) * 512 + 64].bitcast(BF16),
                            RT[:, 1024 + b * 128:1024 + b * 128 + 128],
                            IDENT[:]), wait=[(DV, dv_prev)])
                        dv_prev = op("vector", lambda e, b=b: e.tensor_copy(
                            VG[:, b * 128:b * 128 + 128],
                            PS[:, 2048 + (b % 2) * 512:2048 + (b % 2) * 512 + 64].bitcast(BF16)),
                            wait=[(TE, te_t)])
                    dma("gpsimd", ccB_in_v[0, 1], VG[:, 0:1024],
                        wait=[(DV, dv_prev)], ch="blob")
            te_mm_end[s % 2] = TE.v
            if s < 4:
                prep_tok = cast_prep(l1wT, (s + 1) * 512, ((s + 1) % 2) * 16,
                                     ncols=(512 if s < 3 else 256),
                                     wbf_free_te=te_mm_end[(s + 1) % 2])
        barrier()

        ccc = chan("cc")
        blobc = chan("blob")
        cc_tok = (ccc, ccc.bump(1))
        blob_full = (blobc, blobc.v)

        def fcc_emit(e, bf=blob_full):
            e.wait_ge(bf[0].sem, bf[1])
            inst = e.collective_compute(
                "AllGather", ALU.bypass,
                replica_groups=[[0, 1], [2, 3], [4, 5], [6, 7]],
                ins=[ccB_in], outs=[ccB_out])
            inst.then_inc(ccc.sem, 1)
        streams["gpsimd"].append(fcc_emit)
        ccB_out_v = ccB_out.rearrange("a (r k p c) -> a r k p c",
                                      r=2, p=128, c=1024)
        for par in range(2):
            dma("gpsimd",
                KG[:, 0:2048].rearrange("p (b two c) -> p b two c",
                                        two=2, c=128)[:, :, par],
                ccB_out_v[0, par, 0].rearrange("p (b c) -> p b c", c=128),
                wait=([cc_tok] if par == 0 else []), ch="gath")
            dma("gpsimd",
                VG[:, 0:2048].rearrange("p (b two c) -> p b two c",
                                        two=2, c=128)[:, :, par],
                ccB_out_v[0, par, 1].rearrange("p (b c) -> p b c", c=128),
                ch="gath")
        barrier()

        attention(lambda h: 0, QB, XN)
        barrier()

        prep_tok = cast_prep(l1owT, 0, 0)
        te_mm_end = [0, 0]
        evac_tok = [None, None]
        for s in range(4):
            slot0 = (s % 2) * 16
            for o4 in range(4):
                oc = s * 4 + o4
                pb = (o4 % 2) * 1024
                for hf in range(2):
                    for g in range(CH):
                        wl = []
                        if g == 0 and hf == 0:
                            if o4 == 0:
                                wl = [(SC, prep_tok)]
                            if evac_tok[o4 % 2]:
                                wl = wl + [evac_tok[o4 % 2]]
                        op("tensor", lambda e, slot0=slot0, g=g, o4=o4, hf=hf, pb=pb: e.matmul(
                            PS[:, pb + hf * 512:pb + hf * 512 + 512],
                            WBF[:, (slot0 + g) * 512 + o4 * 128:
                                (slot0 + g) * 512 + o4 * 128 + 128],
                            XN[:, g * T + hf * 512:g * T + hf * 512 + 512],
                            start=(g == 0), stop=(g == CH - 1)), wait=wl)
                te_mm = TE.v
                d_e = op("vector", lambda e, oc=oc, pb=pb: e.tensor_scalar(
                    out=QB[:, oc * T:(oc + 1) * T], in0=PS[:, pb:pb + 1024],
                    scalar1=L1OB[:, oc:oc + 1], scalar2=None, op0=ALU.add),
                    wait=[(TE, te_mm)])
                evac_tok[o4 % 2] = (DV, d_e)
            te_mm_end[s % 2] = TE.v
            if s < 3:
                prep_tok = cast_prep(l1owT, (s + 1) * 512, ((s + 1) % 2) * 16,
                                     wbf_free_te=te_mm_end[(s + 1) % 2])
        barrier()

        te_st2 = ln_stats_sbuf(lambda c: (BIG, 16384 + c * T))
        te_bc2 = ln_smalls(te_st2)
        ln2_residual(lambda c: (BIG, 16384 + c * T), src_hbm, dst_hbm, te_bc2)
        barrier()

    def block_l1_ffn(src_hbm, dst_hbm):
        def phase_done(n):
            return PHASE_LIMIT <= n
        def fws(e):
            e.tensor_scalar(out=SC2[0:1, 2:4], in0=SC2[0:1, 0:2],
                            scalar1=1.0 / 448.0, scalar2=1e-6,
                            op0=ALU.mult, op1=ALU.max)
            e.drain()
            e.reciprocal(SC2[0:1, 6:8], SC2[0:1, 2:4])
            e.drain()
            return e.tensor_scalar(out=SC2[0:1, 4:6], in0=SC2[0:1, 6:8],
                                   scalar1=0.5, scalar2=None, op0=ALU.mult)
        d_ws = op("vector", fws,
                  wait=[(DV, wmax_state["done"]["dwf"]),
                        (DV, wmax_state["done"]["dwp"])])
        barrier()
        if phase_done(3.1):
            return

        for c in range(CH):
            t_x = dma("sync", XCH[:], xview(src_hbm, c),
                      wait=[(TE, TE.v if c > 0 else 0)], ch="xch")
            sc_tok = op("scalar", lambda e: e.copy(SW[:], XCH[:]),
                        wait=[t_x])
            dv_tok = op("vector", lambda e, c=c: e.tensor_mul(
                SQ[:, (c % 2) * 1024:(c % 2) * 1024 + 1024], SW[:], SW[:]),
                wait=[(SC, sc_tok)])
            for hf in range(2):
                lo = 3072 + hf * 512
                op("tensor", lambda e, c=c, hf=hf, lo=lo: e.matmul(
                    PS[0:1, lo:lo + 512], AONE[:, 32:33],
                    SW[:, hf * 512:hf * 512 + 512],
                    start=(c == 0), stop=(c == CH - 1)), wait=[(SC, sc_tok)])
                op("tensor", lambda e, c=c, hf=hf, lo=lo: e.matmul(
                    PS[32:33, lo:lo + 512], AONE[:, 32:33],
                    SQ[:, (c % 2) * 1024 + hf * 512:(c % 2) * 1024 + hf * 512 + 512],
                    start=(c == 0), stop=(c == CH - 1)), wait=[(DV, dv_tok)])
        te_bc = ln_smalls(TE.v, fp32_bcast=True)
        barrier()
        if phase_done(3.2):
            return

        first = True
        d_mx = 0
        sc_gate = SC.v
        for c in range(CH):
            t_x = dma("sync", XCH[:], xview(src_hbm, c),
                      wait=[(DV, d_mx)] + ([(SC, sc_gate)] if c == 0 else []),
                      ch="xch")

            def f(e, first=first):
                e.tensor_mul(XC2[:], XCH[:], PS[:, 0:1024])
                e.drain()
                e.tensor_add(XC2[:], XC2[:], PS[:, 1024:2048])
                e.drain()
                e.tensor_reduce(out=SCL[:, 10:11], in_=XC2[:],
                                axis=mybir.AxisListType.X, op=ALU.max,
                                apply_absolute_value=True)
                e.drain()
                if first:
                    return e.tensor_copy(SCL[:, 13:14], SCL[:, 10:11])
                return e.tensor_tensor(SCL[:, 13:14], SCL[:, 13:14],
                                       SCL[:, 10:11], op=ALU.max)
            d_mx = op("vector", f, wait=[t_x, (TE, te_bc)])
            first = False
        d_xm = rowmax_to_scalar(13, 8, d_mx)
        g2 = dma("gpsimd", ccm_in, SC2[0:1, 8:9], wait=[(DV, d_xm)], ch="msc")
        ccc = chan("cc")
        cc1 = (ccc, ccc.bump(1))

        def fcc1(e, g2=g2):
            e.wait_ge(g2[0].sem, g2[1])
            inst = e.collective_compute(
                "AllReduce", ALU.max,
                replica_groups=[[0, 1, 2, 3, 4, 5, 6, 7]],
                ins=[ccm_in], outs=[ccm_out])
            inst.then_inc(ccc.sem, 1)
        streams["gpsimd"].append(fcc1)
        g3 = dma("gpsimd", SC2[0:1, 9:10], ccm_out, wait=[cc1], ch="msc")

        def fxs(e):
            e.tensor_scalar(out=SC2[0:1, 10:11], in0=SC2[0:1, 9:10],
                            scalar1=1.0 / 448.0, scalar2=1e-6,
                            op0=ALU.mult, op1=ALU.max)
            e.drain()
            e.reciprocal(SC2[0:1, 13:14], SC2[0:1, 10:11])
            e.drain()
            e.tensor_scalar(out=SC2[0:1, 11:12], in0=SC2[0:1, 13:14],
                            scalar1=0.5, scalar2=None, op0=ALU.mult)
            e.drain()
            e.tensor_mul(SC2[0:1, 12:13], SC2[0:1, 10:11], SC2[0:1, 2:3])
            e.drain()
            return e.tensor_scalar(out=SC2[0:1, 12:13], in0=SC2[0:1, 12:13],
                                   scalar1=4.0, scalar2=None, op0=ALU.mult)
        d_xs = op("vector", fxs, wait=[g3])
        d_sc = bcast_scalars_f32([11, 4, 5, 12], 4, d_xs)
        barrier()
        if phase_done(3.3):
            return

        d_q = DV.v
        for c in range(CH):
            t_x = dma("sync", XCH[:], xview(src_hbm, c), wait=[(DV, d_q)],
                      ch="xch")

            def f(e, c=c):
                e.tensor_mul(XC2[:], XCH[:], PS[:, 0:1024])
                e.drain()
                e.tensor_add(XC2[:], XC2[:], PS[:, 1024:2048])
                e.drain()
                return e.tensor_scalar(out=XQ8[:, c * T:(c + 1) * T],
                                       in0=XC2[:], scalar1=SCL[:, 4:5],
                                       scalar2=None, op0=ALU.mult)
            d_q = op("vector", f, wait=[t_x, (DV, d_sc)])
        barrier()
        if phase_done(3.4):
            return

        first_h = True
        te_mm_end = [0, 0]
        h_tok = [None, None]
        hst_last = [None]
        for os_i in range(16):
            prep_toks = []
            w8_tok = [DV.v, DV.v]
            for g in range(16):
                t_w = dma("sync", WRAW[:, (g % 2) * 512:(g % 2) * 512 + 512],
                          wfcT[g * 128:(g + 1) * 128,
                               os_i * 512:os_i * 512 + 512],
                          wait=[(DV, w8_tok[g % 2])], ch="wr%d" % (g % 2))
                tok = op("vector", lambda e, g=g, os_i=os_i: e.tensor_scalar(
                    out=W8[:, ((os_i % 2) * 16 + g) * 512:
                           ((os_i % 2) * 16 + g) * 512 + 512],
                    in0=WRAW[:, (g % 2) * 512:(g % 2) * 512 + 512],
                    scalar1=SCL[:, 5:6], scalar2=None, op0=ALU.mult),
                    wait=[t_w] +
                         ([(TE, te_mm_end[os_i % 2])] if g == 0 else []))
                w8_tok[g % 2] = tok
                prep_toks.append(tok)
            prep_done = prep_toks[-1]
            for o4 in range(4):
                oc = os_i * 4 + o4
                pb = (o4 % 2) * 1024
                for hf in range(2):
                    for g in range(CH):
                        wl = []
                        if g == 0 and hf == 0:
                            if o4 == 0:
                                wl = [(DV, prep_done)]
                            if h_tok[o4 % 2]:
                                wl = wl + [h_tok[o4 % 2]]
                        op("tensor", lambda e, g=g, o4=o4, hf=hf, pb=pb, os_i=os_i: e.matmul(
                            PS[:, pb + hf * 512:pb + hf * 512 + 512],
                            W8[:, ((os_i % 2) * 16 + g) * 512 + o4 * 128:
                               ((os_i % 2) * 16 + g) * 512 + o4 * 128 + 128],
                            XQ8[:, g * T + hf * 512:g * T + hf * 512 + 512],
                            start=(g == 0), stop=(g == CH - 1)), wait=wl)
                te_mm = TE.v
                s_r = op("scalar", lambda e, pb=pb: e.activation(
                    out=XC2[:], in_=PS[:, pb:pb + 1024], func=AF.Relu,
                    bias=0.0, scale=SCL[:, 7:8]),
                    wait=[(TE, te_mm), (DV, DV.v)])

                def fh(e, pb=pb, first_h=first_h):
                    e.scalar_tensor_tensor(out=XCH[:], in0=PS[:, pb:pb + 1024],
                                           scalar=SCL[:, 7:8], in1=XC2[:],
                                           op0=ALU.mult, op1=ALU.mult)
                    e.drain()
                    e.tensor_reduce(out=SCL[:, 10:11], in_=XCH[:],
                                    axis=mybir.AxisListType.X, op=ALU.max)
                    e.drain()
                    if first_h:
                        return e.tensor_copy(SCL[:, 14:15], SCL[:, 10:11])
                    return e.tensor_tensor(SCL[:, 14:15], SCL[:, 14:15],
                                           SCL[:, 10:11], op=ALU.max)
                d_h = op("vector", fh, wait=[(SC, s_r)] +
                         ([hst_last[0]] if hst_last[0] else []))
                first_h = False
                h_tok[o4 % 2] = (DV, d_h)
                hst_last[0] = dma("gpsimd",
                                  h_dram.rearrange("(ch p) t -> ch p t", p=128)[oc],
                                  XCH[:], wait=[(DV, d_h)], ch="hst")
            te_mm_end[os_i % 2] = TE.v
        barrier()
        if phase_done(3.5):
            return

        d_hm = rowmax_to_scalar(14, 16, DV.v)
        g2 = dma("gpsimd", cch_in, SC2[0:1, 16:17], wait=[(DV, d_hm)], ch="msc")
        ccc = chan("cc")
        cc2 = (ccc, ccc.bump(1))

        def fcc2(e, g2=g2):
            e.wait_ge(g2[0].sem, g2[1])
            inst = e.collective_compute(
                "AllReduce", ALU.max,
                replica_groups=[[0, 1, 2, 3, 4, 5, 6, 7]],
                ins=[cch_in], outs=[cch_out])
            inst.then_inc(ccc.sem, 1)
        streams["gpsimd"].append(fcc2)
        g3 = dma("gpsimd", SC2[0:1, 17:18], cch_out, wait=[cc2], ch="msc")

        def fhs(e):
            e.tensor_scalar(out=SC2[0:1, 18:19], in0=SC2[0:1, 17:18],
                            scalar1=1.0 / 448.0, scalar2=1e-6,
                            op0=ALU.mult, op1=ALU.max)
            e.drain()
            e.reciprocal(SC2[0:1, 21:22], SC2[0:1, 18:19])
            e.drain()
            e.tensor_scalar(out=SC2[0:1, 19:20], in0=SC2[0:1, 21:22],
                            scalar1=0.5, scalar2=None, op0=ALU.mult)
            e.drain()
            e.tensor_mul(SC2[0:1, 20:21], SC2[0:1, 18:19], SC2[0:1, 3:4])
            e.drain()
            return e.tensor_scalar(out=SC2[0:1, 20:21], in0=SC2[0:1, 20:21],
                                   scalar1=4.0, scalar2=None, op0=ALU.mult)
        d_hs = op("vector", fhs, wait=[g3])
        d_sc2 = bcast_scalars_f32([19, 20], 8, d_hs)
        barrier()
        if phase_done(3.6):
            return

        d_q = DV.v
        for c in range(64):
            t_h = dma("sync", XCH[:],
                      h_dram.rearrange("(ch p) t -> ch p t", p=128)[c],
                      wait=[(DV, d_q)], ch="xch")
            d_q = op("vector", lambda e, c=c: e.tensor_scalar(
                out=H8[:, c * T:(c + 1) * T], in0=XCH[:],
                scalar1=SCL[:, 8:9], scalar2=None, op0=ALU.mult),
                wait=[t_h, (DV, d_sc2)])
        barrier()
        if phase_done(3.7):
            return

        te_mm_last = 0
        ev_last = [None, None]
        ust_last = [None]
        for os_i in range(8):
            prep_toks = []
            w8_tok = [DV.v] * 4
            for g in range(64):
                slot = g % 4
                t_w = dma("sync", WRAW[:, slot * 256:slot * 256 + 256],
                          wprT[g * 128:(g + 1) * 128,
                               os_i * 256:os_i * 256 + 256],
                          wait=[(DV, w8_tok[slot])], ch="wr%d" % slot)
                tok = op("vector", lambda e, g=g, slot=slot: e.tensor_scalar(
                    out=W8[:, g * 256:g * 256 + 256],
                    in0=WRAW[:, slot * 256:slot * 256 + 256],
                    scalar1=SCL[:, 6:7], scalar2=None, op0=ALU.mult),
                    wait=[t_w] +
                         ([(TE, te_mm_last)] if g == 0 else []))
                w8_tok[slot] = tok
                prep_toks.append(tok)
            prep_done = prep_toks[-1]
            for oc2 in range(2):
                oc = os_i * 2 + oc2
                pb = oc2 * 1024
                for hf in range(2):
                    for g in range(64):
                        wl = []
                        if g == 0 and hf == 0:
                            wl = [(DV, prep_done)]
                            if ev_last[oc2]:
                                wl = wl + [ev_last[oc2]]
                        op("tensor", lambda e, g=g, oc2=oc2, hf=hf, pb=pb: e.matmul(
                            PS[:, pb + hf * 512:pb + hf * 512 + 512],
                            W8[:, g * 256 + oc2 * 128:g * 256 + oc2 * 128 + 128],
                            H8[:, g * T + hf * 512:g * T + hf * 512 + 512],
                            start=(g == 0), stop=(g == 63)), wait=wl)
                te_mm = TE.v
                d_ev = op("vector", lambda e, pb=pb: e.tensor_scalar(
                    out=PT[:], in0=PS[:, pb:pb + 1024],
                    scalar1=SCL[:, 9:10], scalar2=None, op0=ALU.mult),
                    wait=[(TE, te_mm)] + ([ust_last[0]] if ust_last[0] else []))
                ust_last[0] = dma("gpsimd", xview(u_dram, oc), PT[:],
                                  wait=[(DV, d_ev)], ch="ust")
                ev_last[oc2] = (DV, d_ev)
            te_mm_last = TE.v
        barrier()
        if phase_done(3.8):
            return

        te_st2 = ln_stats_udram(u_dram)
        te_bc2 = ln_smalls(te_st2)
        ln2_residual(None, src_hbm, dst_hbm, te_bc2, u_hbm=u_dram)
        barrier()

    blocks = [block_l0_attn, block_l0_ffn, block_l1_attn, block_l1_ffn]
    srcs = [x0, xs, xs, xs]
    dsts = [xs, xs, xs, out_t]
    dsts[N_BLOCKS - 1] = out_t
    for i in range(N_BLOCKS):
        blocks[i](srcs[i], dsts[i])

    final_toks = [(c.sem, c.v) for c in channels.values() if c.v > 0]

    def _final(e, wl=final_toks):
        for (s, v) in wl:
            e.wait_ge(s, v)
    streams["gpsimd"].append(_final)

    with es:
        with nc.Block() as block:
            @block.sync
            def _(e):
                for f in streams["sync"]:
                    f(e)

            @block.tensor
            def _(e):
                for f in streams["tensor"]:
                    f(e)

            @block.vector
            def _(e):
                for f in streams["vector"]:
                    f(e)

            @block.scalar
            def _(e):
                for f in streams["scalar"]:
                    f(e)

            @block.gpsimd
            def _(e):
                for f in streams["gpsimd"]:
                    f(e)
    return nc


def _host_layout(inputs):
    x = np.ascontiguousarray(inputs["L0_attn_x"])
    wqkvT = np.ascontiguousarray(inputs["L0_attn_w_qkv"].T)
    wprojT = np.ascontiguousarray(inputs["L0_attn_w_proj"].T)
    qgc = np.ascontiguousarray(
        (inputs["L0_attn_q_gain"].astype(np.float32) *
         np.float32(INV_SQRT_HD)).reshape(16, 1))

    def u8T(a):
        v = np.ascontiguousarray(a).view(np.uint8).reshape(
            a.shape[0], a.shape[1], 4)[:, :, 0]
        return np.ascontiguousarray(v.T)
    wqgT = u8T(inputs["L0_ffn_wq_gate"])
    wquT = u8T(inputs["L0_ffn_wq_up"])
    wqdT = u8T(inputs["L0_ffn_wq_down"])
    sg_c = np.ascontiguousarray(inputs["L0_ffn_s_gate"].reshape(44, 128).T)
    su_c = np.ascontiguousarray(inputs["L0_ffn_s_up"].reshape(44, 128).T)
    sd_c = np.ascontiguousarray(inputs["L0_ffn_s_down"].reshape(16, 128).T)
    l1wT = np.ascontiguousarray(inputs["L1_attn_Wqkv_w"].T)
    l1bias = np.ascontiguousarray(inputs["L1_attn_Wqkv_b"].reshape(18, 128).T)
    l1owT = np.ascontiguousarray(inputs["L1_attn_out_w"].T)
    l1ob = np.ascontiguousarray(inputs["L1_attn_out_b"].reshape(16, 128).T)
    wfcT = np.ascontiguousarray(inputs["L1_ffn_w_fc"].T)
    wprT = np.ascontiguousarray(inputs["L1_ffn_w_proj"].T)

    ident = np.eye(128, dtype=np.float32).astype(bf)
    aone = np.zeros((128, 64), dtype=np.float32)
    aone[:, 32] = 1.0
    aone = aone.astype(bf)
    onesrow = np.zeros((65, 128), dtype=np.float32)
    onesrow[[0, 32, 64], :] = 1.0
    onesrow = onesrow.astype(bf)
    negrow = np.zeros((65, 128), dtype=np.float32)
    negrow[[0, 32, 64], :] = -1.0
    negrow = negrow.astype(bf)
    ceps = np.zeros((128, 2), dtype=np.float32)
    ceps[:, 0] = LN_EPS
    ceps[:, 1] = RMS_EPS

    inv = 1.0 / (10000.0 ** (np.arange(0, 128, 2, dtype=np.float64) / 128.0))
    in_maps, pos_list = [], []
    for core in range(8):
        b, p = core // 2, core % 2
        pos = np.concatenate([np.arange(128) + (2 * j + p) * 128
                              for j in range(8)])
        pos_list.append(pos)
        xc = np.ascontiguousarray(x[b][pos].T)
        fr = np.outer(pos.astype(np.float64), inv)
        c64 = np.cos(fr).T.astype(np.float32)
        s64 = np.sin(fr).T.astype(np.float32)
        cosx = np.ascontiguousarray(
            np.concatenate([c64, c64], axis=0)).astype(bf)
        sinx = np.ascontiguousarray(
            np.concatenate([s64, -s64], axis=0)).astype(bf)
        lens = np.zeros((128, 8), dtype=np.float32)
        for j in range(8):
            lens[:, j] = (2 * j + p) * 128 + np.arange(128) + 1
        lens2 = np.maximum(lens - 1024.0, 0.0).astype(np.float32)
        in_maps.append(dict(
            x0=xc, wqkvT=wqkvT, wprojT=wprojT, qgc=qgc,
            wqgT=wqgT, wquT=wquT, wqdT=wqdT,
            sg_c=sg_c, su_c=su_c, sd_c=sd_c,
            l1wT=l1wT, l1bias=l1bias, l1owT=l1owT, l1ob=l1ob,
            wfcT=wfcT, wprT=wprT,
            cosx=cosx, sinx=sinx, lens_in=lens, lens2_in=lens2,
            ident_in=ident, aone_in=aone, onesrow_in=onesrow,
            negrow_in=negrow, ceps_in=ceps,
        ))
    return in_maps, pos_list


def kernel(**inputs):
    in_maps, pos_list = _host_layout(inputs)
    nc = build_program()
    lower_extended_insts(nc)
    res = run_bass_kernel_spmd(nc, in_maps, core_ids=list(range(8)))
    out = np.empty((4, 2048, 2048), dtype=np.float32)
    for core in range(8):
        b = core // 2
        out[b, pos_list[core], :] = np.asarray(res.results[core]["out"]).T
    return out


# version 06
# speedup vs baseline: 1.1401x; 1.0099x over previous
import numpy as np
import ml_dtypes
import concourse.bass as bass
import concourse.mybir as mybir
from concourse.bass_utils import run_bass_kernel_spmd
from concourse.library_overlay import lower_extended_insts
from contextlib import ExitStack

F32 = mybir.dt.float32
BF16 = mybir.dt.bfloat16
U8 = mybir.dt.uint8
FP8 = mybir.dt.float8e4
AF = mybir.ActivationFunctionType
ALU = mybir.AluOpType
bf = ml_dtypes.bfloat16

T = 1024
CH = 16
D = 2048
RMS_EPS = 1.1920929e-07
LN_EPS = 1e-5
CROUND = 12582912.0
POSBIG = 3.4028235e38
INV_SQRT_HD = 0.08838834764831845

N_BLOCKS = 4
PHASE_LIMIT = 99
ENGS = ("sync", "tensor", "vector", "scalar", "gpsimd")


class Ctr:
    def __init__(self, nc, es, name):
        self.sem = es.enter_context(nc.semaphore(name))
        self.v = 0

    def bump(self, n=1):
        self.v += n
        return self.v


def _bank_splits(lo, hi):
    outl = []
    c = lo
    while c < hi:
        nxt = min((c // 512 + 1) * 512, hi)
        outl.append((c, nxt))
        c = nxt
    return outl


def build_program():
    nc = bass.Bass("TRN2", target_bir_lowering=False)
    es = ExitStack()
    streams = {e: [] for e in ENGS}

    dram = lambda n, s, d, k="ExternalInput": nc.dram_tensor(n, s, d, kind=k).ap()
    x0 = dram("x0", [D, T], F32)
    wqkvT = dram("wqkvT", [D, 3072], F32)
    wprojT = dram("wprojT", [D, D], F32)
    qgc = dram("qgc", [16, 1], F32)
    wqgT = dram("wqgT", [D, 5632], U8)
    wquT = dram("wquT", [D, 5632], U8)
    wqdT = dram("wqdT", [5632, D], U8)
    sg_c = dram("sg_c", [128, 44], F32)
    su_c = dram("su_c", [128, 44], F32)
    sd_c = dram("sd_c", [128, 16], F32)
    l1wT = dram("l1wT", [D, 2304], F32)
    l1bias = dram("l1bias", [128, 18], F32)
    l1owT = dram("l1owT", [D, D], F32)
    l1ob = dram("l1ob", [128, 16], F32)
    wfcT = dram("wfcT", [D, 8192], F32)
    wprT = dram("wprT", [8192, D], F32)
    cosx = dram("cosx", [128, T], BF16)
    sinx = dram("sinx", [128, T], BF16)
    lens_in = dram("lens_in", [128, 8], F32)
    lens2_in = dram("lens2_in", [128, 8], F32)
    ident_in = dram("ident_in", [128, 128], BF16)
    aone_in = dram("aone_in", [128, 64], BF16)
    onesrow_in = dram("onesrow_in", [65, 128], BF16)
    negrow_in = dram("negrow_in", [65, 128], BF16)
    ceps_in = dram("ceps_in", [128, 2], F32)

    out_t = dram("out", [D, T], F32, "ExternalOutput")
    xs = dram("xs", [D, T], F32, "Internal")
    u_dram = dram("u_dram", [D, T], BF16, "Internal")
    h_dram = dram("h_dram", [8192, T], F32, "Internal")
    scr128 = dram("scr128", [128, 1], F32, "Internal")
    ccA_in = dram("ccA_in", [1, 1024 * 1024], BF16, "Internal")
    ccA_out = dram("ccA_out", [1, 2 * 1024 * 1024], BF16, "Internal")
    ccB_in = dram("ccB_in", [1, 2 * 128 * 1024], BF16, "Internal")
    ccB_out = dram("ccB_out", [1, 4 * 128 * 1024], BF16, "Internal")
    ccm_in = dram("ccm_in", [1, 1], F32, "Internal")
    ccm_out = dram("ccm_out", [1, 1], F32, "Internal")
    cch_in = dram("cch_in", [1, 1], F32, "Internal")
    cch_out = dram("cch_out", [1, 1], F32, "Internal")

    def xview(t, c):
        return t.rearrange("(ch p) t -> ch p t", p=128)[c]

    sbuf = lambda n, s, d: es.enter_context(nc.sbuf_tensor(n, s, d))
    IDENT = sbuf("IDENT", [128, 128], BF16)
    AONE = sbuf("AONE", [128, 64], BF16)
    ONESROW = sbuf("ONESROW", [65, 128], BF16)
    NEGROW = sbuf("NEGROW", [65, 128], BF16)
    CEPS = sbuf("CEPS", [128, 2], F32)
    COSX = sbuf("COSX", [128, T], BF16)
    SINX = sbuf("SINX", [128, T], BF16)
    LENS = sbuf("LENS", [128, 8], F32)
    LENS2 = sbuf("LENS2", [128, 8], F32)
    QGC = sbuf("QGC", [16, 1], F32)
    SGC = sbuf("SGC", [128, 44], F32)
    SUC = sbuf("SUC", [128, 44], F32)
    SDC = sbuf("SDC", [128, 16], F32)
    L1BI = sbuf("L1BI", [128, 18], F32)
    L1OB = sbuf("L1OB", [128, 16], F32)
    WRAW = sbuf("WRAW", [128, 1024], F32)
    WU8 = sbuf("WU8", [128, 2048], U8)
    WBF = sbuf("WBF", [128, 16384], BF16)
    SQ = sbuf("SQ", [128, 2048], BF16)
    XCH = sbuf("XCH", [128, 1024], F32)
    XC2 = sbuf("XC2", [128, 1024], F32)
    ROWS = sbuf("ROWS", [128, 3072], F32)
    BROWS = sbuf("BROWS", [128, 2048], BF16)
    SC2 = sbuf("SC2", [128, 32], F32)
    RMAX = sbuf("RMAX", [1, 256], F32)
    TERN = sbuf("TERN", [16, 512], F32)
    TERNB = sbuf("TERNB", [16, 1536], BF16)
    VW = sbuf("VW", [128, 1024], BF16)
    RT = sbuf("RT", [128, 2048], BF16)
    SMF = sbuf("SMF", [128, 2048], F32)
    XSLOT2 = None
    PT = sbuf("PT", [128, 1024], BF16)
    SW = sbuf("SW", [128, 1024], BF16)
    SCL = sbuf("SCL", [128, 16], F32)
    BIG = sbuf("BIG", [128, 57344], BF16)

    W8 = WBF[:, 0:8192].bitcast(FP8)
    XN = BIG[:, 0:16384]
    QB = BIG[:, 16384:32768]
    KG = BIG[:, 32768:40960]
    VG = BIG[:, 40960:49152]
    PPB = BIG[:, 49152:57344]
    XQ8 = BIG[:, 0:8192].bitcast(FP8)
    H8 = BIG[:, 8192:40960].bitcast(FP8)

    XSLOT = [XCH, None]

    def tcol(hc):
        if hc < 40:
            return BIG, 16384 + hc * 1024
        if hc < 42:
            return SQ, (hc - 40) * 1024
        return RT, (hc - 42) * 1024

    PS = es.enter_context(nc.psum_tensor("PS", [128, 4096], F32))

    def xsl(c):
        return XCH[:] if c % 2 == 0 else SMF[:, 1024:2048]

    def usl(c):
        return SW[:] if c % 2 == 0 else RT[:, 0:1024]

    SY = Ctr(nc, es, "SY")
    GD = Ctr(nc, es, "GD")
    TE = Ctr(nc, es, "TE")
    DV = Ctr(nc, es, "DV")
    SC = Ctr(nc, es, "SC")
    CTRS = {"sync": SY, "gpsimd": GD, "tensor": TE, "vector": DV, "scalar": SC}

    channels = {}

    def chan(name):
        if name not in channels:
            channels[name] = Ctr(nc, es, "ch_" + name)
        return channels[name]

    def dma(queue, dst, src, wait=(), ch="g"):
        ctr = chan(ch)
        tok = ctr.bump(16)
        wl = [(c.sem, v) for (c, v) in wait if v > 0]

        def f(e, dst=dst, src=src, wl=wl):
            for (s, v) in wl:
                e.wait_ge(s, v)
            e.dma_start(out=dst, in_=src).then_inc(ctr.sem, 16)
        streams[queue].append(f)
        return (ctr, tok)

    def op(eng, fn, wait=(), inc=1):
        ctr = CTRS[eng]
        tok = ctr.bump(inc) if inc else ctr.v
        wl = [(c.sem, v) for (c, v) in wait if v > 0]

        def f(e, fn=fn, wl=wl, inc=inc):
            for (s, v) in wl:
                e.wait_ge(s, v)
            inst = fn(e)
            if inc:
                inst.then_inc(ctr.sem, inc)
        streams[eng].append(f)
        return tok

    def barrier():
        allc = list(CTRS.items()) + [("ch_" + k, c) for k, c in channels.items()]
        toks = [(c.sem, c.v) for _, c in allc if c.v > 0]
        for eng in ENGS:
            wl = [(s, v) for (s, v) in toks if s is not CTRS[eng].sem]

            def f(e, wl=wl):
                for (s, v) in wl:
                    e.wait_ge(s, v)
            streams[eng].append(f)

    dma("sync", IDENT[:], ident_in, ch="con")
    dma("sync", AONE[:], aone_in, ch="con")
    dma("sync", ONESROW[:], onesrow_in, ch="con")
    dma("sync", NEGROW[:], negrow_in, ch="con")
    dma("sync", CEPS[:], ceps_in, ch="con")
    dma("sync", COSX[:], cosx, ch="con")
    dma("sync", SINX[:], sinx, ch="con")
    dma("sync", LENS[:], lens_in, ch="con")
    dma("sync", LENS2[:], lens2_in, ch="con")
    dma("sync", QGC[:], qgc, ch="con")
    dma("sync", SGC[:], sg_c, ch="con")
    dma("sync", SUC[:], su_c, ch="con")
    dma("sync", SDC[:], sd_c, ch="con")
    dma("sync", L1BI[:], l1bias, ch="con")
    dma("sync", L1OB[:], l1ob, ch="con")
    barrier()

    def ln_stats_hbm(src_hbm, xb_dst):
        sq_te = [0, 0]
        sc_slot = [SC.v, SC.v]
        for c in range(CH):
            t_x = dma("sync", xsl(c), xview(src_hbm, c),
                      wait=[(SC, sc_slot[c % 2])], ch="xch%d" % (c % 2))
            sc_tok = op("scalar", lambda e, c=c: e.copy(
                xb_dst[:, c * T:(c + 1) * T], xsl(c)), wait=[t_x])
            sc_slot[c % 2] = sc_tok
            dv_tok = op("vector", lambda e, c=c: e.tensor_mul(
                SQ[:, (c % 2) * 1024:(c % 2) * 1024 + 1024],
                xb_dst[:, c * T:(c + 1) * T], xb_dst[:, c * T:(c + 1) * T]),
                wait=[(SC, sc_tok), (TE, sq_te[c % 2])])
            last_dv = dv_tok
            for hf in range(2):
                lo = 3072 + hf * 512
                op("tensor", lambda e, c=c, hf=hf, lo=lo: e.matmul(
                    PS[0:1, lo:lo + 512], AONE[:, 32:33],
                    xb_dst[:, c * T + hf * 512:c * T + hf * 512 + 512],
                    start=(c == 0), stop=(c == CH - 1)), wait=[(SC, sc_tok)])
                op("tensor", lambda e, c=c, hf=hf, lo=lo: e.matmul(
                    PS[32:33, lo:lo + 512], AONE[:, 32:33],
                    SQ[:, (c % 2) * 1024 + hf * 512:(c % 2) * 1024 + hf * 512 + 512],
                    start=(c == 0), stop=(c == CH - 1)), wait=[(DV, dv_tok)])
            sq_te[c % 2] = TE.v
        return TE.v

    def ln_stats_sbuf(cols):
        sq_te = [0, 0]
        for c in range(CH):
            tns, off = cols(c)
            dv_tok = op("vector", lambda e, tns=tns, off=off, c=c: e.tensor_mul(
                SQ[:, (c % 2) * 1024:(c % 2) * 1024 + 1024],
                tns[:, off:off + T], tns[:, off:off + T]),
                wait=[(TE, sq_te[c % 2])])
            for hf in range(2):
                lo = 3072 + hf * 512
                op("tensor", lambda e, tns=tns, off=off, c=c, hf=hf, lo=lo: e.matmul(
                    PS[0:1, lo:lo + 512], AONE[:, 32:33],
                    tns[:, off + hf * 512:off + hf * 512 + 512],
                    start=(c == 0), stop=(c == CH - 1)))
                op("tensor", lambda e, c=c, hf=hf, lo=lo: e.matmul(
                    PS[32:33, lo:lo + 512], AONE[:, 32:33],
                    SQ[:, (c % 2) * 1024 + hf * 512:(c % 2) * 1024 + hf * 512 + 512],
                    start=(c == 0), stop=(c == CH - 1)), wait=[(DV, dv_tok)])
            sq_te[c % 2] = TE.v
        return TE.v

    def ln_stats_udram(src_hbm_u):
        te_slot = [TE.v, TE.v]
        for c in range(CH):
            t_u = dma("sync", usl(c), xview(src_hbm_u, c),
                      wait=[(TE, te_slot[c % 2])], ch="sw%d" % (c % 2))
            dv_sq = op("vector", lambda e, c=c: e.tensor_mul(
                SQ[:, (c % 2) * 1024:(c % 2) * 1024 + 1024], usl(c), usl(c)),
                wait=[t_u])
            for hf in range(2):
                lo = 3072 + hf * 512
                op("tensor", lambda e, c=c, hf=hf, lo=lo: e.matmul(
                    PS[0:1, lo:lo + 512], AONE[:, 32:33],
                    usl(c)[:, hf * 512:hf * 512 + 512],
                    start=(c == 0), stop=(c == CH - 1)), wait=[t_u])
                op("tensor", lambda e, c=c, hf=hf, lo=lo: e.matmul(
                    PS[32:33, lo:lo + 512], AONE[:, 32:33],
                    SQ[:, (c % 2) * 1024 + hf * 512:(c % 2) * 1024 + hf * 512 + 512],
                    start=(c == 0), stop=(c == CH - 1)), wait=[(DV, dv_sq)])
            te_slot[c % 2] = TE.v
        return TE.v

    def ln_smalls(te_stats, fp32_bcast=False):
        def f1(e):
            e.tensor_scalar(out=ROWS[0:1, 0:1024], in0=PS[0:1, 3072:4096],
                            scalar1=1.0 / D, scalar2=None, op0=ALU.mult)
            e.tensor_scalar(out=ROWS[0:1, 1024:2048], in0=PS[32:33, 3072:4096],
                            scalar1=1.0 / D, scalar2=None, op0=ALU.mult)
            e.drain()
            e.tensor_mul(ROWS[0:1, 2048:3072], ROWS[0:1, 0:1024],
                         ROWS[0:1, 0:1024])
            e.drain()
            return e.tensor_sub(ROWS[32:33, 0:1024], ROWS[0:1, 1024:2048],
                                ROWS[0:1, 2048:3072])
        d1 = op("vector", f1, wait=[(TE, te_stats)])
        s1 = op("scalar", lambda e: e.activation(
            out=ROWS[32:33, 1024:2048], in_=ROWS[32:33, 0:1024], func=AF.Sqrt,
            bias=CEPS[32:33, 0:1], scale=1.0), wait=[(DV, d1)])

        def f2(e):
            e.reciprocal(ROWS[0:1, 2048:3072], ROWS[32:33, 1024:2048])
            e.drain()
            e.tensor_mul(ROWS[32:33, 0:1024], ROWS[0:1, 0:1024],
                         ROWS[0:1, 2048:3072])
            e.drain()
            e.tensor_copy(BROWS[0:1, 0:1024], ROWS[0:1, 2048:3072])
            e.tensor_copy(BROWS[32:33, 0:1024], ROWS[32:33, 0:1024])
            if fp32_bcast:
                e.drain()
                e.tensor_copy(ROWS[0:1, 1024:2048], ROWS[32:33, 0:1024])
                e.tensor_copy(SMF[0:1, 0:2048].bitcast(BF16)[:, 0:1024],
                              ROWS[32:33, 0:1024])
                e.drain()
                e.tensor_sub(ROWS[64:65, 0:1024], ROWS[0:1, 2048:3072],
                             BROWS[0:1, 0:1024])
                e.tensor_sub(ROWS[64:65, 1024:2048], ROWS[0:1, 1024:2048],
                             SMF[0:1, 0:2048].bitcast(BF16)[:, 0:1024])
                e.drain()
                e.tensor_copy(BROWS[0:1, 1024:2048], ROWS[64:65, 0:1024])
                return e.tensor_copy(
                    SMF[0:1, 0:2048].bitcast(BF16)[:, 1024:2048],
                    ROWS[64:65, 1024:2048])
            return e.engine_nop()
        d2 = op("vector", f2, wait=[(SC, s1)])
        for hf in range(2):
            sl = slice(hf * 512, hf * 512 + 512)
            sl2 = slice(1024 + hf * 512, 1024 + hf * 512 + 512)
            op("tensor", lambda e, sl=sl: e.matmul(
                PS[:, sl], ONESROW[0:1, :], BROWS[0:1, sl],
                start=True, stop=not fp32_bcast), wait=[(DV, d2)])
            if fp32_bcast:
                op("tensor", lambda e, sl=sl: e.matmul(
                    PS[:, sl], ONESROW[0:1, :],
                    BROWS[0:1, 1024 + sl.start:1024 + sl.start + 512],
                    start=False, stop=True))
            if not fp32_bcast:
                op("tensor", lambda e, sl=sl, sl2=sl2: e.matmul(
                    PS[:, sl2], NEGROW[32:33, :], BROWS[32:33, sl],
                    start=True, stop=True))
            else:
                op("tensor", lambda e, sl=sl, sl2=sl2: e.matmul(
                    PS[:, sl2], NEGROW[0:1, :],
                    SMF[0:1, 0:2048].bitcast(BF16)[:, sl],
                    start=True, stop=False))
                op("tensor", lambda e, sl=sl, sl2=sl2: e.matmul(
                    PS[:, sl2], NEGROW[0:1, :],
                    SMF[0:1, 0:2048].bitcast(BF16)[:, 1024 + sl.start:
                                                   1024 + sl.start + 512],
                    start=False, stop=True))
        return TE.v

    def ln_apply(src_hbm, te_bc, dst_bf):
        last = 0
        dv_slot = [DV.v, DV.v]
        sc_gate = SC.v
        for c in range(CH):
            t_x = dma("sync", xsl(c), xview(src_hbm, c),
                      wait=[(DV, dv_slot[c % 2])] +
                           ([(SC, sc_gate)] if c < 2 else []),
                      ch="xch%d" % (c % 2))

            def f(e, c=c):
                e.tensor_mul(XC2[:], xsl(c), PS[:, 0:1024])
                e.drain()
                return e.tensor_add(dst_bf[:, c * T:(c + 1) * T],
                                    XC2[:], PS[:, 1024:2048])
            last = op("vector", f, wait=[t_x, (TE, te_bc), (DV, last)])
            dv_slot[c % 2] = last
        return last

    def ln2_residual(u_cols, src_hbm, dst_hbm, te_bc, u_hbm=None):
        last_store = None
        dv_uslot = [DV.v, DV.v]
        dv_xslot = [DV.v, DV.v]
        te_gate = TE.v
        sc_gate = SC.v
        for c in range(CH):
            if u_hbm is not None:
                t_u = dma("sync", usl(c), xview(u_hbm, c),
                          wait=[(DV, dv_uslot[c % 2])] +
                               ([(TE, te_gate)] if c < 2 else []),
                          ch="sw%d" % (c % 2))
                usrc = usl(c)
                wl_extra = [t_u]
            else:
                tns, off = u_cols(c)
                usrc = tns[:, off:off + T]
                wl_extra = []
            t_x = dma("sync", xsl(c), xview(src_hbm, c),
                      wait=[(DV, dv_xslot[c % 2])] +
                           ([(SC, sc_gate)] if c < 2 else []),
                      ch="xch%d" % (c % 2))

            def f(e, usrc=usrc, c=c):
                e.tensor_mul(XC2[:], usrc, PS[:, 0:1024])
                e.drain()
                e.tensor_add(XC2[:], XC2[:], PS[:, 1024:2048])
                e.drain()
                return e.tensor_add(XC2[:], XC2[:], xsl(c))
            d = op("vector", f, wait=[t_x, (TE, te_bc), (DV, DV.v)] +
                   wl_extra + ([last_store] if last_store else []))
            dv_uslot[c % 2] = d
            dv_xslot[c % 2] = d
            last_store = dma("gpsimd", xview(dst_hbm, c), XC2[:],
                             wait=[(DV, d)], ch="xst")
        return last_store

    def tern_prep(w_hbm, s_off, slot0, mean_free_dv=0, wbf_free_te=0):
        stop_at = PHASE_LIMIT
        """Ternary prep of a 512-col superchunk into WBF slots slot0..+15.
        mean_free_dv: PS bank4 free (prev superchunk's v-ops done).
        wbf_free_te: WBF slots free (matmuls of 2-ago superchunk done)."""
        sc_toks = []
        te_means = [TE.v, TE.v]
        sc_gate = SC.v
        for g in range(16):
            t_w = dma("sync", WRAW[:, (g % 2) * 512:(g % 2) * 512 + 512],
                      w_hbm[g * 128:(g + 1) * 128, s_off:s_off + 512],
                      wait=[(SC, sc_toks[g - 2] if g >= 2 else sc_gate)],
                      ch="wr%d" % (g % 2))

            def fc(e, g=g):
                sl = slice((g % 2) * 512, (g % 2) * 512 + 512)
                e.copy(WBF[:, (slot0 + g) * 512:(slot0 + g) * 512 + 512],
                       WRAW[:, sl])
                return e.activation(out=SW[:, (g % 2) * 512:(g % 2) * 512 + 512],
                                    in_=WRAW[:, sl], func=AF.Abs)
            st = op("scalar", fc,
                    wait=[t_w, (TE, te_means[g % 2])] +
                         ([(TE, wbf_free_te)] if g == 0 else []))
            sc_toks.append(st)
            op("tensor", lambda e, g=g: e.matmul(
                PS[0:16, 2048:2560], AONE[:, 32 - g:48 - g],
                SW[:, (g % 2) * 512:(g % 2) * 512 + 512],
                start=(g == 0), stop=(g == 15)),
                wait=[(SC, st)] + ([(DV, mean_free_dv)] if g == 0 else []))
            te_means[g % 2] = TE.v

        def fsm(e):
            e.tensor_scalar(out=TERN[0:16, 0:512], in0=PS[0:16, 2048:2560],
                            scalar1=1.0 / 128.0, scalar2=None, op0=ALU.mult)
            e.drain()
            e.tensor_copy(TERNB[0:16, 0:512], TERN[0:16, 0:512])
            e.drain()
            e.tensor_scalar(out=TERNB[0:16, 0:512], in0=TERNB[0:16, 0:512],
                            scalar1=9.99999993922529e-09, scalar2=None,
                            op0=ALU.max)
            e.drain()
            e.reciprocal(TERN[0:16, 0:512], TERNB[0:16, 0:512])
            e.drain()
            e.tensor_copy(TERNB[0:16, 512:1024], TERN[0:16, 0:512])
            e.drain()
            e.tensor_sub(ROWS[32:48, 0:512], TERN[0:16, 0:512],
                         TERNB[0:16, 512:1024])
            e.drain()
            return e.tensor_copy(TERNB[0:16, 1024:1536], ROWS[32:48, 0:512])
        if stop_at <= 1.1:
            return SC.v and DV.v
        dsm = op("vector", fsm, wait=[(TE, TE.v)])
        if stop_at <= 1.15:
            return dsm

        dv_last = dsm
        stg_te = TE.v
        for g in range(16):
            gs = dma("gpsimd", BROWS[0:1, 0:512],
                     TERNB[g:g + 1, 0:512],
                     wait=[(DV, dsm), (TE, stg_te)], ch="stg0")
            gs = dma("gpsimd", BROWS[0:1, 512:1024],
                     TERNB[g:g + 1, 512:1024], ch="stg0")
            gs = dma("gpsimd", BROWS[0:1, 1024:1536],
                     TERNB[g:g + 1, 1024:1536], ch="stg0")
            if stop_at <= 1.18:
                continue
            op("tensor", lambda e: e.matmul(
                PS[:, 2048:2560], ONESROW[0:1, :], BROWS[0:1, 512:1024],
                start=True, stop=False), wait=[gs, (DV, dv_last)])
            op("tensor", lambda e: e.matmul(
                PS[:, 2048:2560], ONESROW[0:1, :], BROWS[0:1, 1024:1536],
                start=False, stop=True))
            op("tensor", lambda e: e.matmul(
                PS[:, 2560:3072], ONESROW[0:1, :], BROWS[0:1, 0:512],
                start=True, stop=True))
            stg_te = TE.v
            if stop_at <= 1.19:
                continue
            te_bc = TE.v

            def fv(e, g=g):
                wsl = slice((slot0 + g) * 512, (slot0 + g) * 512 + 512)
                e.tensor_mul(VW[:, 0:512], WBF[:, wsl], PS[:, 2048:2560])
                e.drain()
                e.tensor_scalar(out=VW[:, 512:1024], in0=VW[:, 0:512],
                                scalar1=-1.25, scalar2=1.25,
                                op0=ALU.max, op1=ALU.min)
                e.drain()
                e.tensor_scalar(out=VW[:, 0:512], in0=VW[:, 512:1024],
                                scalar1=CROUND, scalar2=CROUND,
                                op0=ALU.add, op1=ALU.subtract)
                e.drain()
                return e.tensor_mul(WBF[:, wsl], VW[:, 0:512],
                                    PS[:, 2560:3072])
            dv_last = op("vector", fv, wait=[(TE, te_bc)])
        if stop_at <= 1.19:
            return dsm
        return dv_last

    def cast_prep(w_hbm, s_off, slot0, ncols=512, wbf_free_te=0):
        toks = [SC.v, SC.v]
        for g in range(16):
            t_w = dma("sync", WRAW[:, (g % 2) * 512:(g % 2) * 512 + ncols],
                      w_hbm[g * 128:(g + 1) * 128, s_off:s_off + ncols],
                      wait=[(SC, toks[g % 2])], ch="wr%d" % (g % 2))
            st = op("scalar", lambda e, g=g: e.copy(
                WBF[:, (slot0 + g) * 512:(slot0 + g) * 512 + ncols],
                WRAW[:, (g % 2) * 512:(g % 2) * 512 + ncols]),
                wait=[t_w] + ([(TE, wbf_free_te)] if g == 0 else []))
            toks[g % 2] = st
        return st

    def rowmax_to_scalar(scl_col, sc2_col, wait_dv):
        g1 = dma("gpsimd", scr128, SCL[:, scl_col:scl_col + 1],
                 wait=[(DV, wait_dv)], ch="msc")
        g2 = dma("gpsimd", RMAX[0:1, 0:128],
                 scr128.rearrange("p one -> (one) (p)"), wait=[g1], ch="msc")
        d = op("vector", lambda e: e.tensor_reduce(
            out=SC2[0:1, sc2_col:sc2_col + 1], in_=RMAX[0:1, 0:128],
            axis=mybir.AxisListType.X, op=ALU.max), wait=[g2])
        return d

    def bcast_scalars_f32(sc2_srcs, scl_dst0, wait_dv):
        n = len(sc2_srcs)

        def fhl(e):
            for i, c in enumerate(sc2_srcs):
                e.tensor_copy(BROWS[0:1, 1536 + i:1537 + i],
                              SC2[0:1, c:c + 1])
            e.drain()
            for i, c in enumerate(sc2_srcs):
                e.tensor_sub(SC2[0:1, 24 + i:25 + i], SC2[0:1, c:c + 1],
                             BROWS[0:1, 1536 + i:1537 + i])
            e.drain()
            for i in range(n):
                e.tensor_copy(TERNB[0:1, i:i + 1],
                              SC2[0:1, 24 + i:25 + i])
            return e.engine_nop()
        d_hl = op("vector", fhl, wait=[(DV, wait_dv)])
        for i in range(n):
            op("tensor", lambda e, i=i: e.matmul(
                PS[:, 2048 + i:2048 + i + 1], ONESROW[0:1, :],
                BROWS[0:1, 1536 + i:1537 + i], start=True, stop=False),
                wait=[(DV, d_hl)] if i == 0 else [])
            op("tensor", lambda e, i=i: e.matmul(
                PS[:, 2048 + i:2048 + i + 1], ONESROW[0:1, :],
                TERNB[0:1, i:i + 1], start=False, stop=True))
        d_sc = op("vector", lambda e: e.tensor_copy(
            SCL[:, scl_dst0:scl_dst0 + n], PS[:, 2048:2048 + n]),
            wait=[(TE, TE.v)])
        return d_sc

    def attention(kv_of_head, qsrc, ysrc):
        sbase = [2048, 3072]
        tail = [None]
        sub_tmr = {}
        last_exp = [0, 0]
        last_pt_te = [0, 0]
        last_yev = [0]
        prev_diag_dv = [0]
        unit_ctr = [0]

        def smf_ap(ph, s, ln):
            if ph == 0:
                return SMF[:, s * 1024:s * 1024 + ln]
            return (XCH[:, 0:ln] if s == 0 else XC2[:, 0:ln])

        def emit_head(hidx, j, h, kv):
            L2 = (j + 1) * 256
            nsub = 1 if L2 <= 1024 else 2
            ph = hidx % 2
            tmr_toks = []
            for s in range(nsub):
                lo = s * 1024
                ln = min(1024, L2 - lo)
                buf = unit_ctr[0] % 2
                unit_ctr[0] += 1
                w_sbuf = sub_tmr.get(buf, 0)
                first = True
                for (a, b) in _bank_splits(sbase[buf], sbase[buf] + ln):
                    op("tensor", lambda e, j=j, h=h, kv=kv, a=a, b=b, lo=lo, buf=buf: e.matmul(
                        PS[:, a:b],
                        qsrc[:, h * T + j * 128:h * T + j * 128 + 128],
                        KG[:, kv * 2048 + lo + (a - sbase[buf]):
                           kv * 2048 + lo + (b - sbase[buf])],
                        start=True, stop=True),
                        wait=([(DV, w_sbuf)] if first else []))
                    first = False
                te_s = TE.v
                lens_ap = LENS if s == 0 else LENS2

                def fm(e, j=j, s=s, lo=lo, ln=ln, buf=buf, nsub=nsub,
                       lens_ap=lens_ap, ph=ph):
                    if s > 0:
                        e.drain()
                    e.tensor_mask_reduce(
                        out=smf_ap(ph, s, ln), in_=PS[:, sbase[buf]:sbase[buf] + ln],
                        mask_start=0.0, mask_end=lens_ap[:, j:j + 1], scale=-0.5,
                        accum_in=(POSBIG if s == 0 else SCL[:, 3 * ph:3 * ph + 1]),
                        op=ALU.min, accum_out=SCL[:, 3 * ph:3 * ph + 1])
                    if s == nsub - 1:
                        e.drain()
                        return e.tensor_scalar(
                            out=SCL[:, 1 + 3 * ph:2 + 3 * ph],
                            in0=SCL[:, 3 * ph:3 * ph + 1],
                            scalar1=2.0, scalar2=None, op0=ALU.mult)
                    return e.engine_nop()
                dv_m = op("vector", fm,
                          wait=[(TE, te_s)] +
                               ([(SC, last_exp[ph])] if s == 0 else []))
                sub_tmr[(unit_ctr[0] - 1) % 2] = dv_m
                tmr_toks.append(dv_m)
            sc_e = 0
            for s in range(nsub):
                lo = s * 1024
                ln = min(1024, L2 - lo)
                sc_e = op("scalar", lambda e, lo=lo, ln=ln, s=s, ph=ph: e.activation(
                    out=PPB[:, ph * 4096 + lo:ph * 4096 + lo + ln],
                    in_=smf_ap(ph, s, ln), func=AF.Exp,
                    bias=SCL[:, 1 + 3 * ph:2 + 3 * ph], scale=-2.0,
                    accum_out=(SCL[:, 2 + 3 * ph:3 + 3 * ph] if s == 0 else
                               SCL[:, 8 + ph:9 + ph])),
                    wait=[(DV, tmr_toks[-1])] +
                         ([(TE, last_pt_te[ph])] if s == 0 else []))
            last_exp[ph] = sc_e
            return (j, h, kv, L2, ph, sc_e, nsub)

        def emit_tail(args):
            j, h, kv, L2, ph, sc_e, nsub = args

            def fd(e):
                if nsub == 2:
                    e.tensor_add(SCL[:, 2 + 3 * ph:3 + 3 * ph],
                                 SCL[:, 2 + 3 * ph:3 + 3 * ph],
                                 SCL[:, 8 + ph:9 + ph])
                    e.drain()
                e.reciprocal(SCL[:, 6 + ph:7 + ph],
                             SCL[:, 2 + 3 * ph:3 + 3 * ph])
                e.drain()
                return e.tensor_scalar(out=VW[:, 0:128], in0=IDENT[:],
                                       scalar1=SCL[:, 6 + ph:7 + ph],
                                       scalar2=None, op0=ALU.mult)
            dv_d = op("vector", fd, wait=[(SC, sc_e), (DV, prev_diag_dv[0])])
            prev_diag_dv[0] = dv_d
            nb = L2 // 128
            for b4 in range(0, nb, 4):
                bb = min(4, nb - b4)
                bank = (b4 // 4) % 2
                for b in range(b4, b4 + bb):
                    op("tensor", lambda e, b=b, b4=b4, bank=bank, ph=ph: e.matmul(
                        PS[:, bank * 512 + (b - b4) * 128:
                           bank * 512 + (b - b4) * 128 + 128],
                        PPB[:, ph * 4096 + b * 128:ph * 4096 + b * 128 + 128],
                        VW[:, 0:128], start=True, stop=True),
                        wait=([(DV, dv_d if b4 == 0 else DV.v)]
                              if b == b4 else []))
                te_pt = TE.v
                dv_e = op("vector", lambda e, bank=bank, bb=bb: e.tensor_copy(
                    PT[:, bank * 512:bank * 512 + bb * 128],
                    PS[:, bank * 512:bank * 512 + bb * 128]),
                    wait=[(TE, te_pt)])
                for b in range(b4, b4 + bb):
                    op("tensor", lambda e, b=b, b4=b4, bank=bank, kv=kv, nb=nb: e.matmul(
                        PS[:, 1024:1152],
                        VG[:, kv * 2048 + b * 128:kv * 2048 + b * 128 + 128],
                        PT[:, bank * 512 + (b - b4) * 128:
                           bank * 512 + (b - b4) * 128 + 128],
                        start=(b == 0), stop=(b == nb - 1)),
                        wait=([(DV, dv_e)] if b == b4 else
                              ([(DV, last_yev[0])] if b == 0 else [])))
            last_pt_te[ph] = TE.v
            last_yev[0] = op("vector", lambda e, j=j, h=h: e.tensor_copy(
                ysrc[:, h * T + j * 128:h * T + j * 128 + 128],
                PS[:, 1024:1152]), wait=[(TE, TE.v)])

        hidx = 0
        for j in range(8):
            for h in range(16):
                args = emit_head(hidx, j, h, kv_of_head(h))
                hidx += 1
                if tail[0] is not None:
                    emit_tail(tail[0])
                tail[0] = args
        emit_tail(tail[0])

    def rms_bcast_row(row_ap_bf, ps_lo, base=0, wait=()):
        for hf in range(2):
            op("tensor", lambda e, hf=hf: e.matmul(
                PS[:, ps_lo + hf * 512:ps_lo + hf * 512 + 512],
                ONESROW[base:base + 1, :],
                row_ap_bf[:, hf * 512:hf * 512 + 512], start=True, stop=True),
                wait=(list(wait) if hf == 0 else []))
        return TE.v

    wmax_state = {"d": 0, "first": {10: True, 11: True}, "done": {}}
    wmax_tiles = ([("fc", g, c0, 10) for g in range(16) for c0 in range(0, 8192, 1024)] +
                  [("pr", g, c0, 11) for g in range(64) for c0 in range(0, 2048, 1024)])

    def emit_wmax_tiles(n):
        cnt = 0
        while wmax_tiles and cnt < n:
            kind, g, c0, dst_scl = wmax_tiles.pop(0)
            wsrc = wfcT if kind == "fc" else wprT
            t_w = dma("sync", WRAW[:], wsrc[g * 128:(g + 1) * 128, c0:c0 + 1024],
                      wait=[(DV, wmax_state["d"])], ch="wr0")
            first = wmax_state["first"][dst_scl]
            wmax_state["first"][dst_scl] = False

            def f(e, first=first, dst_scl=dst_scl):
                e.tensor_reduce(out=SCL[:, 12:13], in_=WRAW[:],
                                axis=mybir.AxisListType.X, op=ALU.max,
                                apply_absolute_value=True)
                e.drain()
                if first:
                    return e.tensor_copy(SCL[:, dst_scl:dst_scl + 1],
                                         SCL[:, 12:13])
                return e.tensor_tensor(SCL[:, dst_scl:dst_scl + 1],
                                       SCL[:, dst_scl:dst_scl + 1],
                                       SCL[:, 12:13], op=ALU.max)
            wmax_state["d"] = op("vector", f, wait=[t_w])
            cnt += 1

    def finish_wmax():
        emit_wmax_tiles(10 ** 9)
        wmax_state["done"]["dwf"] = rowmax_to_scalar(10, 0, wmax_state["d"])
        wmax_state["done"]["dwp"] = rowmax_to_scalar(11, 1, wmax_state["d"])

    def block_l0_attn(src_hbm, dst_hbm):
        def phase_done(n):
            return PHASE_LIMIT <= n
        te_st = ln_stats_hbm(src_hbm, XN)
        te_bc = ln_smalls(te_st)
        xn_tok = ln_apply(src_hbm, te_bc, XN)
        barrier()
        if phase_done(1):
            return

        ccA_in_v = ccA_in.rearrange("a (k p c) -> a k p c", p=128, c=1024)
        prep_tok = tern_prep(wqkvT, 0, 0)
        if phase_done(1.2):
            barrier()
            return
        te_mm_end = [0, 0]
        ss_te = [0, 0]
        evac_tok = [0, 0]
        for s in range(6 if not phase_done(1.5) else 1):
            slot0 = (s % 2) * 16
            for o4 in range(4):
                oc = s * 4 + o4
                pb = (o4 % 2) * 1024
                for hf in range(2):
                    for g in range(CH):
                        wl = []
                        if g == 0 and hf == 0:
                            if o4 == 0:
                                wl = [(DV, prep_tok), (DV, xn_tok)]
                            wl = wl + [evac_tok[o4 % 2]] if evac_tok[o4 % 2] else wl
                        op("tensor", lambda e, slot0=slot0, g=g, o4=o4, hf=hf, pb=pb: e.matmul(
                            PS[:, pb + hf * 512:pb + hf * 512 + 512],
                            WBF[:, (slot0 + g) * 512 + o4 * 128:
                                (slot0 + g) * 512 + o4 * 128 + 128],
                            XN[:, g * T + hf * 512:g * T + hf * 512 + 512],
                            start=(g == 0), stop=(g == CH - 1)), wait=wl)
                te_mm = TE.v
                if oc < 16:
                    dst = QB[:, oc * T:(oc + 1) * T]
                elif oc < 20:
                    dst = KG[:, (oc - 16) * 2048:(oc - 16) * 2048 + 1024]
                else:
                    dst = PPB[:, (oc - 20) * 1024:(oc - 20) * 1024 + 1024]
                sc_ev = op("scalar", lambda e, dst=dst, pb=pb: e.copy(
                    dst, PS[:, pb:pb + 1024]), wait=[(TE, te_mm)])
                evac_tok[o4 % 2] = (SC, sc_ev)
                if oc < 20:
                    dv_sq = op("vector", lambda e, dst=dst, oc=oc: e.tensor_mul(
                        SQ[:, (oc % 2) * 1024:(oc % 2) * 1024 + 1024],
                        dst, dst), wait=[(SC, sc_ev), (TE, ss_te[oc % 2])])
                    for hf in range(2):
                        op("tensor", lambda e, oc=oc, hf=hf: e.matmul(
                            PS[0:32, 3072 + hf * 512:3072 + hf * 512 + 512],
                            AONE[:, 32 - oc:64 - oc],
                            SQ[:, (oc % 2) * 1024 + hf * 512:
                               (oc % 2) * 1024 + hf * 512 + 512],
                            start=(oc == 0), stop=(oc == 19)),
                            wait=[(DV, dv_sq)])
                    ss_te[oc % 2] = TE.v
            te_mm_end[s % 2] = TE.v
            if s < 5 and not phase_done(1.5):
                prep_tok = tern_prep(wqkvT, (s + 1) * 512, ((s + 1) % 2) * 16,
                                     mean_free_dv=prep_tok,
                                     wbf_free_te=te_mm_end[(s + 1) % 2])
        barrier()
        if phase_done(1.5):
            return
        if phase_done(2):
            return

        d_r = op("vector", lambda e: e.tensor_scalar(
            out=ROWS[0:20, 0:1024], in0=PS[0:20, 3072:4096],
            scalar1=1.0 / 128.0, scalar2=None, op0=ALU.mult))
        s_r = op("scalar", lambda e: e.activation(
            out=ROWS[32:52, 0:1024], in_=ROWS[0:20, 0:1024], func=AF.Sqrt,
            bias=CEPS[0:20, 1:2], scale=1.0), wait=[(DV, d_r)])

        def frms2(e):
            e.reciprocal(ROWS[0:20, 0:1024], ROWS[32:52, 0:1024])
            e.drain()
            e.tensor_scalar(out=ROWS[0:16, 0:1024], in0=ROWS[0:16, 0:1024],
                            scalar1=QGC[:, 0:1], scalar2=None, op0=ALU.mult)
            e.drain()
            return e.tensor_copy(BROWS[0:20, 1024:2048], ROWS[0:20, 0:1024])
        d_r2 = op("vector", frms2, wait=[(SC, s_r)])

        d_rope = d_r2
        te_b = 0
        for idx in range(20):
            if idx < 16:
                src_c = QB[:, idx * T:(idx + 1) * T]
            else:
                src_c = KG[:, (idx - 16) * 2048:(idx - 16) * 2048 + 1024]
            g1 = dma("gpsimd", SW[0:64, :], src_c[64:128, :],
                     wait=[(DV, d_rope)], ch="swp")
            g2 = dma("gpsimd", SW[64:128, :], src_c[0:64, :], ch="swp")
            gs = dma("gpsimd", BROWS[0:1, 0:1024],
                     BROWS[idx:idx + 1, 1024:2048],
                     wait=[(DV, d_r2), (TE, te_b)], ch="stg0")
            te_b = rms_bcast_row(BROWS[0:1, 0:1024], 2048, base=0,
                                 wait=[(DV, d_rope), gs])

            def frope(e, src_c=src_c):
                e.tensor_mul(XCH[:], src_c, COSX[:])
                e.tensor_mul(XC2[:], SW[:], SINX[:])
                e.drain()
                e.tensor_add(XCH[:], XCH[:], XC2[:])
                e.drain()
                return e.tensor_mul(src_c, XCH[:], PS[:, 2048:3072])
            d_rope = op("vector", frope, wait=[g2, (TE, te_b)])
            if idx >= 16:
                dma("gpsimd", ccA_in_v[0, idx - 16], src_c,
                    wait=[(DV, d_rope)], ch="blob")
        dv_prev = d_rope
        for vv in range(4):
            for b in range(8):
                te_t = op("tensor", lambda e, vv=vv, b=b: e.transpose(
                    PS[:, 2048 + (b % 2) * 512:2048 + (b % 2) * 512 + 64].bitcast(BF16),
                    PPB[:, vv * 1024 + b * 128:vv * 1024 + b * 128 + 128],
                    IDENT[:]), wait=[(DV, dv_prev)])
                dv_prev = op("vector", lambda e, vv=vv, b=b: e.tensor_copy(
                    VG[:, vv * 2048 + b * 128:vv * 2048 + b * 128 + 128],
                    PS[:, 2048 + (b % 2) * 512:2048 + (b % 2) * 512 + 64].bitcast(BF16)),
                    wait=[(TE, te_t)])
            dma("gpsimd", ccA_in_v[0, 4 + vv],
                VG[:, vv * 2048:vv * 2048 + 1024], wait=[(DV, dv_prev)],
                ch="blob")
        barrier()
        if phase_done(3):
            return

        ccc = chan("cc")
        blobc = chan("blob")
        cc_tok = (ccc, ccc.bump(1))
        blob_full = (blobc, blobc.v)

        def fcc_emit(e, bf=blob_full):
            e.wait_ge(bf[0].sem, bf[1])
            inst = e.collective_compute(
                "AllGather", ALU.bypass,
                replica_groups=[[0, 1], [2, 3], [4, 5], [6, 7]],
                ins=[ccA_in], outs=[ccA_out])
            inst.then_inc(ccc.sem, 1)
        streams["gpsimd"].append(fcc_emit)

        ccA_out_v = ccA_out.rearrange("a (r k p c) -> a r k p c",
                                      r=2, p=128, c=1024)
        for kk in range(4):
            for par in range(2):
                dma("gpsimd",
                    KG[:, kk * 2048:kk * 2048 + 2048].rearrange(
                        "p (b two c) -> p b two c", two=2, c=128)[:, :, par],
                    ccA_out_v[0, par, kk].rearrange("p (b c) -> p b c", c=128),
                    wait=([cc_tok] if (kk == 0 and par == 0) else []), ch="gath")
        for vv in range(4):
            for par in range(2):
                dma("gpsimd",
                    VG[:, vv * 2048:vv * 2048 + 2048].rearrange(
                        "p (b two c) -> p b two c", two=2, c=128)[:, :, par],
                    ccA_out_v[0, par, 4 + vv].rearrange("p (b c) -> p b c",
                                                        c=128), ch="gath")
        barrier()
        if phase_done(4):
            return

        attention(lambda h: h // 4, QB, XN)
        barrier()
        if phase_done(5):
            return

        te_ys = ln_stats_sbuf(lambda c: (BIG, c * T))

        def fyr(e):
            return e.tensor_scalar(out=ROWS[0:1, 0:1024],
                                   in0=PS[32:33, 3072:4096],
                                   scalar1=1.0 / D, scalar2=None, op0=ALU.mult)
        d_y = op("vector", fyr, wait=[(TE, te_ys)])
        s_y = op("scalar", lambda e: e.activation(
            out=ROWS[32:33, 0:1024], in_=ROWS[0:1, 0:1024], func=AF.Sqrt,
            bias=CEPS[0:1, 1:2], scale=1.0), wait=[(DV, d_y)])

        def fyr2(e):
            e.reciprocal(ROWS[0:1, 0:1024], ROWS[32:33, 0:1024])
            e.drain()
            return e.tensor_copy(BROWS[0:1, 0:1024], ROWS[0:1, 0:1024])
        d_y2 = op("vector", fyr2, wait=[(SC, s_y)])
        te_yb = rms_bcast_row(BROWS[0:1, 0:1024], 0, wait=[(DV, d_y2)])
        yn_tok = 0
        for c in range(CH):
            yn_tok = op("vector", lambda e, c=c: e.tensor_mul(
                QB[:, c * T:(c + 1) * T], XN[:, c * T:(c + 1) * T],
                PS[:, 0:1024]), wait=[(TE, te_yb)])
        barrier()

        prep_tok = tern_prep(wprojT, 0, 0)
        te_mm_end = [0, 0]
        evac_tok = [None, None]
        for s in range(4):
            slot0 = (s % 2) * 16
            for o4 in range(4):
                oc = s * 4 + o4
                pb = (o4 % 2) * 1024
                for hf in range(2):
                    for g in range(CH):
                        wl = []
                        if g == 0 and hf == 0:
                            if o4 == 0:
                                wl = [(DV, prep_tok), (DV, yn_tok)]
                            if evac_tok[o4 % 2]:
                                wl = wl + [evac_tok[o4 % 2]]
                        op("tensor", lambda e, slot0=slot0, g=g, o4=o4, hf=hf, pb=pb: e.matmul(
                            PS[:, pb + hf * 512:pb + hf * 512 + 512],
                            WBF[:, (slot0 + g) * 512 + o4 * 128:
                                (slot0 + g) * 512 + o4 * 128 + 128],
                            QB[:, g * T + hf * 512:g * T + hf * 512 + 512],
                            start=(g == 0), stop=(g == CH - 1)), wait=wl)
                te_mm = TE.v
                sc_ev = op("scalar", lambda e, oc=oc, pb=pb: e.copy(
                    XN[:, oc * T:(oc + 1) * T], PS[:, pb:pb + 1024]),
                    wait=[(TE, te_mm)])
                evac_tok[o4 % 2] = (SC, sc_ev)
            te_mm_end[s % 2] = TE.v
            if s < 3:
                prep_tok = tern_prep(wprojT, (s + 1) * 512, ((s + 1) % 2) * 16,
                                     mean_free_dv=prep_tok,
                                     wbf_free_te=te_mm_end[(s + 1) % 2])
        barrier()

        te_st2 = ln_stats_sbuf(lambda c: (BIG, c * T))
        te_bc2 = ln_smalls(te_st2)
        ln2_residual(lambda c: (BIG, c * T), src_hbm, dst_hbm, te_bc2)
        barrier()

    def block_l0_ffn(src_hbm, dst_hbm):
        te_st = ln_stats_hbm(src_hbm, XN)
        te_bc = ln_smalls(te_st)
        xn_tok = ln_apply(src_hbm, te_bc, XN)
        barrier()

        te_mm_last = 0
        ev_last = {}
        for sc_i in range(11):
            prep_toks = []
            u8_tok = [DV.v] * 4
            for w_i, wsrc in ((0, wqgT), (16, wquT)):
                for g in range(16):
                    slot = (0 if w_i == 0 else 2) + (g % 2)
                    t_w = dma("sync", WU8[:, slot * 512:slot * 512 + 512],
                              wsrc[g * 128:(g + 1) * 128,
                                   sc_i * 512:sc_i * 512 + 512],
                              wait=[(DV, u8_tok[slot])], ch="wu%d" % slot)
                    tok = op("vector", lambda e, w_i=w_i, g=g, slot=slot: e.tensor_scalar(
                        out=WBF[:, (w_i + g) * 512:(w_i + g) * 512 + 512],
                        in0=WU8[:, slot * 512:slot * 512 + 512],
                        scalar1=128.0, scalar2=None, op0=ALU.subtract),
                        wait=[t_w] +
                             ([(TE, te_mm_last)] if (w_i == 0 and g == 0) else []))
                    u8_tok[slot] = tok
                    prep_toks.append(tok)
            prep_done = prep_toks[-1]
            for q_i in range(4):
                hc = sc_i * 4 + q_i
                gb = 0
                ub = 1024
                for hf in range(2):
                    for g in range(CH):
                        wl = []
                        if g == 0 and hf == 0:
                            wl = [(DV, prep_done)] + \
                                ([(DV, xn_tok)] if hc == 0 else []) + \
                                (ev_last.get(0, []))
                        op("tensor", lambda e, g=g, q_i=q_i, hf=hf, gb=gb: e.matmul(
                            PS[:, gb + hf * 512:gb + hf * 512 + 512],
                            WBF[:, g * 512 + q_i * 128:g * 512 + q_i * 128 + 128],
                            XN[:, g * T + hf * 512:g * T + hf * 512 + 512],
                            start=(g == 0), stop=(g == CH - 1)), wait=wl)
                    for g in range(CH):
                        op("tensor", lambda e, g=g, q_i=q_i, hf=hf, ub=ub: e.matmul(
                            PS[:, ub + hf * 512:ub + hf * 512 + 512],
                            WBF[:, (16 + g) * 512 + q_i * 128:
                                (16 + g) * 512 + q_i * 128 + 128],
                            XN[:, g * T + hf * 512:g * T + hf * 512 + 512],
                            start=(g == 0), stop=(g == CH - 1)))
                te_mm = TE.v
                s_si = op("scalar", lambda e, hc=hc, gb=gb: e.activation(
                    out=VW[:], in_=PS[:, gb:gb + 1024], func=AF.Silu,
                    bias=0.0, scale=SGC[:, hc:hc + 1]), wait=[(TE, te_mm)])
                tns, off = tcol(hc)
                d_t = op("vector", lambda e, tns=tns, off=off, hc=hc, ub=ub: e.scalar_tensor_tensor(
                    out=tns[:, off:off + T], in0=PS[:, ub:ub + 1024],
                    scalar=SUC[:, hc:hc + 1], in1=VW[:],
                    op0=ALU.mult, op1=ALU.mult), wait=[(SC, s_si)])
                ev_last[0] = [(SC, s_si), (DV, d_t)]
                emit_wmax_tiles(5)
            te_mm_last = TE.v
        barrier()

        te_mm_last = 0
        ev_last = [None, None]
        ust_last = [None]
        for os_i in range(8):
            prep_toks = []
            u8_tok = [DV.v] * 4
            for g in range(44):
                slot = g % 4
                t_w = dma("sync", WU8[:, slot * 512:slot * 512 + 256],
                          wqdT[g * 128:(g + 1) * 128,
                               os_i * 256:os_i * 256 + 256],
                          wait=[(DV, u8_tok[slot])], ch="wu%d" % slot)
                tok = op("vector", lambda e, g=g, slot=slot: e.tensor_scalar(
                    out=BIG[:, g * 256:g * 256 + 256],
                    in0=WU8[:, slot * 512:slot * 512 + 256],
                    scalar1=128.0, scalar2=None, op0=ALU.subtract),
                    wait=[t_w] +
                         ([(TE, te_mm_last)] if g == 0 else []))
                u8_tok[slot] = tok
                prep_toks.append(tok)
            prep_done = prep_toks[-1]
            for oc2 in range(2):
                oc = os_i * 2 + oc2
                pb = oc2 * 1024
                for hf in range(2):
                    for g in range(44):
                        tns, off = tcol(g)
                        wl = []
                        if g == 0 and hf == 0:
                            wl = [(DV, prep_done)]
                            if ev_last[oc2]:
                                wl = wl + [ev_last[oc2]]
                        op("tensor", lambda e, g=g, oc2=oc2, hf=hf, pb=pb, tns=tns, off=off: e.matmul(
                            PS[:, pb + hf * 512:pb + hf * 512 + 512],
                            BIG[:, g * 256 + oc2 * 128:g * 256 + oc2 * 128 + 128],
                            tns[:, off + hf * 512:off + hf * 512 + 512],
                            start=(g == 0), stop=(g == 43)), wait=wl)
                te_mm = TE.v
                d_ev = op("vector", lambda e, oc=oc, pb=pb: e.tensor_scalar(
                    out=PT[:], in0=PS[:, pb:pb + 1024],
                    scalar1=SDC[:, oc:oc + 1], scalar2=None, op0=ALU.mult),
                    wait=[(TE, te_mm)] + ([ust_last[0]] if ust_last[0] else []))
                ust_last[0] = dma("gpsimd", xview(u_dram, oc), PT[:],
                                  wait=[(DV, d_ev)], ch="ust")
                ev_last[oc2] = (DV, d_ev)
                emit_wmax_tiles(4)
            te_mm_last = TE.v
        finish_wmax()
        barrier()

        te_st2 = ln_stats_udram(u_dram)
        te_bc2 = ln_smalls(te_st2)
        ln2_residual(None, src_hbm, dst_hbm, te_bc2, u_hbm=u_dram)
        barrier()

    def block_l1_attn(src_hbm, dst_hbm):
        te_st = ln_stats_hbm(src_hbm, XN)
        te_bc = ln_smalls(te_st)
        xn_tok = ln_apply(src_hbm, te_bc, XN)
        barrier()

        ccB_in_v = ccB_in.rearrange("a (k p c) -> a k p c", p=128, c=1024)
        prep_tok = cast_prep(l1wT, 0, 0)
        te_mm_end = [0, 0]
        evac_tok = [None, None]
        for s in range(5):
            slot0 = (s % 2) * 16
            ncols = 512 if s < 4 else 256
            for o4 in range(ncols // 128):
                oc = s * 4 + o4
                pb = (o4 % 2) * 1024
                for hf in range(2):
                    for g in range(CH):
                        wl = []
                        if g == 0 and hf == 0:
                            if o4 == 0:
                                wl = [(SC, prep_tok), (DV, xn_tok)]
                            if evac_tok[o4 % 2]:
                                wl = wl + [evac_tok[o4 % 2]]
                        op("tensor", lambda e, slot0=slot0, g=g, o4=o4, hf=hf, pb=pb: e.matmul(
                            PS[:, pb + hf * 512:pb + hf * 512 + 512],
                            WBF[:, (slot0 + g) * 512 + o4 * 128:
                                (slot0 + g) * 512 + o4 * 128 + 128],
                            XN[:, g * T + hf * 512:g * T + hf * 512 + 512],
                            start=(g == 0), stop=(g == CH - 1)), wait=wl)
                te_mm = TE.v
                if oc < 16:
                    d_e = op("vector", lambda e, oc=oc, pb=pb: e.tensor_scalar(
                        out=QB[:, oc * T:(oc + 1) * T], in0=PS[:, pb:pb + 1024],
                        scalar1=L1BI[:, oc:oc + 1], scalar2=INV_SQRT_HD,
                        op0=ALU.add, op1=ALU.mult), wait=[(TE, te_mm)])
                    evac_tok[o4 % 2] = (DV, d_e)
                elif oc == 16:
                    d_e = op("vector", lambda e, pb=pb: e.tensor_scalar(
                        out=RT[:, 0:1024], in0=PS[:, pb:pb + 1024],
                        scalar1=L1BI[:, 16:17], scalar2=None, op0=ALU.add),
                        wait=[(TE, te_mm)])
                    dma("gpsimd", ccB_in_v[0, 0], RT[:, 0:1024],
                        wait=[(DV, d_e)], ch="blob")
                    evac_tok[o4 % 2] = (DV, d_e)
                else:
                    d_e = op("vector", lambda e, pb=pb: e.tensor_scalar(
                        out=RT[:, 1024:2048], in0=PS[:, pb:pb + 1024],
                        scalar1=L1BI[:, 17:18], scalar2=None, op0=ALU.add),
                        wait=[(TE, te_mm)])
                    evac_tok[o4 % 2] = (DV, d_e)
                    dv_prev = d_e
                    for b in range(8):
                        te_t = op("tensor", lambda e, b=b: e.transpose(
                            PS[:, 2048 + (b % 2) * 512:2048 + (b % 2) * 512 + 64].bitcast(BF16),
                            RT[:, 1024 + b * 128:1024 + b * 128 + 128],
                            IDENT[:]), wait=[(DV, dv_prev)])
                        dv_prev = op("vector", lambda e, b=b: e.tensor_copy(
                            VG[:, b * 128:b * 128 + 128],
                            PS[:, 2048 + (b % 2) * 512:2048 + (b % 2) * 512 + 64].bitcast(BF16)),
                            wait=[(TE, te_t)])
                    dma("gpsimd", ccB_in_v[0, 1], VG[:, 0:1024],
                        wait=[(DV, dv_prev)], ch="blob")
            te_mm_end[s % 2] = TE.v
            if s < 4:
                prep_tok = cast_prep(l1wT, (s + 1) * 512, ((s + 1) % 2) * 16,
                                     ncols=(512 if s < 3 else 256),
                                     wbf_free_te=te_mm_end[(s + 1) % 2])
        barrier()

        ccc = chan("cc")
        blobc = chan("blob")
        cc_tok = (ccc, ccc.bump(1))
        blob_full = (blobc, blobc.v)

        def fcc_emit(e, bf=blob_full):
            e.wait_ge(bf[0].sem, bf[1])
            inst = e.collective_compute(
                "AllGather", ALU.bypass,
                replica_groups=[[0, 1], [2, 3], [4, 5], [6, 7]],
                ins=[ccB_in], outs=[ccB_out])
            inst.then_inc(ccc.sem, 1)
        streams["gpsimd"].append(fcc_emit)
        ccB_out_v = ccB_out.rearrange("a (r k p c) -> a r k p c",
                                      r=2, p=128, c=1024)
        for par in range(2):
            dma("gpsimd",
                KG[:, 0:2048].rearrange("p (b two c) -> p b two c",
                                        two=2, c=128)[:, :, par],
                ccB_out_v[0, par, 0].rearrange("p (b c) -> p b c", c=128),
                wait=([cc_tok] if par == 0 else []), ch="gath")
            dma("gpsimd",
                VG[:, 0:2048].rearrange("p (b two c) -> p b two c",
                                        two=2, c=128)[:, :, par],
                ccB_out_v[0, par, 1].rearrange("p (b c) -> p b c", c=128),
                ch="gath")
        barrier()

        attention(lambda h: 0, QB, XN)
        barrier()

        prep_tok = cast_prep(l1owT, 0, 0)
        te_mm_end = [0, 0]
        evac_tok = [None, None]
        for s in range(4):
            slot0 = (s % 2) * 16
            for o4 in range(4):
                oc = s * 4 + o4
                pb = (o4 % 2) * 1024
                for hf in range(2):
                    for g in range(CH):
                        wl = []
                        if g == 0 and hf == 0:
                            if o4 == 0:
                                wl = [(SC, prep_tok)]
                            if evac_tok[o4 % 2]:
                                wl = wl + [evac_tok[o4 % 2]]
                        op("tensor", lambda e, slot0=slot0, g=g, o4=o4, hf=hf, pb=pb: e.matmul(
                            PS[:, pb + hf * 512:pb + hf * 512 + 512],
                            WBF[:, (slot0 + g) * 512 + o4 * 128:
                                (slot0 + g) * 512 + o4 * 128 + 128],
                            XN[:, g * T + hf * 512:g * T + hf * 512 + 512],
                            start=(g == 0), stop=(g == CH - 1)), wait=wl)
                te_mm = TE.v
                d_e = op("vector", lambda e, oc=oc, pb=pb: e.tensor_scalar(
                    out=QB[:, oc * T:(oc + 1) * T], in0=PS[:, pb:pb + 1024],
                    scalar1=L1OB[:, oc:oc + 1], scalar2=None, op0=ALU.add),
                    wait=[(TE, te_mm)])
                evac_tok[o4 % 2] = (DV, d_e)
            te_mm_end[s % 2] = TE.v
            if s < 3:
                prep_tok = cast_prep(l1owT, (s + 1) * 512, ((s + 1) % 2) * 16,
                                     wbf_free_te=te_mm_end[(s + 1) % 2])
        barrier()

        te_st2 = ln_stats_sbuf(lambda c: (BIG, 16384 + c * T))
        te_bc2 = ln_smalls(te_st2)
        ln2_residual(lambda c: (BIG, 16384 + c * T), src_hbm, dst_hbm, te_bc2)
        barrier()

    def block_l1_ffn(src_hbm, dst_hbm):
        def phase_done(n):
            return PHASE_LIMIT <= n
        def fws(e):
            e.tensor_scalar(out=SC2[0:1, 2:4], in0=SC2[0:1, 0:2],
                            scalar1=1.0 / 448.0, scalar2=1e-6,
                            op0=ALU.mult, op1=ALU.max)
            e.drain()
            e.reciprocal(SC2[0:1, 6:8], SC2[0:1, 2:4])
            e.drain()
            return e.tensor_scalar(out=SC2[0:1, 4:6], in0=SC2[0:1, 6:8],
                                   scalar1=0.5, scalar2=None, op0=ALU.mult)
        d_ws = op("vector", fws,
                  wait=[(DV, wmax_state["done"]["dwf"]),
                        (DV, wmax_state["done"]["dwp"])])
        barrier()
        if phase_done(3.1):
            return

        for c in range(CH):
            t_x = dma("sync", XCH[:], xview(src_hbm, c),
                      wait=[(TE, TE.v if c > 0 else 0)], ch="xch")
            sc_tok = op("scalar", lambda e: e.copy(SW[:], XCH[:]),
                        wait=[t_x])
            dv_tok = op("vector", lambda e, c=c: e.tensor_mul(
                SQ[:, (c % 2) * 1024:(c % 2) * 1024 + 1024], SW[:], SW[:]),
                wait=[(SC, sc_tok)])
            for hf in range(2):
                lo = 3072 + hf * 512
                op("tensor", lambda e, c=c, hf=hf, lo=lo: e.matmul(
                    PS[0:1, lo:lo + 512], AONE[:, 32:33],
                    SW[:, hf * 512:hf * 512 + 512],
                    start=(c == 0), stop=(c == CH - 1)), wait=[(SC, sc_tok)])
                op("tensor", lambda e, c=c, hf=hf, lo=lo: e.matmul(
                    PS[32:33, lo:lo + 512], AONE[:, 32:33],
                    SQ[:, (c % 2) * 1024 + hf * 512:(c % 2) * 1024 + hf * 512 + 512],
                    start=(c == 0), stop=(c == CH - 1)), wait=[(DV, dv_tok)])
        te_bc = ln_smalls(TE.v, fp32_bcast=True)
        barrier()
        if phase_done(3.2):
            return

        first = True
        d_mx = 0
        sc_gate = SC.v
        for c in range(CH):
            t_x = dma("sync", XCH[:], xview(src_hbm, c),
                      wait=[(DV, d_mx)] + ([(SC, sc_gate)] if c == 0 else []),
                      ch="xch")

            def f(e, first=first):
                e.tensor_mul(XC2[:], XCH[:], PS[:, 0:1024])
                e.drain()
                e.tensor_add(XC2[:], XC2[:], PS[:, 1024:2048])
                e.drain()
                e.tensor_reduce(out=SCL[:, 10:11], in_=XC2[:],
                                axis=mybir.AxisListType.X, op=ALU.max,
                                apply_absolute_value=True)
                e.drain()
                if first:
                    return e.tensor_copy(SCL[:, 13:14], SCL[:, 10:11])
                return e.tensor_tensor(SCL[:, 13:14], SCL[:, 13:14],
                                       SCL[:, 10:11], op=ALU.max)
            d_mx = op("vector", f, wait=[t_x, (TE, te_bc)])
            first = False
        d_xm = rowmax_to_scalar(13, 8, d_mx)
        g2 = dma("gpsimd", ccm_in, SC2[0:1, 8:9], wait=[(DV, d_xm)], ch="msc")
        ccc = chan("cc")
        cc1 = (ccc, ccc.bump(1))

        def fcc1(e, g2=g2):
            e.wait_ge(g2[0].sem, g2[1])
            inst = e.collective_compute(
                "AllReduce", ALU.max,
                replica_groups=[[0, 1, 2, 3, 4, 5, 6, 7]],
                ins=[ccm_in], outs=[ccm_out])
            inst.then_inc(ccc.sem, 1)
        streams["gpsimd"].append(fcc1)
        g3 = dma("gpsimd", SC2[0:1, 9:10], ccm_out, wait=[cc1], ch="msc")

        def fxs(e):
            e.tensor_scalar(out=SC2[0:1, 10:11], in0=SC2[0:1, 9:10],
                            scalar1=1.0 / 448.0, scalar2=1e-6,
                            op0=ALU.mult, op1=ALU.max)
            e.drain()
            e.reciprocal(SC2[0:1, 13:14], SC2[0:1, 10:11])
            e.drain()
            e.tensor_scalar(out=SC2[0:1, 11:12], in0=SC2[0:1, 13:14],
                            scalar1=0.5, scalar2=None, op0=ALU.mult)
            e.drain()
            e.tensor_mul(SC2[0:1, 12:13], SC2[0:1, 10:11], SC2[0:1, 2:3])
            e.drain()
            return e.tensor_scalar(out=SC2[0:1, 12:13], in0=SC2[0:1, 12:13],
                                   scalar1=4.0, scalar2=None, op0=ALU.mult)
        d_xs = op("vector", fxs, wait=[g3])
        d_sc = bcast_scalars_f32([11, 4, 5, 12], 4, d_xs)
        barrier()
        if phase_done(3.3):
            return

        d_q = DV.v
        for c in range(CH):
            t_x = dma("sync", XCH[:], xview(src_hbm, c), wait=[(DV, d_q)],
                      ch="xch")

            def f(e, c=c):
                e.tensor_mul(XC2[:], XCH[:], PS[:, 0:1024])
                e.drain()
                e.tensor_add(XC2[:], XC2[:], PS[:, 1024:2048])
                e.drain()
                return e.tensor_scalar(out=XQ8[:, c * T:(c + 1) * T],
                                       in0=XC2[:], scalar1=SCL[:, 4:5],
                                       scalar2=None, op0=ALU.mult)
            d_q = op("vector", f, wait=[t_x, (DV, d_sc)])
        barrier()
        if phase_done(3.4):
            return

        first_h = True
        te_mm_end = [0, 0]
        h_tok = [None, None]
        hst_last = [None]
        for os_i in range(16):
            prep_toks = []
            w8_tok = [DV.v, DV.v]
            for g in range(16):
                t_w = dma("sync", WRAW[:, (g % 2) * 512:(g % 2) * 512 + 512],
                          wfcT[g * 128:(g + 1) * 128,
                               os_i * 512:os_i * 512 + 512],
                          wait=[(DV, w8_tok[g % 2])], ch="wr%d" % (g % 2))
                tok = op("vector", lambda e, g=g, os_i=os_i: e.tensor_scalar(
                    out=W8[:, ((os_i % 2) * 16 + g) * 512:
                           ((os_i % 2) * 16 + g) * 512 + 512],
                    in0=WRAW[:, (g % 2) * 512:(g % 2) * 512 + 512],
                    scalar1=SCL[:, 5:6], scalar2=None, op0=ALU.mult),
                    wait=[t_w] +
                         ([(TE, te_mm_end[os_i % 2])] if g == 0 else []))
                w8_tok[g % 2] = tok
                prep_toks.append(tok)
            prep_done = prep_toks[-1]
            for o4 in range(4):
                oc = os_i * 4 + o4
                pb = (o4 % 2) * 1024
                for hf in range(2):
                    for g in range(CH):
                        wl = []
                        if g == 0 and hf == 0:
                            if o4 == 0:
                                wl = [(DV, prep_done)]
                            if h_tok[o4 % 2]:
                                wl = wl + [h_tok[o4 % 2]]
                        op("tensor", lambda e, g=g, o4=o4, hf=hf, pb=pb, os_i=os_i: e.matmul(
                            PS[:, pb + hf * 512:pb + hf * 512 + 512],
                            W8[:, ((os_i % 2) * 16 + g) * 512 + o4 * 128:
                               ((os_i % 2) * 16 + g) * 512 + o4 * 128 + 128],
                            XQ8[:, g * T + hf * 512:g * T + hf * 512 + 512],
                            start=(g == 0), stop=(g == CH - 1)), wait=wl)
                te_mm = TE.v
                s_r = op("scalar", lambda e, pb=pb: e.activation(
                    out=XC2[:], in_=PS[:, pb:pb + 1024], func=AF.Relu,
                    bias=0.0, scale=SCL[:, 7:8]),
                    wait=[(TE, te_mm), (DV, DV.v)])

                def fh(e, pb=pb, first_h=first_h):
                    e.scalar_tensor_tensor(out=XCH[:], in0=PS[:, pb:pb + 1024],
                                           scalar=SCL[:, 7:8], in1=XC2[:],
                                           op0=ALU.mult, op1=ALU.mult)
                    e.drain()
                    e.tensor_reduce(out=SCL[:, 10:11], in_=XCH[:],
                                    axis=mybir.AxisListType.X, op=ALU.max)
                    e.drain()
                    if first_h:
                        return e.tensor_copy(SCL[:, 14:15], SCL[:, 10:11])
                    return e.tensor_tensor(SCL[:, 14:15], SCL[:, 14:15],
                                           SCL[:, 10:11], op=ALU.max)
                d_h = op("vector", fh, wait=[(SC, s_r)] +
                         ([hst_last[0]] if hst_last[0] else []))
                first_h = False
                h_tok[o4 % 2] = (DV, d_h)
                hst_last[0] = dma("gpsimd",
                                  h_dram.rearrange("(ch p) t -> ch p t", p=128)[oc],
                                  XCH[:], wait=[(DV, d_h)], ch="hst")
            te_mm_end[os_i % 2] = TE.v
        barrier()
        if phase_done(3.5):
            return

        d_hm = rowmax_to_scalar(14, 16, DV.v)
        g2 = dma("gpsimd", cch_in, SC2[0:1, 16:17], wait=[(DV, d_hm)], ch="msc")
        ccc = chan("cc")
        cc2 = (ccc, ccc.bump(1))

        def fcc2(e, g2=g2):
            e.wait_ge(g2[0].sem, g2[1])
            inst = e.collective_compute(
                "AllReduce", ALU.max,
                replica_groups=[[0, 1, 2, 3, 4, 5, 6, 7]],
                ins=[cch_in], outs=[cch_out])
            inst.then_inc(ccc.sem, 1)
        streams["gpsimd"].append(fcc2)
        g3 = dma("gpsimd", SC2[0:1, 17:18], cch_out, wait=[cc2], ch="msc")

        def fhs(e):
            e.tensor_scalar(out=SC2[0:1, 18:19], in0=SC2[0:1, 17:18],
                            scalar1=1.0 / 448.0, scalar2=1e-6,
                            op0=ALU.mult, op1=ALU.max)
            e.drain()
            e.reciprocal(SC2[0:1, 21:22], SC2[0:1, 18:19])
            e.drain()
            e.tensor_scalar(out=SC2[0:1, 19:20], in0=SC2[0:1, 21:22],
                            scalar1=0.5, scalar2=None, op0=ALU.mult)
            e.drain()
            e.tensor_mul(SC2[0:1, 20:21], SC2[0:1, 18:19], SC2[0:1, 3:4])
            e.drain()
            return e.tensor_scalar(out=SC2[0:1, 20:21], in0=SC2[0:1, 20:21],
                                   scalar1=4.0, scalar2=None, op0=ALU.mult)
        d_hs = op("vector", fhs, wait=[g3])
        d_sc2 = bcast_scalars_f32([19, 20], 8, d_hs)
        barrier()
        if phase_done(3.6):
            return

        d_q = DV.v
        for c in range(64):
            t_h = dma("sync", XCH[:],
                      h_dram.rearrange("(ch p) t -> ch p t", p=128)[c],
                      wait=[(DV, d_q)], ch="xch")
            d_q = op("vector", lambda e, c=c: e.tensor_scalar(
                out=H8[:, c * T:(c + 1) * T], in0=XCH[:],
                scalar1=SCL[:, 8:9], scalar2=None, op0=ALU.mult),
                wait=[t_h, (DV, d_sc2)])
        barrier()
        if phase_done(3.7):
            return

        te_mm_last = 0
        ev_last = [None, None]
        ust_last = [None]
        for os_i in range(8):
            prep_toks = []
            w8_tok = [DV.v] * 4
            for g in range(64):
                slot = g % 4
                t_w = dma("sync", WRAW[:, slot * 256:slot * 256 + 256],
                          wprT[g * 128:(g + 1) * 128,
                               os_i * 256:os_i * 256 + 256],
                          wait=[(DV, w8_tok[slot])], ch="wr%d" % slot)
                tok = op("vector", lambda e, g=g, slot=slot: e.tensor_scalar(
                    out=W8[:, g * 256:g * 256 + 256],
                    in0=WRAW[:, slot * 256:slot * 256 + 256],
                    scalar1=SCL[:, 6:7], scalar2=None, op0=ALU.mult),
                    wait=[t_w] +
                         ([(TE, te_mm_last)] if g == 0 else []))
                w8_tok[slot] = tok
                prep_toks.append(tok)
            prep_done = prep_toks[-1]
            for oc2 in range(2):
                oc = os_i * 2 + oc2
                pb = oc2 * 1024
                for hf in range(2):
                    for g in range(64):
                        wl = []
                        if g == 0 and hf == 0:
                            wl = [(DV, prep_done)]
                            if ev_last[oc2]:
                                wl = wl + [ev_last[oc2]]
                        op("tensor", lambda e, g=g, oc2=oc2, hf=hf, pb=pb: e.matmul(
                            PS[:, pb + hf * 512:pb + hf * 512 + 512],
                            W8[:, g * 256 + oc2 * 128:g * 256 + oc2 * 128 + 128],
                            H8[:, g * T + hf * 512:g * T + hf * 512 + 512],
                            start=(g == 0), stop=(g == 63)), wait=wl)
                te_mm = TE.v
                d_ev = op("vector", lambda e, pb=pb: e.tensor_scalar(
                    out=PT[:], in0=PS[:, pb:pb + 1024],
                    scalar1=SCL[:, 9:10], scalar2=None, op0=ALU.mult),
                    wait=[(TE, te_mm)] + ([ust_last[0]] if ust_last[0] else []))
                ust_last[0] = dma("gpsimd", xview(u_dram, oc), PT[:],
                                  wait=[(DV, d_ev)], ch="ust")
                ev_last[oc2] = (DV, d_ev)
            te_mm_last = TE.v
        barrier()
        if phase_done(3.8):
            return

        te_st2 = ln_stats_udram(u_dram)
        te_bc2 = ln_smalls(te_st2)
        ln2_residual(None, src_hbm, dst_hbm, te_bc2, u_hbm=u_dram)
        barrier()

    blocks = [block_l0_attn, block_l0_ffn, block_l1_attn, block_l1_ffn]
    srcs = [x0, xs, xs, xs]
    dsts = [xs, xs, xs, out_t]
    dsts[N_BLOCKS - 1] = out_t
    for i in range(N_BLOCKS):
        blocks[i](srcs[i], dsts[i])

    final_toks = [(c.sem, c.v) for c in channels.values() if c.v > 0]

    def _final(e, wl=final_toks):
        for (s, v) in wl:
            e.wait_ge(s, v)
    streams["gpsimd"].append(_final)

    with es:
        with nc.Block() as block:
            @block.sync
            def _(e):
                for f in streams["sync"]:
                    f(e)

            @block.tensor
            def _(e):
                for f in streams["tensor"]:
                    f(e)

            @block.vector
            def _(e):
                for f in streams["vector"]:
                    f(e)

            @block.scalar
            def _(e):
                for f in streams["scalar"]:
                    f(e)

            @block.gpsimd
            def _(e):
                for f in streams["gpsimd"]:
                    f(e)
    return nc


def _host_layout(inputs):
    x = np.ascontiguousarray(inputs["L0_attn_x"])
    wqkvT = np.ascontiguousarray(inputs["L0_attn_w_qkv"].T)
    wprojT = np.ascontiguousarray(inputs["L0_attn_w_proj"].T)
    qgc = np.ascontiguousarray(
        (inputs["L0_attn_q_gain"].astype(np.float32) *
         np.float32(INV_SQRT_HD)).reshape(16, 1))

    def u8T(a):
        v = np.ascontiguousarray(a).view(np.uint8).reshape(
            a.shape[0], a.shape[1], 4)[:, :, 0]
        return np.ascontiguousarray(v.T)
    wqgT = u8T(inputs["L0_ffn_wq_gate"])
    wquT = u8T(inputs["L0_ffn_wq_up"])
    wqdT = u8T(inputs["L0_ffn_wq_down"])
    sg_c = np.ascontiguousarray(inputs["L0_ffn_s_gate"].reshape(44, 128).T)
    su_c = np.ascontiguousarray(inputs["L0_ffn_s_up"].reshape(44, 128).T)
    sd_c = np.ascontiguousarray(inputs["L0_ffn_s_down"].reshape(16, 128).T)
    l1wT = np.ascontiguousarray(inputs["L1_attn_Wqkv_w"].T)
    l1bias = np.ascontiguousarray(inputs["L1_attn_Wqkv_b"].reshape(18, 128).T)
    l1owT = np.ascontiguousarray(inputs["L1_attn_out_w"].T)
    l1ob = np.ascontiguousarray(inputs["L1_attn_out_b"].reshape(16, 128).T)
    wfcT = np.ascontiguousarray(inputs["L1_ffn_w_fc"].T)
    wprT = np.ascontiguousarray(inputs["L1_ffn_w_proj"].T)

    ident = np.eye(128, dtype=np.float32).astype(bf)
    aone = np.zeros((128, 64), dtype=np.float32)
    aone[:, 32] = 1.0
    aone = aone.astype(bf)
    onesrow = np.zeros((65, 128), dtype=np.float32)
    onesrow[[0, 32, 64], :] = 1.0
    onesrow = onesrow.astype(bf)
    negrow = np.zeros((65, 128), dtype=np.float32)
    negrow[[0, 32, 64], :] = -1.0
    negrow = negrow.astype(bf)
    ceps = np.zeros((128, 2), dtype=np.float32)
    ceps[:, 0] = LN_EPS
    ceps[:, 1] = RMS_EPS

    inv = 1.0 / (10000.0 ** (np.arange(0, 128, 2, dtype=np.float64) / 128.0))
    in_maps, pos_list = [], []
    for core in range(8):
        b, p = core // 2, core % 2
        pos = np.concatenate([np.arange(128) + (2 * j + p) * 128
                              for j in range(8)])
        pos_list.append(pos)
        xc = np.ascontiguousarray(x[b][pos].T)
        fr = np.outer(pos.astype(np.float64), inv)
        c64 = np.cos(fr).T.astype(np.float32)
        s64 = np.sin(fr).T.astype(np.float32)
        cosx = np.ascontiguousarray(
            np.concatenate([c64, c64], axis=0)).astype(bf)
        sinx = np.ascontiguousarray(
            np.concatenate([s64, -s64], axis=0)).astype(bf)
        lens = np.zeros((128, 8), dtype=np.float32)
        for j in range(8):
            lens[:, j] = (2 * j + p) * 128 + np.arange(128) + 1
        lens2 = np.maximum(lens - 1024.0, 0.0).astype(np.float32)
        in_maps.append(dict(
            x0=xc, wqkvT=wqkvT, wprojT=wprojT, qgc=qgc,
            wqgT=wqgT, wquT=wquT, wqdT=wqdT,
            sg_c=sg_c, su_c=su_c, sd_c=sd_c,
            l1wT=l1wT, l1bias=l1bias, l1owT=l1owT, l1ob=l1ob,
            wfcT=wfcT, wprT=wprT,
            cosx=cosx, sinx=sinx, lens_in=lens, lens2_in=lens2,
            ident_in=ident, aone_in=aone, onesrow_in=onesrow,
            negrow_in=negrow, ceps_in=ceps,
        ))
    return in_maps, pos_list


def kernel(**inputs):
    in_maps, pos_list = _host_layout(inputs)
    nc = build_program()
    lower_extended_insts(nc)
    res = run_bass_kernel_spmd(nc, in_maps, core_ids=list(range(8)))
    out = np.empty((4, 2048, 2048), dtype=np.float32)
    for core in range(8):
        b = core // 2
        out[b, pos_list[core], :] = np.asarray(res.results[core]["out"]).T
    return out


# version 07
# speedup vs baseline: 1.1435x; 1.0030x over previous
import numpy as np
import ml_dtypes
import concourse.bass as bass
import concourse.mybir as mybir
from concourse.bass_utils import run_bass_kernel_spmd
from concourse.library_overlay import lower_extended_insts
from contextlib import ExitStack

F32 = mybir.dt.float32
BF16 = mybir.dt.bfloat16
U8 = mybir.dt.uint8
FP8 = mybir.dt.float8e4
AF = mybir.ActivationFunctionType
ALU = mybir.AluOpType
bf = ml_dtypes.bfloat16

T = 1024
CH = 16
D = 2048
RMS_EPS = 1.1920929e-07
LN_EPS = 1e-5
CROUND = 12582912.0
POSBIG = 3.4028235e38
INV_SQRT_HD = 0.08838834764831845

N_BLOCKS = 4
PHASE_LIMIT = 99
ENGS = ("sync", "tensor", "vector", "scalar", "gpsimd")


class Ctr:
    def __init__(self, nc, es, name):
        self.sem = es.enter_context(nc.semaphore(name))
        self.v = 0

    def bump(self, n=1):
        self.v += n
        return self.v


def _bank_splits(lo, hi):
    outl = []
    c = lo
    while c < hi:
        nxt = min((c // 512 + 1) * 512, hi)
        outl.append((c, nxt))
        c = nxt
    return outl


def build_program():
    nc = bass.Bass("TRN2", target_bir_lowering=False)
    es = ExitStack()
    streams = {e: [] for e in ENGS}

    dram = lambda n, s, d, k="ExternalInput": nc.dram_tensor(n, s, d, kind=k).ap()
    x0 = dram("x0", [D, T], F32)
    wqkvT = dram("wqkvT", [D, 3072], F32)
    wprojT = dram("wprojT", [D, D], F32)
    qgc = dram("qgc", [16, 1], F32)
    wqgT = dram("wqgT", [D, 5632], U8)
    wquT = dram("wquT", [D, 5632], U8)
    wqdT = dram("wqdT", [5632, D], U8)
    sg_c = dram("sg_c", [128, 44], F32)
    su_c = dram("su_c", [128, 44], F32)
    sd_c = dram("sd_c", [128, 16], F32)
    l1wT = dram("l1wT", [D, 2304], F32)
    l1bias = dram("l1bias", [128, 18], F32)
    l1owT = dram("l1owT", [D, D], F32)
    l1ob = dram("l1ob", [128, 16], F32)
    wfcT = dram("wfcT", [D, 8192], F32)
    wprT = dram("wprT", [8192, D], F32)
    cosx = dram("cosx", [128, T], BF16)
    sinx = dram("sinx", [128, T], BF16)
    lens_in = dram("lens_in", [128, 8], F32)
    lens2_in = dram("lens2_in", [128, 8], F32)
    ident_in = dram("ident_in", [128, 128], BF16)
    aone_in = dram("aone_in", [128, 64], BF16)
    onesrow_in = dram("onesrow_in", [65, 128], BF16)
    negrow_in = dram("negrow_in", [65, 128], BF16)
    ceps_in = dram("ceps_in", [128, 2], F32)

    out_t = dram("out", [D, T], F32, "ExternalOutput")
    xs = dram("xs", [D, T], F32, "Internal")
    u_dram = dram("u_dram", [D, T], BF16, "Internal")
    h_dram = dram("h_dram", [8192, T], F32, "Internal")
    scr128 = dram("scr128", [128, 1], F32, "Internal")
    ccA_in = dram("ccA_in", [1, 1024 * 1024], BF16, "Internal")
    ccA_out = dram("ccA_out", [1, 2 * 1024 * 1024], BF16, "Internal")
    ccB_in = dram("ccB_in", [1, 2 * 128 * 1024], BF16, "Internal")
    ccB_out = dram("ccB_out", [1, 4 * 128 * 1024], BF16, "Internal")
    ccm_in = dram("ccm_in", [1, 1], F32, "Internal")
    ccm_out = dram("ccm_out", [1, 1], F32, "Internal")
    cch_in = dram("cch_in", [1, 1], F32, "Internal")
    cch_out = dram("cch_out", [1, 1], F32, "Internal")

    def xview(t, c):
        return t.rearrange("(ch p) t -> ch p t", p=128)[c]

    sbuf = lambda n, s, d: es.enter_context(nc.sbuf_tensor(n, s, d))
    IDENT = sbuf("IDENT", [128, 128], BF16)
    AONE = sbuf("AONE", [128, 64], BF16)
    ONESROW = sbuf("ONESROW", [65, 128], BF16)
    NEGROW = sbuf("NEGROW", [65, 128], BF16)
    CEPS = sbuf("CEPS", [128, 2], F32)
    COSX = sbuf("COSX", [128, T], BF16)
    SINX = sbuf("SINX", [128, T], BF16)
    LENS = sbuf("LENS", [128, 8], F32)
    LENS2 = sbuf("LENS2", [128, 8], F32)
    QGC = sbuf("QGC", [16, 1], F32)
    SGC = sbuf("SGC", [128, 44], F32)
    SUC = sbuf("SUC", [128, 44], F32)
    SDC = sbuf("SDC", [128, 16], F32)
    L1BI = sbuf("L1BI", [128, 18], F32)
    L1OB = sbuf("L1OB", [128, 16], F32)
    WRAW = sbuf("WRAW", [128, 1024], F32)
    WU8 = sbuf("WU8", [128, 2048], U8)
    WBF = sbuf("WBF", [128, 16384], BF16)
    SQ = sbuf("SQ", [128, 2048], BF16)
    XCH = sbuf("XCH", [128, 1024], F32)
    XC2 = sbuf("XC2", [128, 1024], F32)
    ROWS = sbuf("ROWS", [128, 3072], F32)
    BROWS = sbuf("BROWS", [128, 2048], BF16)
    SC2 = sbuf("SC2", [128, 32], F32)
    RMAX = sbuf("RMAX", [1, 256], F32)
    TERN = sbuf("TERN", [16, 512], F32)
    TERNB = sbuf("TERNB", [16, 1536], BF16)
    VW = sbuf("VW", [128, 1024], BF16)
    RT = sbuf("RT", [128, 2048], BF16)
    SMF = sbuf("SMF", [128, 2048], F32)
    XSLOT2 = None
    PT = sbuf("PT", [128, 1024], BF16)
    SW = sbuf("SW", [128, 1024], BF16)
    SCL = sbuf("SCL", [128, 16], F32)
    BIG = sbuf("BIG", [128, 57344], BF16)

    W8 = WBF[:, 0:8192].bitcast(FP8)
    XN = BIG[:, 0:16384]
    QB = BIG[:, 16384:32768]
    KG = BIG[:, 32768:40960]
    VG = BIG[:, 40960:49152]
    PPB = BIG[:, 49152:57344]
    XQ8 = BIG[:, 0:8192].bitcast(FP8)
    H8 = BIG[:, 8192:40960].bitcast(FP8)

    XSLOT = [XCH, None]

    def tcol(hc):
        if hc < 40:
            return BIG, 16384 + hc * 1024
        if hc < 42:
            return SQ, (hc - 40) * 1024
        return RT, (hc - 42) * 1024

    PS = es.enter_context(nc.psum_tensor("PS", [128, 4096], F32))

    def xsl(c):
        return XCH[:] if c % 2 == 0 else SMF[:, 1024:2048]

    def usl(c):
        return SW[:] if c % 2 == 0 else RT[:, 0:1024]

    SY = Ctr(nc, es, "SY")
    GD = Ctr(nc, es, "GD")
    TE = Ctr(nc, es, "TE")
    DV = Ctr(nc, es, "DV")
    SC = Ctr(nc, es, "SC")
    CTRS = {"sync": SY, "gpsimd": GD, "tensor": TE, "vector": DV, "scalar": SC}

    channels = {}

    def chan(name):
        if name not in channels:
            channels[name] = Ctr(nc, es, "ch_" + name)
        return channels[name]

    def dma(queue, dst, src, wait=(), ch="g"):
        ctr = chan(ch)
        tok = ctr.bump(16)
        wl = [(c.sem, v) for (c, v) in wait if v > 0]

        def f(e, dst=dst, src=src, wl=wl):
            for (s, v) in wl:
                e.wait_ge(s, v)
            e.dma_start(out=dst, in_=src).then_inc(ctr.sem, 16)
        streams[queue].append(f)
        return (ctr, tok)

    def op(eng, fn, wait=(), inc=1):
        ctr = CTRS[eng]
        tok = ctr.bump(inc) if inc else ctr.v
        wl = [(c.sem, v) for (c, v) in wait if v > 0]

        def f(e, fn=fn, wl=wl, inc=inc):
            for (s, v) in wl:
                e.wait_ge(s, v)
            inst = fn(e)
            if inc:
                inst.then_inc(ctr.sem, inc)
        streams[eng].append(f)
        return tok

    def barrier():
        allc = list(CTRS.items()) + [("ch_" + k, c) for k, c in channels.items()]
        toks = [(c.sem, c.v) for _, c in allc if c.v > 0]
        for eng in ENGS:
            wl = [(s, v) for (s, v) in toks if s is not CTRS[eng].sem]

            def f(e, wl=wl):
                for (s, v) in wl:
                    e.wait_ge(s, v)
            streams[eng].append(f)

    dma("sync", IDENT[:], ident_in, ch="con")
    dma("sync", AONE[:], aone_in, ch="con")
    dma("sync", ONESROW[:], onesrow_in, ch="con")
    dma("sync", NEGROW[:], negrow_in, ch="con")
    dma("sync", CEPS[:], ceps_in, ch="con")
    dma("sync", COSX[:], cosx, ch="con")
    dma("sync", SINX[:], sinx, ch="con")
    dma("sync", LENS[:], lens_in, ch="con")
    dma("sync", LENS2[:], lens2_in, ch="con")
    dma("sync", QGC[:], qgc, ch="con")
    dma("sync", SGC[:], sg_c, ch="con")
    dma("sync", SUC[:], su_c, ch="con")
    dma("sync", SDC[:], sd_c, ch="con")
    dma("sync", L1BI[:], l1bias, ch="con")
    dma("sync", L1OB[:], l1ob, ch="con")
    barrier()

    def ln_stats_hbm(src_hbm, xb_dst):
        sq_te = [0, 0]
        sc_slot = [SC.v, SC.v]
        for c in range(CH):
            t_x = dma("sync", xsl(c), xview(src_hbm, c),
                      wait=[(SC, sc_slot[c % 2])], ch="xch%d" % (c % 2))
            sc_tok = op("scalar", lambda e, c=c: e.copy(
                xb_dst[:, c * T:(c + 1) * T], xsl(c)), wait=[t_x])
            sc_slot[c % 2] = sc_tok
            dv_tok = op("vector", lambda e, c=c: e.tensor_mul(
                SQ[:, (c % 2) * 1024:(c % 2) * 1024 + 1024],
                xb_dst[:, c * T:(c + 1) * T], xb_dst[:, c * T:(c + 1) * T]),
                wait=[(SC, sc_tok), (TE, sq_te[c % 2])])
            last_dv = dv_tok
            for hf in range(2):
                lo = 3072 + hf * 512
                op("tensor", lambda e, c=c, hf=hf, lo=lo: e.matmul(
                    PS[0:1, lo:lo + 512], AONE[:, 32:33],
                    xb_dst[:, c * T + hf * 512:c * T + hf * 512 + 512],
                    start=(c == 0), stop=(c == CH - 1)), wait=[(SC, sc_tok)])
                op("tensor", lambda e, c=c, hf=hf, lo=lo: e.matmul(
                    PS[32:33, lo:lo + 512], AONE[:, 32:33],
                    SQ[:, (c % 2) * 1024 + hf * 512:(c % 2) * 1024 + hf * 512 + 512],
                    start=(c == 0), stop=(c == CH - 1)), wait=[(DV, dv_tok)])
            sq_te[c % 2] = TE.v
        return TE.v

    def ln_stats_sbuf(cols):
        sq_te = [0, 0]
        for c in range(CH):
            tns, off = cols(c)
            dv_tok = op("vector", lambda e, tns=tns, off=off, c=c: e.tensor_mul(
                SQ[:, (c % 2) * 1024:(c % 2) * 1024 + 1024],
                tns[:, off:off + T], tns[:, off:off + T]),
                wait=[(TE, sq_te[c % 2])])
            for hf in range(2):
                lo = 3072 + hf * 512
                op("tensor", lambda e, tns=tns, off=off, c=c, hf=hf, lo=lo: e.matmul(
                    PS[0:1, lo:lo + 512], AONE[:, 32:33],
                    tns[:, off + hf * 512:off + hf * 512 + 512],
                    start=(c == 0), stop=(c == CH - 1)))
                op("tensor", lambda e, c=c, hf=hf, lo=lo: e.matmul(
                    PS[32:33, lo:lo + 512], AONE[:, 32:33],
                    SQ[:, (c % 2) * 1024 + hf * 512:(c % 2) * 1024 + hf * 512 + 512],
                    start=(c == 0), stop=(c == CH - 1)), wait=[(DV, dv_tok)])
            sq_te[c % 2] = TE.v
        return TE.v

    def ln_stats_udram(src_hbm_u):
        te_slot = [TE.v, TE.v]
        for c in range(CH):
            t_u = dma("sync", usl(c), xview(src_hbm_u, c),
                      wait=[(TE, te_slot[c % 2])], ch="sw%d" % (c % 2))
            dv_sq = op("vector", lambda e, c=c: e.tensor_mul(
                SQ[:, (c % 2) * 1024:(c % 2) * 1024 + 1024], usl(c), usl(c)),
                wait=[t_u])
            for hf in range(2):
                lo = 3072 + hf * 512
                op("tensor", lambda e, c=c, hf=hf, lo=lo: e.matmul(
                    PS[0:1, lo:lo + 512], AONE[:, 32:33],
                    usl(c)[:, hf * 512:hf * 512 + 512],
                    start=(c == 0), stop=(c == CH - 1)), wait=[t_u])
                op("tensor", lambda e, c=c, hf=hf, lo=lo: e.matmul(
                    PS[32:33, lo:lo + 512], AONE[:, 32:33],
                    SQ[:, (c % 2) * 1024 + hf * 512:(c % 2) * 1024 + hf * 512 + 512],
                    start=(c == 0), stop=(c == CH - 1)), wait=[(DV, dv_sq)])
            te_slot[c % 2] = TE.v
        return TE.v

    def ln_smalls(te_stats, fp32_bcast=False):
        def f1(e):
            e.tensor_scalar(out=ROWS[0:1, 0:1024], in0=PS[0:1, 3072:4096],
                            scalar1=1.0 / D, scalar2=None, op0=ALU.mult)
            e.tensor_scalar(out=ROWS[0:1, 1024:2048], in0=PS[32:33, 3072:4096],
                            scalar1=1.0 / D, scalar2=None, op0=ALU.mult)
            e.drain()
            e.tensor_mul(ROWS[0:1, 2048:3072], ROWS[0:1, 0:1024],
                         ROWS[0:1, 0:1024])
            e.drain()
            return e.tensor_sub(ROWS[32:33, 0:1024], ROWS[0:1, 1024:2048],
                                ROWS[0:1, 2048:3072])
        d1 = op("vector", f1, wait=[(TE, te_stats)])
        s1 = op("scalar", lambda e: e.activation(
            out=ROWS[32:33, 1024:2048], in_=ROWS[32:33, 0:1024], func=AF.Sqrt,
            bias=CEPS[32:33, 0:1], scale=1.0), wait=[(DV, d1)])

        def f2(e):
            e.reciprocal(ROWS[0:1, 2048:3072], ROWS[32:33, 1024:2048])
            e.drain()
            e.tensor_mul(ROWS[32:33, 0:1024], ROWS[0:1, 0:1024],
                         ROWS[0:1, 2048:3072])
            e.drain()
            e.tensor_copy(BROWS[0:1, 0:1024], ROWS[0:1, 2048:3072])
            e.tensor_copy(BROWS[32:33, 0:1024], ROWS[32:33, 0:1024])
            if fp32_bcast:
                e.drain()
                e.tensor_copy(ROWS[0:1, 1024:2048], ROWS[32:33, 0:1024])
                e.tensor_copy(SMF[0:1, 0:2048].bitcast(BF16)[:, 0:1024],
                              ROWS[32:33, 0:1024])
                e.drain()
                e.tensor_sub(ROWS[64:65, 0:1024], ROWS[0:1, 2048:3072],
                             BROWS[0:1, 0:1024])
                e.tensor_sub(ROWS[64:65, 1024:2048], ROWS[0:1, 1024:2048],
                             SMF[0:1, 0:2048].bitcast(BF16)[:, 0:1024])
                e.drain()
                e.tensor_copy(BROWS[0:1, 1024:2048], ROWS[64:65, 0:1024])
                return e.tensor_copy(
                    SMF[0:1, 0:2048].bitcast(BF16)[:, 1024:2048],
                    ROWS[64:65, 1024:2048])
            return e.engine_nop()
        d2 = op("vector", f2, wait=[(SC, s1)])
        for hf in range(2):
            sl = slice(hf * 512, hf * 512 + 512)
            sl2 = slice(1024 + hf * 512, 1024 + hf * 512 + 512)
            op("tensor", lambda e, sl=sl: e.matmul(
                PS[:, sl], ONESROW[0:1, :], BROWS[0:1, sl],
                start=True, stop=not fp32_bcast), wait=[(DV, d2)])
            if fp32_bcast:
                op("tensor", lambda e, sl=sl: e.matmul(
                    PS[:, sl], ONESROW[0:1, :],
                    BROWS[0:1, 1024 + sl.start:1024 + sl.start + 512],
                    start=False, stop=True))
            if not fp32_bcast:
                op("tensor", lambda e, sl=sl, sl2=sl2: e.matmul(
                    PS[:, sl2], NEGROW[32:33, :], BROWS[32:33, sl],
                    start=True, stop=True))
            else:
                op("tensor", lambda e, sl=sl, sl2=sl2: e.matmul(
                    PS[:, sl2], NEGROW[0:1, :],
                    SMF[0:1, 0:2048].bitcast(BF16)[:, sl],
                    start=True, stop=False))
                op("tensor", lambda e, sl=sl, sl2=sl2: e.matmul(
                    PS[:, sl2], NEGROW[0:1, :],
                    SMF[0:1, 0:2048].bitcast(BF16)[:, 1024 + sl.start:
                                                   1024 + sl.start + 512],
                    start=False, stop=True))
        return TE.v

    def ln_apply(src_hbm, te_bc, dst_bf):
        last = 0
        dv_slot = [DV.v, DV.v]
        sc_gate = SC.v
        for c in range(CH):
            t_x = dma("sync", xsl(c), xview(src_hbm, c),
                      wait=[(DV, dv_slot[c % 2])] +
                           ([(SC, sc_gate)] if c < 2 else []),
                      ch="xch%d" % (c % 2))

            def f(e, c=c):
                e.tensor_mul(XC2[:], xsl(c), PS[:, 0:1024])
                e.drain()
                return e.tensor_add(dst_bf[:, c * T:(c + 1) * T],
                                    XC2[:], PS[:, 1024:2048])
            last = op("vector", f, wait=[t_x, (TE, te_bc), (DV, last)])
            dv_slot[c % 2] = last
        return last

    def ln2_residual(u_cols, src_hbm, dst_hbm, te_bc, u_hbm=None):
        last_store = None
        dv_uslot = [DV.v, DV.v]
        dv_xslot = [DV.v, DV.v]
        te_gate = TE.v
        sc_gate = SC.v
        for c in range(CH):
            if u_hbm is not None:
                t_u = dma("sync", usl(c), xview(u_hbm, c),
                          wait=[(DV, dv_uslot[c % 2])] +
                               ([(TE, te_gate)] if c < 2 else []),
                          ch="sw%d" % (c % 2))
                usrc = usl(c)
                wl_extra = [t_u]
            else:
                tns, off = u_cols(c)
                usrc = tns[:, off:off + T]
                wl_extra = []
            t_x = dma("sync", xsl(c), xview(src_hbm, c),
                      wait=[(DV, dv_xslot[c % 2])] +
                           ([(SC, sc_gate)] if c < 2 else []),
                      ch="xch%d" % (c % 2))

            def f(e, usrc=usrc, c=c):
                e.tensor_mul(XC2[:], usrc, PS[:, 0:1024])
                e.drain()
                e.tensor_add(XC2[:], XC2[:], PS[:, 1024:2048])
                e.drain()
                return e.tensor_add(XC2[:], XC2[:], xsl(c))
            d = op("vector", f, wait=[t_x, (TE, te_bc), (DV, DV.v)] +
                   wl_extra + ([last_store] if last_store else []))
            dv_uslot[c % 2] = d
            dv_xslot[c % 2] = d
            last_store = dma("gpsimd", xview(dst_hbm, c), XC2[:],
                             wait=[(DV, d)], ch="xst")
        return last_store

    def tern_prep(w_hbm, s_off, slot0, mean_free_dv=0, wbf_free_te=0):
        stop_at = PHASE_LIMIT
        """Ternary prep of a 512-col superchunk into WBF slots slot0..+15.
        mean_free_dv: PS bank4 free (prev superchunk's v-ops done).
        wbf_free_te: WBF slots free (matmuls of 2-ago superchunk done)."""
        sc_toks = []
        te_means = [TE.v, TE.v]
        sc_gate = SC.v
        for g in range(16):
            t_w = dma("sync", WRAW[:, (g % 2) * 512:(g % 2) * 512 + 512],
                      w_hbm[g * 128:(g + 1) * 128, s_off:s_off + 512],
                      wait=[(SC, sc_toks[g - 2] if g >= 2 else sc_gate)],
                      ch="wr%d" % (g % 2))

            def fc(e, g=g):
                sl = slice((g % 2) * 512, (g % 2) * 512 + 512)
                e.copy(WBF[:, (slot0 + g) * 512:(slot0 + g) * 512 + 512],
                       WRAW[:, sl])
                return e.activation(out=SW[:, (g % 2) * 512:(g % 2) * 512 + 512],
                                    in_=WRAW[:, sl], func=AF.Abs)
            st = op("scalar", fc,
                    wait=[t_w, (TE, te_means[g % 2])] +
                         ([(TE, wbf_free_te)] if g == 0 else []))
            sc_toks.append(st)
            op("tensor", lambda e, g=g: e.matmul(
                PS[0:16, 2048:2560], AONE[:, 32 - g:48 - g],
                SW[:, (g % 2) * 512:(g % 2) * 512 + 512],
                start=(g == 0), stop=(g == 15)),
                wait=[(SC, st)] + ([(DV, mean_free_dv)] if g == 0 else []))
            te_means[g % 2] = TE.v

        def fsm(e):
            e.tensor_scalar(out=TERN[0:16, 0:512], in0=PS[0:16, 2048:2560],
                            scalar1=1.0 / 128.0, scalar2=None, op0=ALU.mult)
            e.drain()
            e.tensor_copy(TERNB[0:16, 0:512], TERN[0:16, 0:512])
            e.drain()
            e.tensor_scalar(out=TERNB[0:16, 0:512], in0=TERNB[0:16, 0:512],
                            scalar1=9.99999993922529e-09, scalar2=None,
                            op0=ALU.max)
            e.drain()
            e.reciprocal(TERN[0:16, 0:512], TERNB[0:16, 0:512])
            e.drain()
            e.tensor_copy(TERNB[0:16, 512:1024], TERN[0:16, 0:512])
            e.drain()
            e.tensor_sub(ROWS[32:48, 0:512], TERN[0:16, 0:512],
                         TERNB[0:16, 512:1024])
            e.drain()
            return e.tensor_copy(TERNB[0:16, 1024:1536], ROWS[32:48, 0:512])
        if stop_at <= 1.1:
            return SC.v and DV.v
        dsm = op("vector", fsm, wait=[(TE, TE.v)])
        if stop_at <= 1.15:
            return dsm

        dv_last = dsm
        stg_te = TE.v
        sc_ev2 = [0, 0]
        for g in range(16):
            sl2 = (g % 2) * 512
            gs = dma("gpsimd", BROWS[0:1, 0:512],
                     TERNB[g:g + 1, 0:512],
                     wait=[(DV, dsm), (TE, stg_te)], ch="stg0")
            gs = dma("gpsimd", BROWS[0:1, 512:1024],
                     TERNB[g:g + 1, 512:1024], ch="stg0")
            gs = dma("gpsimd", BROWS[0:1, 1024:1536],
                     TERNB[g:g + 1, 1024:1536], ch="stg0")
            if stop_at <= 1.18:
                continue
            op("tensor", lambda e: e.matmul(
                PS[:, 2048:2560], ONESROW[0:1, :], BROWS[0:1, 512:1024],
                start=True, stop=False),
                wait=[gs, (SC, sc_ev2[g % 2])])
            op("tensor", lambda e: e.matmul(
                PS[:, 2048:2560], ONESROW[0:1, :], BROWS[0:1, 1024:1536],
                start=False, stop=True))
            op("tensor", lambda e: e.matmul(
                PS[:, 2560:3072], ONESROW[0:1, :], BROWS[0:1, 0:512],
                start=True, stop=True))
            stg_te = TE.v
            if stop_at <= 1.19:
                continue

            def fev(e, sl2=sl2):
                e.copy(SMF[:, sl2:sl2 + 512], PS[:, 2048:2560])
                return e.copy(PT[:, sl2:sl2 + 512], PS[:, 2560:3072])
            sc_ev = op("scalar", fev, wait=[(TE, stg_te)])
            sc_ev2[g % 2] = sc_ev

            def fv(e, g=g, sl2=sl2):
                wsl = slice((slot0 + g) * 512, (slot0 + g) * 512 + 512)
                e.tensor_mul(VW[:, 0:512], WBF[:, wsl], SMF[:, sl2:sl2 + 512])
                e.drain()
                e.tensor_scalar(out=VW[:, 512:1024], in0=VW[:, 0:512],
                                scalar1=-1.25, scalar2=1.25,
                                op0=ALU.max, op1=ALU.min)
                e.drain()
                e.tensor_scalar(out=VW[:, 0:512], in0=VW[:, 512:1024],
                                scalar1=CROUND, scalar2=CROUND,
                                op0=ALU.add, op1=ALU.subtract)
                e.drain()
                return e.tensor_mul(WBF[:, wsl], VW[:, 0:512],
                                    PT[:, sl2:sl2 + 512])
            dv_last = op("vector", fv, wait=[(SC, sc_ev), (DV, dv_last)])
        if stop_at <= 1.19:
            return dsm
        return dv_last

    def cast_prep(w_hbm, s_off, slot0, ncols=512, wbf_free_te=0):
        toks = [SC.v, SC.v]
        for g in range(16):
            t_w = dma("sync", WRAW[:, (g % 2) * 512:(g % 2) * 512 + ncols],
                      w_hbm[g * 128:(g + 1) * 128, s_off:s_off + ncols],
                      wait=[(SC, toks[g % 2])], ch="wr%d" % (g % 2))
            st = op("scalar", lambda e, g=g: e.copy(
                WBF[:, (slot0 + g) * 512:(slot0 + g) * 512 + ncols],
                WRAW[:, (g % 2) * 512:(g % 2) * 512 + ncols]),
                wait=[t_w] + ([(TE, wbf_free_te)] if g == 0 else []))
            toks[g % 2] = st
        return st

    def rowmax_to_scalar(scl_col, sc2_col, wait_dv):
        g1 = dma("gpsimd", scr128, SCL[:, scl_col:scl_col + 1],
                 wait=[(DV, wait_dv)], ch="msc")
        g2 = dma("gpsimd", RMAX[0:1, 0:128],
                 scr128.rearrange("p one -> (one) (p)"), wait=[g1], ch="msc")
        d = op("vector", lambda e: e.tensor_reduce(
            out=SC2[0:1, sc2_col:sc2_col + 1], in_=RMAX[0:1, 0:128],
            axis=mybir.AxisListType.X, op=ALU.max), wait=[g2])
        return d

    def bcast_scalars_f32(sc2_srcs, scl_dst0, wait_dv):
        n = len(sc2_srcs)

        def fhl(e):
            for i, c in enumerate(sc2_srcs):
                e.tensor_copy(BROWS[0:1, 1536 + i:1537 + i],
                              SC2[0:1, c:c + 1])
            e.drain()
            for i, c in enumerate(sc2_srcs):
                e.tensor_sub(SC2[0:1, 24 + i:25 + i], SC2[0:1, c:c + 1],
                             BROWS[0:1, 1536 + i:1537 + i])
            e.drain()
            for i in range(n):
                e.tensor_copy(TERNB[0:1, i:i + 1],
                              SC2[0:1, 24 + i:25 + i])
            return e.engine_nop()
        d_hl = op("vector", fhl, wait=[(DV, wait_dv)])
        for i in range(n):
            op("tensor", lambda e, i=i: e.matmul(
                PS[:, 2048 + i:2048 + i + 1], ONESROW[0:1, :],
                BROWS[0:1, 1536 + i:1537 + i], start=True, stop=False),
                wait=[(DV, d_hl)] if i == 0 else [])
            op("tensor", lambda e, i=i: e.matmul(
                PS[:, 2048 + i:2048 + i + 1], ONESROW[0:1, :],
                TERNB[0:1, i:i + 1], start=False, stop=True))
        d_sc = op("vector", lambda e: e.tensor_copy(
            SCL[:, scl_dst0:scl_dst0 + n], PS[:, 2048:2048 + n]),
            wait=[(TE, TE.v)])
        return d_sc

    def attention(kv_of_head, qsrc, ysrc):
        sbase = [2048, 3072]
        tail = [None]
        sub_tmr = {}
        last_exp = [0, 0]
        last_pt_te = [0, 0]
        last_yev = [0]
        prev_diag_dv = [0]
        unit_ctr = [0]

        def smf_ap(ph, s, ln):
            if ph == 0:
                return SMF[:, s * 1024:s * 1024 + ln]
            return (XCH[:, 0:ln] if s == 0 else XC2[:, 0:ln])

        def emit_head(hidx, j, h, kv):
            L2 = (j + 1) * 256
            nsub = 1 if L2 <= 1024 else 2
            ph = hidx % 2
            tmr_toks = []
            for s in range(nsub):
                lo = s * 1024
                ln = min(1024, L2 - lo)
                buf = unit_ctr[0] % 2
                unit_ctr[0] += 1
                w_sbuf = sub_tmr.get(buf, 0)
                first = True
                for (a, b) in _bank_splits(sbase[buf], sbase[buf] + ln):
                    op("tensor", lambda e, j=j, h=h, kv=kv, a=a, b=b, lo=lo, buf=buf: e.matmul(
                        PS[:, a:b],
                        qsrc[:, h * T + j * 128:h * T + j * 128 + 128],
                        KG[:, kv * 2048 + lo + (a - sbase[buf]):
                           kv * 2048 + lo + (b - sbase[buf])],
                        start=True, stop=True),
                        wait=([(DV, w_sbuf)] if first else []))
                    first = False
                te_s = TE.v
                lens_ap = LENS if s == 0 else LENS2

                def fm(e, j=j, s=s, lo=lo, ln=ln, buf=buf, nsub=nsub,
                       lens_ap=lens_ap, ph=ph):
                    if s > 0:
                        e.drain()
                    e.tensor_mask_reduce(
                        out=smf_ap(ph, s, ln), in_=PS[:, sbase[buf]:sbase[buf] + ln],
                        mask_start=0.0, mask_end=lens_ap[:, j:j + 1], scale=-0.5,
                        accum_in=(POSBIG if s == 0 else SCL[:, 3 * ph:3 * ph + 1]),
                        op=ALU.min, accum_out=SCL[:, 3 * ph:3 * ph + 1])
                    if s == nsub - 1:
                        e.drain()
                        return e.tensor_scalar(
                            out=SCL[:, 1 + 3 * ph:2 + 3 * ph],
                            in0=SCL[:, 3 * ph:3 * ph + 1],
                            scalar1=2.0, scalar2=None, op0=ALU.mult)
                    return e.engine_nop()
                dv_m = op("vector", fm,
                          wait=[(TE, te_s)] +
                               ([(SC, last_exp[ph])] if s == 0 else []))
                sub_tmr[(unit_ctr[0] - 1) % 2] = dv_m
                tmr_toks.append(dv_m)
            sc_e = 0
            for s in range(nsub):
                lo = s * 1024
                ln = min(1024, L2 - lo)
                sc_e = op("scalar", lambda e, lo=lo, ln=ln, s=s, ph=ph: e.activation(
                    out=PPB[:, ph * 4096 + lo:ph * 4096 + lo + ln],
                    in_=smf_ap(ph, s, ln), func=AF.Exp,
                    bias=SCL[:, 1 + 3 * ph:2 + 3 * ph], scale=-2.0,
                    accum_out=(SCL[:, 2 + 3 * ph:3 + 3 * ph] if s == 0 else
                               SCL[:, 8 + ph:9 + ph])),
                    wait=[(DV, tmr_toks[-1])] +
                         ([(TE, last_pt_te[ph])] if s == 0 else []))
            last_exp[ph] = sc_e
            return (j, h, kv, L2, ph, sc_e, nsub)

        def emit_tail(args):
            j, h, kv, L2, ph, sc_e, nsub = args

            def fd(e):
                if nsub == 2:
                    e.tensor_add(SCL[:, 2 + 3 * ph:3 + 3 * ph],
                                 SCL[:, 2 + 3 * ph:3 + 3 * ph],
                                 SCL[:, 8 + ph:9 + ph])
                    e.drain()
                e.reciprocal(SCL[:, 6 + ph:7 + ph],
                             SCL[:, 2 + 3 * ph:3 + 3 * ph])
                e.drain()
                return e.tensor_scalar(out=VW[:, 0:128], in0=IDENT[:],
                                       scalar1=SCL[:, 6 + ph:7 + ph],
                                       scalar2=None, op0=ALU.mult)
            dv_d = op("vector", fd, wait=[(SC, sc_e), (DV, prev_diag_dv[0])])
            prev_diag_dv[0] = dv_d
            nb = L2 // 128
            for b4 in range(0, nb, 4):
                bb = min(4, nb - b4)
                bank = (b4 // 4) % 2
                for b in range(b4, b4 + bb):
                    op("tensor", lambda e, b=b, b4=b4, bank=bank, ph=ph: e.matmul(
                        PS[:, bank * 512 + (b - b4) * 128:
                           bank * 512 + (b - b4) * 128 + 128],
                        PPB[:, ph * 4096 + b * 128:ph * 4096 + b * 128 + 128],
                        VW[:, 0:128], start=True, stop=True),
                        wait=([(DV, dv_d if b4 == 0 else DV.v)]
                              if b == b4 else []))
                te_pt = TE.v
                dv_e = op("scalar", lambda e, bank=bank, bb=bb: e.copy(
                    PT[:, bank * 512:bank * 512 + bb * 128],
                    PS[:, bank * 512:bank * 512 + bb * 128]),
                    wait=[(TE, te_pt)])
                for b in range(b4, b4 + bb):
                    op("tensor", lambda e, b=b, b4=b4, bank=bank, kv=kv, nb=nb: e.matmul(
                        PS[:, 1024:1152],
                        VG[:, kv * 2048 + b * 128:kv * 2048 + b * 128 + 128],
                        PT[:, bank * 512 + (b - b4) * 128:
                           bank * 512 + (b - b4) * 128 + 128],
                        start=(b == 0), stop=(b == nb - 1)),
                        wait=([(SC, dv_e)] if b == b4 else
                              ([(DV, last_yev[0])] if b == 0 else [])))
            last_pt_te[ph] = TE.v
            last_yev[0] = op("vector", lambda e, j=j, h=h: e.tensor_copy(
                ysrc[:, h * T + j * 128:h * T + j * 128 + 128],
                PS[:, 1024:1152]), wait=[(TE, TE.v)])

        hidx = 0
        for j in range(8):
            for h in range(16):
                args = emit_head(hidx, j, h, kv_of_head(h))
                hidx += 1
                if tail[0] is not None:
                    emit_tail(tail[0])
                tail[0] = args
        emit_tail(tail[0])

    def rms_bcast_row(row_ap_bf, ps_lo, base=0, wait=()):
        for hf in range(2):
            op("tensor", lambda e, hf=hf: e.matmul(
                PS[:, ps_lo + hf * 512:ps_lo + hf * 512 + 512],
                ONESROW[base:base + 1, :],
                row_ap_bf[:, hf * 512:hf * 512 + 512], start=True, stop=True),
                wait=(list(wait) if hf == 0 else []))
        return TE.v

    wmax_state = {"d": 0, "first": {10: True, 11: True}, "done": {}}
    wmax_tiles = ([("fc", g, c0, 10) for g in range(16) for c0 in range(0, 8192, 1024)] +
                  [("pr", g, c0, 11) for g in range(64) for c0 in range(0, 2048, 1024)])

    def emit_wmax_tiles(n):
        cnt = 0
        while wmax_tiles and cnt < n:
            kind, g, c0, dst_scl = wmax_tiles.pop(0)
            wsrc = wfcT if kind == "fc" else wprT
            t_w = dma("sync", WRAW[:], wsrc[g * 128:(g + 1) * 128, c0:c0 + 1024],
                      wait=[(DV, wmax_state["d"])], ch="wr0")
            first = wmax_state["first"][dst_scl]
            wmax_state["first"][dst_scl] = False

            def f(e, first=first, dst_scl=dst_scl):
                e.tensor_reduce(out=SCL[:, 12:13], in_=WRAW[:],
                                axis=mybir.AxisListType.X, op=ALU.max,
                                apply_absolute_value=True)
                e.drain()
                if first:
                    return e.tensor_copy(SCL[:, dst_scl:dst_scl + 1],
                                         SCL[:, 12:13])
                return e.tensor_tensor(SCL[:, dst_scl:dst_scl + 1],
                                       SCL[:, dst_scl:dst_scl + 1],
                                       SCL[:, 12:13], op=ALU.max)
            wmax_state["d"] = op("vector", f, wait=[t_w])
            cnt += 1

    def finish_wmax():
        emit_wmax_tiles(10 ** 9)
        wmax_state["done"]["dwf"] = rowmax_to_scalar(10, 0, wmax_state["d"])
        wmax_state["done"]["dwp"] = rowmax_to_scalar(11, 1, wmax_state["d"])

    def block_l0_attn(src_hbm, dst_hbm):
        def phase_done(n):
            return PHASE_LIMIT <= n
        te_st = ln_stats_hbm(src_hbm, XN)
        te_bc = ln_smalls(te_st)
        xn_tok = ln_apply(src_hbm, te_bc, XN)
        barrier()
        if phase_done(1):
            return

        ccA_in_v = ccA_in.rearrange("a (k p c) -> a k p c", p=128, c=1024)
        prep_tok = tern_prep(wqkvT, 0, 0)
        if phase_done(1.2):
            barrier()
            return
        te_mm_end = [0, 0]
        ss_te = [0, 0]
        evac_tok = [0, 0]
        for s in range(6 if not phase_done(1.5) else 1):
            slot0 = (s % 2) * 16
            for o4 in range(4):
                oc = s * 4 + o4
                pb = (o4 % 2) * 1024
                for hf in range(2):
                    for g in range(CH):
                        wl = []
                        if g == 0 and hf == 0:
                            if o4 == 0:
                                wl = [(DV, prep_tok), (DV, xn_tok)]
                            wl = wl + [evac_tok[o4 % 2]] if evac_tok[o4 % 2] else wl
                        op("tensor", lambda e, slot0=slot0, g=g, o4=o4, hf=hf, pb=pb: e.matmul(
                            PS[:, pb + hf * 512:pb + hf * 512 + 512],
                            WBF[:, (slot0 + g) * 512 + o4 * 128:
                                (slot0 + g) * 512 + o4 * 128 + 128],
                            XN[:, g * T + hf * 512:g * T + hf * 512 + 512],
                            start=(g == 0), stop=(g == CH - 1)), wait=wl)
                te_mm = TE.v
                if oc < 16:
                    dst = QB[:, oc * T:(oc + 1) * T]
                elif oc < 20:
                    dst = KG[:, (oc - 16) * 2048:(oc - 16) * 2048 + 1024]
                else:
                    dst = PPB[:, (oc - 20) * 1024:(oc - 20) * 1024 + 1024]
                sc_ev = op("scalar", lambda e, dst=dst, pb=pb: e.copy(
                    dst, PS[:, pb:pb + 1024]), wait=[(TE, te_mm)])
                evac_tok[o4 % 2] = (SC, sc_ev)
                if oc < 20:
                    dv_sq = op("vector", lambda e, dst=dst, oc=oc: e.tensor_mul(
                        SQ[:, (oc % 2) * 1024:(oc % 2) * 1024 + 1024],
                        dst, dst), wait=[(SC, sc_ev), (TE, ss_te[oc % 2])])
                    for hf in range(2):
                        op("tensor", lambda e, oc=oc, hf=hf: e.matmul(
                            PS[0:32, 3072 + hf * 512:3072 + hf * 512 + 512],
                            AONE[:, 32 - oc:64 - oc],
                            SQ[:, (oc % 2) * 1024 + hf * 512:
                               (oc % 2) * 1024 + hf * 512 + 512],
                            start=(oc == 0), stop=(oc == 19)),
                            wait=[(DV, dv_sq)])
                    ss_te[oc % 2] = TE.v
            te_mm_end[s % 2] = TE.v
            if s < 5 and not phase_done(1.5):
                prep_tok = tern_prep(wqkvT, (s + 1) * 512, ((s + 1) % 2) * 16,
                                     mean_free_dv=prep_tok,
                                     wbf_free_te=te_mm_end[(s + 1) % 2])
        barrier()
        if phase_done(1.5):
            return
        if phase_done(2):
            return

        d_r = op("vector", lambda e: e.tensor_scalar(
            out=ROWS[0:20, 0:1024], in0=PS[0:20, 3072:4096],
            scalar1=1.0 / 128.0, scalar2=None, op0=ALU.mult))
        s_r = op("scalar", lambda e: e.activation(
            out=ROWS[32:52, 0:1024], in_=ROWS[0:20, 0:1024], func=AF.Sqrt,
            bias=CEPS[0:20, 1:2], scale=1.0), wait=[(DV, d_r)])

        def frms2(e):
            e.reciprocal(ROWS[0:20, 0:1024], ROWS[32:52, 0:1024])
            e.drain()
            e.tensor_scalar(out=ROWS[0:16, 0:1024], in0=ROWS[0:16, 0:1024],
                            scalar1=QGC[:, 0:1], scalar2=None, op0=ALU.mult)
            e.drain()
            return e.tensor_copy(BROWS[0:20, 1024:2048], ROWS[0:20, 0:1024])
        d_r2 = op("vector", frms2, wait=[(SC, s_r)])

        d_rope = d_r2
        te_b = 0
        for idx in range(20):
            if idx < 16:
                src_c = QB[:, idx * T:(idx + 1) * T]
            else:
                src_c = KG[:, (idx - 16) * 2048:(idx - 16) * 2048 + 1024]
            g1 = dma("gpsimd", SW[0:64, :], src_c[64:128, :],
                     wait=[(DV, d_rope)], ch="swp")
            g2 = dma("gpsimd", SW[64:128, :], src_c[0:64, :], ch="swp")
            gs = dma("gpsimd", BROWS[0:1, 0:1024],
                     BROWS[idx:idx + 1, 1024:2048],
                     wait=[(DV, d_r2), (TE, te_b)], ch="stg0")
            te_b = rms_bcast_row(BROWS[0:1, 0:1024], 2048, base=0,
                                 wait=[(DV, d_rope), gs])

            def frope(e, src_c=src_c):
                e.tensor_mul(XCH[:], src_c, COSX[:])
                e.tensor_mul(XC2[:], SW[:], SINX[:])
                e.drain()
                e.tensor_add(XCH[:], XCH[:], XC2[:])
                e.drain()
                return e.tensor_mul(src_c, XCH[:], PS[:, 2048:3072])
            d_rope = op("vector", frope, wait=[g2, (TE, te_b)])
            if idx >= 16:
                dma("gpsimd", ccA_in_v[0, idx - 16], src_c,
                    wait=[(DV, d_rope)], ch="blob")
        dv_prev = d_rope
        for vv in range(4):
            for b in range(8):
                te_t = op("tensor", lambda e, vv=vv, b=b: e.transpose(
                    PS[:, 2048 + (b % 2) * 512:2048 + (b % 2) * 512 + 64].bitcast(BF16),
                    PPB[:, vv * 1024 + b * 128:vv * 1024 + b * 128 + 128],
                    IDENT[:]), wait=[(DV, dv_prev)])
                dv_prev = op("vector", lambda e, vv=vv, b=b: e.tensor_copy(
                    VG[:, vv * 2048 + b * 128:vv * 2048 + b * 128 + 128],
                    PS[:, 2048 + (b % 2) * 512:2048 + (b % 2) * 512 + 64].bitcast(BF16)),
                    wait=[(TE, te_t)])
            dma("gpsimd", ccA_in_v[0, 4 + vv],
                VG[:, vv * 2048:vv * 2048 + 1024], wait=[(DV, dv_prev)],
                ch="blob")
        barrier()
        if phase_done(3):
            return

        ccc = chan("cc")
        blobc = chan("blob")
        cc_tok = (ccc, ccc.bump(1))
        blob_full = (blobc, blobc.v)

        def fcc_emit(e, bf=blob_full):
            e.wait_ge(bf[0].sem, bf[1])
            inst = e.collective_compute(
                "AllGather", ALU.bypass,
                replica_groups=[[0, 1], [2, 3], [4, 5], [6, 7]],
                ins=[ccA_in], outs=[ccA_out])
            inst.then_inc(ccc.sem, 1)
        streams["gpsimd"].append(fcc_emit)

        ccA_out_v = ccA_out.rearrange("a (r k p c) -> a r k p c",
                                      r=2, p=128, c=1024)
        for kk in range(4):
            for par in range(2):
                dma("gpsimd",
                    KG[:, kk * 2048:kk * 2048 + 2048].rearrange(
                        "p (b two c) -> p b two c", two=2, c=128)[:, :, par],
                    ccA_out_v[0, par, kk].rearrange("p (b c) -> p b c", c=128),
                    wait=([cc_tok] if (kk == 0 and par == 0) else []), ch="gath")
        for vv in range(4):
            for par in range(2):
                dma("gpsimd",
                    VG[:, vv * 2048:vv * 2048 + 2048].rearrange(
                        "p (b two c) -> p b two c", two=2, c=128)[:, :, par],
                    ccA_out_v[0, par, 4 + vv].rearrange("p (b c) -> p b c",
                                                        c=128), ch="gath")
        barrier()
        if phase_done(4):
            return

        attention(lambda h: h // 4, QB, XN)
        barrier()
        if phase_done(5):
            return

        te_ys = ln_stats_sbuf(lambda c: (BIG, c * T))

        def fyr(e):
            return e.tensor_scalar(out=ROWS[0:1, 0:1024],
                                   in0=PS[32:33, 3072:4096],
                                   scalar1=1.0 / D, scalar2=None, op0=ALU.mult)
        d_y = op("vector", fyr, wait=[(TE, te_ys)])
        s_y = op("scalar", lambda e: e.activation(
            out=ROWS[32:33, 0:1024], in_=ROWS[0:1, 0:1024], func=AF.Sqrt,
            bias=CEPS[0:1, 1:2], scale=1.0), wait=[(DV, d_y)])

        def fyr2(e):
            e.reciprocal(ROWS[0:1, 0:1024], ROWS[32:33, 0:1024])
            e.drain()
            return e.tensor_copy(BROWS[0:1, 0:1024], ROWS[0:1, 0:1024])
        d_y2 = op("vector", fyr2, wait=[(SC, s_y)])
        te_yb = rms_bcast_row(BROWS[0:1, 0:1024], 0, wait=[(DV, d_y2)])
        yn_tok = 0
        for c in range(CH):
            yn_tok = op("vector", lambda e, c=c: e.tensor_mul(
                QB[:, c * T:(c + 1) * T], XN[:, c * T:(c + 1) * T],
                PS[:, 0:1024]), wait=[(TE, te_yb)])
        barrier()

        prep_tok = tern_prep(wprojT, 0, 0)
        te_mm_end = [0, 0]
        evac_tok = [None, None]
        for s in range(4):
            slot0 = (s % 2) * 16
            for o4 in range(4):
                oc = s * 4 + o4
                pb = (o4 % 2) * 1024
                for hf in range(2):
                    for g in range(CH):
                        wl = []
                        if g == 0 and hf == 0:
                            if o4 == 0:
                                wl = [(DV, prep_tok), (DV, yn_tok)]
                            if evac_tok[o4 % 2]:
                                wl = wl + [evac_tok[o4 % 2]]
                        op("tensor", lambda e, slot0=slot0, g=g, o4=o4, hf=hf, pb=pb: e.matmul(
                            PS[:, pb + hf * 512:pb + hf * 512 + 512],
                            WBF[:, (slot0 + g) * 512 + o4 * 128:
                                (slot0 + g) * 512 + o4 * 128 + 128],
                            QB[:, g * T + hf * 512:g * T + hf * 512 + 512],
                            start=(g == 0), stop=(g == CH - 1)), wait=wl)
                te_mm = TE.v
                sc_ev = op("scalar", lambda e, oc=oc, pb=pb: e.copy(
                    XN[:, oc * T:(oc + 1) * T], PS[:, pb:pb + 1024]),
                    wait=[(TE, te_mm)])
                evac_tok[o4 % 2] = (SC, sc_ev)
            te_mm_end[s % 2] = TE.v
            if s < 3:
                prep_tok = tern_prep(wprojT, (s + 1) * 512, ((s + 1) % 2) * 16,
                                     mean_free_dv=prep_tok,
                                     wbf_free_te=te_mm_end[(s + 1) % 2])
        barrier()

        te_st2 = ln_stats_sbuf(lambda c: (BIG, c * T))
        te_bc2 = ln_smalls(te_st2)
        ln2_residual(lambda c: (BIG, c * T), src_hbm, dst_hbm, te_bc2)
        barrier()

    def block_l0_ffn(src_hbm, dst_hbm):
        te_st = ln_stats_hbm(src_hbm, XN)
        te_bc = ln_smalls(te_st)
        xn_tok = ln_apply(src_hbm, te_bc, XN)
        barrier()

        te_mm_last = 0
        ev_last = {}
        for sc_i in range(11):
            prep_toks = []
            u8_tok = [DV.v] * 4
            for w_i, wsrc in ((0, wqgT), (16, wquT)):
                for g in range(16):
                    slot = (0 if w_i == 0 else 2) + (g % 2)
                    t_w = dma("sync", WU8[:, slot * 512:slot * 512 + 512],
                              wsrc[g * 128:(g + 1) * 128,
                                   sc_i * 512:sc_i * 512 + 512],
                              wait=[(DV, u8_tok[slot])], ch="wu%d" % slot)
                    tok = op("vector", lambda e, w_i=w_i, g=g, slot=slot: e.tensor_scalar(
                        out=WBF[:, (w_i + g) * 512:(w_i + g) * 512 + 512],
                        in0=WU8[:, slot * 512:slot * 512 + 512],
                        scalar1=128.0, scalar2=None, op0=ALU.subtract),
                        wait=[t_w] +
                             ([(TE, te_mm_last)] if (w_i == 0 and g == 0) else []))
                    u8_tok[slot] = tok
                    prep_toks.append(tok)
            prep_done = prep_toks[-1]
            for q_i in range(4):
                hc = sc_i * 4 + q_i
                gb = 0
                ub = 1024
                for hf in range(2):
                    for g in range(CH):
                        wl = []
                        if g == 0 and hf == 0:
                            wl = [(DV, prep_done)] + \
                                ([(DV, xn_tok)] if hc == 0 else []) + \
                                (ev_last.get(0, []))
                        op("tensor", lambda e, g=g, q_i=q_i, hf=hf, gb=gb: e.matmul(
                            PS[:, gb + hf * 512:gb + hf * 512 + 512],
                            WBF[:, g * 512 + q_i * 128:g * 512 + q_i * 128 + 128],
                            XN[:, g * T + hf * 512:g * T + hf * 512 + 512],
                            start=(g == 0), stop=(g == CH - 1)), wait=wl)
                    for g in range(CH):
                        op("tensor", lambda e, g=g, q_i=q_i, hf=hf, ub=ub: e.matmul(
                            PS[:, ub + hf * 512:ub + hf * 512 + 512],
                            WBF[:, (16 + g) * 512 + q_i * 128:
                                (16 + g) * 512 + q_i * 128 + 128],
                            XN[:, g * T + hf * 512:g * T + hf * 512 + 512],
                            start=(g == 0), stop=(g == CH - 1)))
                te_mm = TE.v
                s_si = op("scalar", lambda e, hc=hc, gb=gb: e.activation(
                    out=VW[:], in_=PS[:, gb:gb + 1024], func=AF.Silu,
                    bias=0.0, scale=SGC[:, hc:hc + 1]), wait=[(TE, te_mm)])
                tns, off = tcol(hc)
                d_t = op("vector", lambda e, tns=tns, off=off, hc=hc, ub=ub: e.scalar_tensor_tensor(
                    out=tns[:, off:off + T], in0=PS[:, ub:ub + 1024],
                    scalar=SUC[:, hc:hc + 1], in1=VW[:],
                    op0=ALU.mult, op1=ALU.mult), wait=[(SC, s_si)])
                ev_last[0] = [(SC, s_si), (DV, d_t)]
                emit_wmax_tiles(5)
            te_mm_last = TE.v
        barrier()

        te_mm_last = 0
        ev_last = [None, None]
        ust_last = [None]
        for os_i in range(8):
            prep_toks = []
            u8_tok = [DV.v] * 4
            for g in range(44):
                slot = g % 4
                t_w = dma("sync", WU8[:, slot * 512:slot * 512 + 256],
                          wqdT[g * 128:(g + 1) * 128,
                               os_i * 256:os_i * 256 + 256],
                          wait=[(DV, u8_tok[slot])], ch="wu%d" % slot)
                tok = op("vector", lambda e, g=g, slot=slot: e.tensor_scalar(
                    out=BIG[:, g * 256:g * 256 + 256],
                    in0=WU8[:, slot * 512:slot * 512 + 256],
                    scalar1=128.0, scalar2=None, op0=ALU.subtract),
                    wait=[t_w] +
                         ([(TE, te_mm_last)] if g == 0 else []))
                u8_tok[slot] = tok
                prep_toks.append(tok)
            prep_done = prep_toks[-1]
            for oc2 in range(2):
                oc = os_i * 2 + oc2
                pb = oc2 * 1024
                for hf in range(2):
                    for g in range(44):
                        tns, off = tcol(g)
                        wl = []
                        if g == 0 and hf == 0:
                            wl = [(DV, prep_done)]
                            if ev_last[oc2]:
                                wl = wl + [ev_last[oc2]]
                        op("tensor", lambda e, g=g, oc2=oc2, hf=hf, pb=pb, tns=tns, off=off: e.matmul(
                            PS[:, pb + hf * 512:pb + hf * 512 + 512],
                            BIG[:, g * 256 + oc2 * 128:g * 256 + oc2 * 128 + 128],
                            tns[:, off + hf * 512:off + hf * 512 + 512],
                            start=(g == 0), stop=(g == 43)), wait=wl)
                te_mm = TE.v
                d_ev = op("vector", lambda e, oc=oc, pb=pb: e.tensor_scalar(
                    out=PT[:], in0=PS[:, pb:pb + 1024],
                    scalar1=SDC[:, oc:oc + 1], scalar2=None, op0=ALU.mult),
                    wait=[(TE, te_mm)] + ([ust_last[0]] if ust_last[0] else []))
                ust_last[0] = dma("gpsimd", xview(u_dram, oc), PT[:],
                                  wait=[(DV, d_ev)], ch="ust")
                ev_last[oc2] = (DV, d_ev)
                emit_wmax_tiles(4)
            te_mm_last = TE.v
        finish_wmax()
        barrier()

        te_st2 = ln_stats_udram(u_dram)
        te_bc2 = ln_smalls(te_st2)
        ln2_residual(None, src_hbm, dst_hbm, te_bc2, u_hbm=u_dram)
        barrier()

    def block_l1_attn(src_hbm, dst_hbm):
        te_st = ln_stats_hbm(src_hbm, XN)
        te_bc = ln_smalls(te_st)
        xn_tok = ln_apply(src_hbm, te_bc, XN)
        barrier()

        ccB_in_v = ccB_in.rearrange("a (k p c) -> a k p c", p=128, c=1024)
        prep_tok = cast_prep(l1wT, 0, 0)
        te_mm_end = [0, 0]
        evac_tok = [None, None]
        for s in range(5):
            slot0 = (s % 2) * 16
            ncols = 512 if s < 4 else 256
            for o4 in range(ncols // 128):
                oc = s * 4 + o4
                pb = (o4 % 2) * 1024
                for hf in range(2):
                    for g in range(CH):
                        wl = []
                        if g == 0 and hf == 0:
                            if o4 == 0:
                                wl = [(SC, prep_tok), (DV, xn_tok)]
                            if evac_tok[o4 % 2]:
                                wl = wl + [evac_tok[o4 % 2]]
                        op("tensor", lambda e, slot0=slot0, g=g, o4=o4, hf=hf, pb=pb: e.matmul(
                            PS[:, pb + hf * 512:pb + hf * 512 + 512],
                            WBF[:, (slot0 + g) * 512 + o4 * 128:
                                (slot0 + g) * 512 + o4 * 128 + 128],
                            XN[:, g * T + hf * 512:g * T + hf * 512 + 512],
                            start=(g == 0), stop=(g == CH - 1)), wait=wl)
                te_mm = TE.v
                if oc < 16:
                    d_e = op("vector", lambda e, oc=oc, pb=pb: e.tensor_scalar(
                        out=QB[:, oc * T:(oc + 1) * T], in0=PS[:, pb:pb + 1024],
                        scalar1=L1BI[:, oc:oc + 1], scalar2=INV_SQRT_HD,
                        op0=ALU.add, op1=ALU.mult), wait=[(TE, te_mm)])
                    evac_tok[o4 % 2] = (DV, d_e)
                elif oc == 16:
                    d_e = op("vector", lambda e, pb=pb: e.tensor_scalar(
                        out=RT[:, 0:1024], in0=PS[:, pb:pb + 1024],
                        scalar1=L1BI[:, 16:17], scalar2=None, op0=ALU.add),
                        wait=[(TE, te_mm)])
                    dma("gpsimd", ccB_in_v[0, 0], RT[:, 0:1024],
                        wait=[(DV, d_e)], ch="blob")
                    evac_tok[o4 % 2] = (DV, d_e)
                else:
                    d_e = op("vector", lambda e, pb=pb: e.tensor_scalar(
                        out=RT[:, 1024:2048], in0=PS[:, pb:pb + 1024],
                        scalar1=L1BI[:, 17:18], scalar2=None, op0=ALU.add),
                        wait=[(TE, te_mm)])
                    evac_tok[o4 % 2] = (DV, d_e)
                    dv_prev = d_e
                    for b in range(8):
                        te_t = op("tensor", lambda e, b=b: e.transpose(
                            PS[:, 2048 + (b % 2) * 512:2048 + (b % 2) * 512 + 64].bitcast(BF16),
                            RT[:, 1024 + b * 128:1024 + b * 128 + 128],
                            IDENT[:]), wait=[(DV, dv_prev)])
                        dv_prev = op("vector", lambda e, b=b: e.tensor_copy(
                            VG[:, b * 128:b * 128 + 128],
                            PS[:, 2048 + (b % 2) * 512:2048 + (b % 2) * 512 + 64].bitcast(BF16)),
                            wait=[(TE, te_t)])
                    dma("gpsimd", ccB_in_v[0, 1], VG[:, 0:1024],
                        wait=[(DV, dv_prev)], ch="blob")
            te_mm_end[s % 2] = TE.v
            if s < 4:
                prep_tok = cast_prep(l1wT, (s + 1) * 512, ((s + 1) % 2) * 16,
                                     ncols=(512 if s < 3 else 256),
                                     wbf_free_te=te_mm_end[(s + 1) % 2])
        barrier()

        ccc = chan("cc")
        blobc = chan("blob")
        cc_tok = (ccc, ccc.bump(1))
        blob_full = (blobc, blobc.v)

        def fcc_emit(e, bf=blob_full):
            e.wait_ge(bf[0].sem, bf[1])
            inst = e.collective_compute(
                "AllGather", ALU.bypass,
                replica_groups=[[0, 1], [2, 3], [4, 5], [6, 7]],
                ins=[ccB_in], outs=[ccB_out])
            inst.then_inc(ccc.sem, 1)
        streams["gpsimd"].append(fcc_emit)
        ccB_out_v = ccB_out.rearrange("a (r k p c) -> a r k p c",
                                      r=2, p=128, c=1024)
        for par in range(2):
            dma("gpsimd",
                KG[:, 0:2048].rearrange("p (b two c) -> p b two c",
                                        two=2, c=128)[:, :, par],
                ccB_out_v[0, par, 0].rearrange("p (b c) -> p b c", c=128),
                wait=([cc_tok] if par == 0 else []), ch="gath")
            dma("gpsimd",
                VG[:, 0:2048].rearrange("p (b two c) -> p b two c",
                                        two=2, c=128)[:, :, par],
                ccB_out_v[0, par, 1].rearrange("p (b c) -> p b c", c=128),
                ch="gath")
        barrier()

        attention(lambda h: 0, QB, XN)
        barrier()

        prep_tok = cast_prep(l1owT, 0, 0)
        te_mm_end = [0, 0]
        evac_tok = [None, None]
        for s in range(4):
            slot0 = (s % 2) * 16
            for o4 in range(4):
                oc = s * 4 + o4
                pb = (o4 % 2) * 1024
                for hf in range(2):
                    for g in range(CH):
                        wl = []
                        if g == 0 and hf == 0:
                            if o4 == 0:
                                wl = [(SC, prep_tok)]
                            if evac_tok[o4 % 2]:
                                wl = wl + [evac_tok[o4 % 2]]
                        op("tensor", lambda e, slot0=slot0, g=g, o4=o4, hf=hf, pb=pb: e.matmul(
                            PS[:, pb + hf * 512:pb + hf * 512 + 512],
                            WBF[:, (slot0 + g) * 512 + o4 * 128:
                                (slot0 + g) * 512 + o4 * 128 + 128],
                            XN[:, g * T + hf * 512:g * T + hf * 512 + 512],
                            start=(g == 0), stop=(g == CH - 1)), wait=wl)
                te_mm = TE.v
                d_e = op("vector", lambda e, oc=oc, pb=pb: e.tensor_scalar(
                    out=QB[:, oc * T:(oc + 1) * T], in0=PS[:, pb:pb + 1024],
                    scalar1=L1OB[:, oc:oc + 1], scalar2=None, op0=ALU.add),
                    wait=[(TE, te_mm)])
                evac_tok[o4 % 2] = (DV, d_e)
            te_mm_end[s % 2] = TE.v
            if s < 3:
                prep_tok = cast_prep(l1owT, (s + 1) * 512, ((s + 1) % 2) * 16,
                                     wbf_free_te=te_mm_end[(s + 1) % 2])
        barrier()

        te_st2 = ln_stats_sbuf(lambda c: (BIG, 16384 + c * T))
        te_bc2 = ln_smalls(te_st2)
        ln2_residual(lambda c: (BIG, 16384 + c * T), src_hbm, dst_hbm, te_bc2)
        barrier()

    def block_l1_ffn(src_hbm, dst_hbm):
        def phase_done(n):
            return PHASE_LIMIT <= n
        def fws(e):
            e.tensor_scalar(out=SC2[0:1, 2:4], in0=SC2[0:1, 0:2],
                            scalar1=1.0 / 448.0, scalar2=1e-6,
                            op0=ALU.mult, op1=ALU.max)
            e.drain()
            e.reciprocal(SC2[0:1, 6:8], SC2[0:1, 2:4])
            e.drain()
            return e.tensor_scalar(out=SC2[0:1, 4:6], in0=SC2[0:1, 6:8],
                                   scalar1=0.5, scalar2=None, op0=ALU.mult)
        d_ws = op("vector", fws,
                  wait=[(DV, wmax_state["done"]["dwf"]),
                        (DV, wmax_state["done"]["dwp"])])
        barrier()
        if phase_done(3.1):
            return

        for c in range(CH):
            t_x = dma("sync", XCH[:], xview(src_hbm, c),
                      wait=[(TE, TE.v if c > 0 else 0)], ch="xch")
            sc_tok = op("scalar", lambda e: e.copy(SW[:], XCH[:]),
                        wait=[t_x])
            dv_tok = op("vector", lambda e, c=c: e.tensor_mul(
                SQ[:, (c % 2) * 1024:(c % 2) * 1024 + 1024], SW[:], SW[:]),
                wait=[(SC, sc_tok)])
            for hf in range(2):
                lo = 3072 + hf * 512
                op("tensor", lambda e, c=c, hf=hf, lo=lo: e.matmul(
                    PS[0:1, lo:lo + 512], AONE[:, 32:33],
                    SW[:, hf * 512:hf * 512 + 512],
                    start=(c == 0), stop=(c == CH - 1)), wait=[(SC, sc_tok)])
                op("tensor", lambda e, c=c, hf=hf, lo=lo: e.matmul(
                    PS[32:33, lo:lo + 512], AONE[:, 32:33],
                    SQ[:, (c % 2) * 1024 + hf * 512:(c % 2) * 1024 + hf * 512 + 512],
                    start=(c == 0), stop=(c == CH - 1)), wait=[(DV, dv_tok)])
        te_bc = ln_smalls(TE.v, fp32_bcast=True)
        barrier()
        if phase_done(3.2):
            return

        first = True
        d_mx = 0
        sc_gate = SC.v
        for c in range(CH):
            t_x = dma("sync", XCH[:], xview(src_hbm, c),
                      wait=[(DV, d_mx)] + ([(SC, sc_gate)] if c == 0 else []),
                      ch="xch")

            def f(e, first=first):
                e.tensor_mul(XC2[:], XCH[:], PS[:, 0:1024])
                e.drain()
                e.tensor_add(XC2[:], XC2[:], PS[:, 1024:2048])
                e.drain()
                e.tensor_reduce(out=SCL[:, 10:11], in_=XC2[:],
                                axis=mybir.AxisListType.X, op=ALU.max,
                                apply_absolute_value=True)
                e.drain()
                if first:
                    return e.tensor_copy(SCL[:, 13:14], SCL[:, 10:11])
                return e.tensor_tensor(SCL[:, 13:14], SCL[:, 13:14],
                                       SCL[:, 10:11], op=ALU.max)
            d_mx = op("vector", f, wait=[t_x, (TE, te_bc)])
            first = False
        d_xm = rowmax_to_scalar(13, 8, d_mx)
        g2 = dma("gpsimd", ccm_in, SC2[0:1, 8:9], wait=[(DV, d_xm)], ch="msc")
        ccc = chan("cc")
        cc1 = (ccc, ccc.bump(1))

        def fcc1(e, g2=g2):
            e.wait_ge(g2[0].sem, g2[1])
            inst = e.collective_compute(
                "AllReduce", ALU.max,
                replica_groups=[[0, 1, 2, 3, 4, 5, 6, 7]],
                ins=[ccm_in], outs=[ccm_out])
            inst.then_inc(ccc.sem, 1)
        streams["gpsimd"].append(fcc1)
        g3 = dma("gpsimd", SC2[0:1, 9:10], ccm_out, wait=[cc1], ch="msc")

        def fxs(e):
            e.tensor_scalar(out=SC2[0:1, 10:11], in0=SC2[0:1, 9:10],
                            scalar1=1.0 / 448.0, scalar2=1e-6,
                            op0=ALU.mult, op1=ALU.max)
            e.drain()
            e.reciprocal(SC2[0:1, 13:14], SC2[0:1, 10:11])
            e.drain()
            e.tensor_scalar(out=SC2[0:1, 11:12], in0=SC2[0:1, 13:14],
                            scalar1=0.5, scalar2=None, op0=ALU.mult)
            e.drain()
            e.tensor_mul(SC2[0:1, 12:13], SC2[0:1, 10:11], SC2[0:1, 2:3])
            e.drain()
            return e.tensor_scalar(out=SC2[0:1, 12:13], in0=SC2[0:1, 12:13],
                                   scalar1=4.0, scalar2=None, op0=ALU.mult)
        d_xs = op("vector", fxs, wait=[g3])
        d_sc = bcast_scalars_f32([11, 4, 5, 12], 4, d_xs)
        barrier()
        if phase_done(3.3):
            return

        d_q = DV.v
        for c in range(CH):
            t_x = dma("sync", XCH[:], xview(src_hbm, c), wait=[(DV, d_q)],
                      ch="xch")

            def f(e, c=c):
                e.tensor_mul(XC2[:], XCH[:], PS[:, 0:1024])
                e.drain()
                e.tensor_add(XC2[:], XC2[:], PS[:, 1024:2048])
                e.drain()
                return e.tensor_scalar(out=XQ8[:, c * T:(c + 1) * T],
                                       in0=XC2[:], scalar1=SCL[:, 4:5],
                                       scalar2=None, op0=ALU.mult)
            d_q = op("vector", f, wait=[t_x, (DV, d_sc)])
        barrier()
        if phase_done(3.4):
            return

        first_h = True
        te_mm_end = [0, 0]
        h_tok = [None, None]
        hst_last = [None]
        for os_i in range(16):
            prep_toks = []
            w8_tok = [DV.v, DV.v]
            for g in range(16):
                t_w = dma("sync", WRAW[:, (g % 2) * 512:(g % 2) * 512 + 512],
                          wfcT[g * 128:(g + 1) * 128,
                               os_i * 512:os_i * 512 + 512],
                          wait=[(DV, w8_tok[g % 2])], ch="wr%d" % (g % 2))
                tok = op("vector", lambda e, g=g, os_i=os_i: e.tensor_scalar(
                    out=W8[:, ((os_i % 2) * 16 + g) * 512:
                           ((os_i % 2) * 16 + g) * 512 + 512],
                    in0=WRAW[:, (g % 2) * 512:(g % 2) * 512 + 512],
                    scalar1=SCL[:, 5:6], scalar2=None, op0=ALU.mult),
                    wait=[t_w] +
                         ([(TE, te_mm_end[os_i % 2])] if g == 0 else []))
                w8_tok[g % 2] = tok
                prep_toks.append(tok)
            prep_done = prep_toks[-1]
            for o4 in range(4):
                oc = os_i * 4 + o4
                pb = (o4 % 2) * 1024
                for hf in range(2):
                    for g in range(CH):
                        wl = []
                        if g == 0 and hf == 0:
                            if o4 == 0:
                                wl = [(DV, prep_done)]
                            if h_tok[o4 % 2]:
                                wl = wl + [h_tok[o4 % 2]]
                        op("tensor", lambda e, g=g, o4=o4, hf=hf, pb=pb, os_i=os_i: e.matmul(
                            PS[:, pb + hf * 512:pb + hf * 512 + 512],
                            W8[:, ((os_i % 2) * 16 + g) * 512 + o4 * 128:
                               ((os_i % 2) * 16 + g) * 512 + o4 * 128 + 128],
                            XQ8[:, g * T + hf * 512:g * T + hf * 512 + 512],
                            start=(g == 0), stop=(g == CH - 1)), wait=wl)
                te_mm = TE.v
                s_r = op("scalar", lambda e, pb=pb: e.activation(
                    out=XC2[:], in_=PS[:, pb:pb + 1024], func=AF.Relu,
                    bias=0.0, scale=SCL[:, 7:8]),
                    wait=[(TE, te_mm), (DV, DV.v)])

                def fh(e, pb=pb, first_h=first_h):
                    e.scalar_tensor_tensor(out=XCH[:], in0=PS[:, pb:pb + 1024],
                                           scalar=SCL[:, 7:8], in1=XC2[:],
                                           op0=ALU.mult, op1=ALU.mult)
                    e.drain()
                    e.tensor_reduce(out=SCL[:, 10:11], in_=XCH[:],
                                    axis=mybir.AxisListType.X, op=ALU.max)
                    e.drain()
                    if first_h:
                        return e.tensor_copy(SCL[:, 14:15], SCL[:, 10:11])
                    return e.tensor_tensor(SCL[:, 14:15], SCL[:, 14:15],
                                           SCL[:, 10:11], op=ALU.max)
                d_h = op("vector", fh, wait=[(SC, s_r)] +
                         ([hst_last[0]] if hst_last[0] else []))
                first_h = False
                h_tok[o4 % 2] = (DV, d_h)
                hst_last[0] = dma("gpsimd",
                                  h_dram.rearrange("(ch p) t -> ch p t", p=128)[oc],
                                  XCH[:], wait=[(DV, d_h)], ch="hst")
            te_mm_end[os_i % 2] = TE.v
        barrier()
        if phase_done(3.5):
            return

        d_hm = rowmax_to_scalar(14, 16, DV.v)
        g2 = dma("gpsimd", cch_in, SC2[0:1, 16:17], wait=[(DV, d_hm)], ch="msc")
        ccc = chan("cc")
        cc2 = (ccc, ccc.bump(1))

        def fcc2(e, g2=g2):
            e.wait_ge(g2[0].sem, g2[1])
            inst = e.collective_compute(
                "AllReduce", ALU.max,
                replica_groups=[[0, 1, 2, 3, 4, 5, 6, 7]],
                ins=[cch_in], outs=[cch_out])
            inst.then_inc(ccc.sem, 1)
        streams["gpsimd"].append(fcc2)
        g3 = dma("gpsimd", SC2[0:1, 17:18], cch_out, wait=[cc2], ch="msc")

        def fhs(e):
            e.tensor_scalar(out=SC2[0:1, 18:19], in0=SC2[0:1, 17:18],
                            scalar1=1.0 / 448.0, scalar2=1e-6,
                            op0=ALU.mult, op1=ALU.max)
            e.drain()
            e.reciprocal(SC2[0:1, 21:22], SC2[0:1, 18:19])
            e.drain()
            e.tensor_scalar(out=SC2[0:1, 19:20], in0=SC2[0:1, 21:22],
                            scalar1=0.5, scalar2=None, op0=ALU.mult)
            e.drain()
            e.tensor_mul(SC2[0:1, 20:21], SC2[0:1, 18:19], SC2[0:1, 3:4])
            e.drain()
            return e.tensor_scalar(out=SC2[0:1, 20:21], in0=SC2[0:1, 20:21],
                                   scalar1=4.0, scalar2=None, op0=ALU.mult)
        d_hs = op("vector", fhs, wait=[g3])
        d_sc2 = bcast_scalars_f32([19, 20], 8, d_hs)
        barrier()
        if phase_done(3.6):
            return

        d_q = DV.v
        for c in range(64):
            t_h = dma("sync", XCH[:],
                      h_dram.rearrange("(ch p) t -> ch p t", p=128)[c],
                      wait=[(DV, d_q)], ch="xch")
            d_q = op("vector", lambda e, c=c: e.tensor_scalar(
                out=H8[:, c * T:(c + 1) * T], in0=XCH[:],
                scalar1=SCL[:, 8:9], scalar2=None, op0=ALU.mult),
                wait=[t_h, (DV, d_sc2)])
        barrier()
        if phase_done(3.7):
            return

        te_mm_last = 0
        ev_last = [None, None]
        ust_last = [None]
        for os_i in range(8):
            prep_toks = []
            w8_tok = [DV.v] * 4
            for g in range(64):
                slot = g % 4
                t_w = dma("sync", WRAW[:, slot * 256:slot * 256 + 256],
                          wprT[g * 128:(g + 1) * 128,
                               os_i * 256:os_i * 256 + 256],
                          wait=[(DV, w8_tok[slot])], ch="wr%d" % slot)
                tok = op("vector", lambda e, g=g, slot=slot: e.tensor_scalar(
                    out=W8[:, g * 256:g * 256 + 256],
                    in0=WRAW[:, slot * 256:slot * 256 + 256],
                    scalar1=SCL[:, 6:7], scalar2=None, op0=ALU.mult),
                    wait=[t_w] +
                         ([(TE, te_mm_last)] if g == 0 else []))
                w8_tok[slot] = tok
                prep_toks.append(tok)
            prep_done = prep_toks[-1]
            for oc2 in range(2):
                oc = os_i * 2 + oc2
                pb = oc2 * 1024
                for hf in range(2):
                    for g in range(64):
                        wl = []
                        if g == 0 and hf == 0:
                            wl = [(DV, prep_done)]
                            if ev_last[oc2]:
                                wl = wl + [ev_last[oc2]]
                        op("tensor", lambda e, g=g, oc2=oc2, hf=hf, pb=pb: e.matmul(
                            PS[:, pb + hf * 512:pb + hf * 512 + 512],
                            W8[:, g * 256 + oc2 * 128:g * 256 + oc2 * 128 + 128],
                            H8[:, g * T + hf * 512:g * T + hf * 512 + 512],
                            start=(g == 0), stop=(g == 63)), wait=wl)
                te_mm = TE.v
                d_ev = op("vector", lambda e, pb=pb: e.tensor_scalar(
                    out=PT[:], in0=PS[:, pb:pb + 1024],
                    scalar1=SCL[:, 9:10], scalar2=None, op0=ALU.mult),
                    wait=[(TE, te_mm)] + ([ust_last[0]] if ust_last[0] else []))
                ust_last[0] = dma("gpsimd", xview(u_dram, oc), PT[:],
                                  wait=[(DV, d_ev)], ch="ust")
                ev_last[oc2] = (DV, d_ev)
            te_mm_last = TE.v
        barrier()
        if phase_done(3.8):
            return

        te_st2 = ln_stats_udram(u_dram)
        te_bc2 = ln_smalls(te_st2)
        ln2_residual(None, src_hbm, dst_hbm, te_bc2, u_hbm=u_dram)
        barrier()

    blocks = [block_l0_attn, block_l0_ffn, block_l1_attn, block_l1_ffn]
    srcs = [x0, xs, xs, xs]
    dsts = [xs, xs, xs, out_t]
    dsts[N_BLOCKS - 1] = out_t
    for i in range(N_BLOCKS):
        blocks[i](srcs[i], dsts[i])

    final_toks = [(c.sem, c.v) for c in channels.values() if c.v > 0]

    def _final(e, wl=final_toks):
        for (s, v) in wl:
            e.wait_ge(s, v)
    streams["gpsimd"].append(_final)

    with es:
        with nc.Block() as block:
            @block.sync
            def _(e):
                for f in streams["sync"]:
                    f(e)

            @block.tensor
            def _(e):
                for f in streams["tensor"]:
                    f(e)

            @block.vector
            def _(e):
                for f in streams["vector"]:
                    f(e)

            @block.scalar
            def _(e):
                for f in streams["scalar"]:
                    f(e)

            @block.gpsimd
            def _(e):
                for f in streams["gpsimd"]:
                    f(e)
    return nc


def _host_layout(inputs):
    x = np.ascontiguousarray(inputs["L0_attn_x"])
    wqkvT = np.ascontiguousarray(inputs["L0_attn_w_qkv"].T)
    wprojT = np.ascontiguousarray(inputs["L0_attn_w_proj"].T)
    qgc = np.ascontiguousarray(
        (inputs["L0_attn_q_gain"].astype(np.float32) *
         np.float32(INV_SQRT_HD)).reshape(16, 1))

    def u8T(a):
        v = np.ascontiguousarray(a).view(np.uint8).reshape(
            a.shape[0], a.shape[1], 4)[:, :, 0]
        return np.ascontiguousarray(v.T)
    wqgT = u8T(inputs["L0_ffn_wq_gate"])
    wquT = u8T(inputs["L0_ffn_wq_up"])
    wqdT = u8T(inputs["L0_ffn_wq_down"])
    sg_c = np.ascontiguousarray(inputs["L0_ffn_s_gate"].reshape(44, 128).T)
    su_c = np.ascontiguousarray(inputs["L0_ffn_s_up"].reshape(44, 128).T)
    sd_c = np.ascontiguousarray(inputs["L0_ffn_s_down"].reshape(16, 128).T)
    l1wT = np.ascontiguousarray(inputs["L1_attn_Wqkv_w"].T)
    l1bias = np.ascontiguousarray(inputs["L1_attn_Wqkv_b"].reshape(18, 128).T)
    l1owT = np.ascontiguousarray(inputs["L1_attn_out_w"].T)
    l1ob = np.ascontiguousarray(inputs["L1_attn_out_b"].reshape(16, 128).T)
    wfcT = np.ascontiguousarray(inputs["L1_ffn_w_fc"].T)
    wprT = np.ascontiguousarray(inputs["L1_ffn_w_proj"].T)

    ident = np.eye(128, dtype=np.float32).astype(bf)
    aone = np.zeros((128, 64), dtype=np.float32)
    aone[:, 32] = 1.0
    aone = aone.astype(bf)
    onesrow = np.zeros((65, 128), dtype=np.float32)
    onesrow[[0, 32, 64], :] = 1.0
    onesrow = onesrow.astype(bf)
    negrow = np.zeros((65, 128), dtype=np.float32)
    negrow[[0, 32, 64], :] = -1.0
    negrow = negrow.astype(bf)
    ceps = np.zeros((128, 2), dtype=np.float32)
    ceps[:, 0] = LN_EPS
    ceps[:, 1] = RMS_EPS

    inv = 1.0 / (10000.0 ** (np.arange(0, 128, 2, dtype=np.float64) / 128.0))
    in_maps, pos_list = [], []
    for core in range(8):
        b, p = core // 2, core % 2
        pos = np.concatenate([np.arange(128) + (2 * j + p) * 128
                              for j in range(8)])
        pos_list.append(pos)
        xc = np.ascontiguousarray(x[b][pos].T)
        fr = np.outer(pos.astype(np.float64), inv)
        c64 = np.cos(fr).T.astype(np.float32)
        s64 = np.sin(fr).T.astype(np.float32)
        cosx = np.ascontiguousarray(
            np.concatenate([c64, c64], axis=0)).astype(bf)
        sinx = np.ascontiguousarray(
            np.concatenate([s64, -s64], axis=0)).astype(bf)
        lens = np.zeros((128, 8), dtype=np.float32)
        for j in range(8):
            lens[:, j] = (2 * j + p) * 128 + np.arange(128) + 1
        lens2 = np.maximum(lens - 1024.0, 0.0).astype(np.float32)
        in_maps.append(dict(
            x0=xc, wqkvT=wqkvT, wprojT=wprojT, qgc=qgc,
            wqgT=wqgT, wquT=wquT, wqdT=wqdT,
            sg_c=sg_c, su_c=su_c, sd_c=sd_c,
            l1wT=l1wT, l1bias=l1bias, l1owT=l1owT, l1ob=l1ob,
            wfcT=wfcT, wprT=wprT,
            cosx=cosx, sinx=sinx, lens_in=lens, lens2_in=lens2,
            ident_in=ident, aone_in=aone, onesrow_in=onesrow,
            negrow_in=negrow, ceps_in=ceps,
        ))
    return in_maps, pos_list


def kernel(**inputs):
    in_maps, pos_list = _host_layout(inputs)
    nc = build_program()
    lower_extended_insts(nc)
    res = run_bass_kernel_spmd(nc, in_maps, core_ids=list(range(8)))
    out = np.empty((4, 2048, 2048), dtype=np.float32)
    for core in range(8):
        b = core // 2
        out[b, pos_list[core], :] = np.asarray(res.results[core]["out"]).T
    return out
